# version 02
# baseline (speedup 1.0000x reference)
import math
import numpy as np
import concourse.bass as bass
import concourse.mybir as mybir
from concourse.bass_utils import run_bass_kernel_spmd

F32 = mybir.dt.float32
BF16 = mybir.dt.bfloat16
AF = mybir.ActivationFunctionType
OP = mybir.AluOpType
AX = mybir.AxisListType

B, N, D = 4, 8192, 1024
T = 4096
XH = 4
CH = 256
NCH = T // CH
SEG = 1024
NSEG = T // SEG
TT = 512
NT = T // TT
G = 8
NEGBIG = -3.0e38
MASKV = -1.0e30
C2_SERIAL = False
C2D = 2
C2BAR = False

KH = 514
KS = 4608
NKC = KS // 128
HIDP = 2816
NHC = HIDP // 128


def _fm(a):
    return np.ascontiguousarray(np.asarray(a, dtype=np.float32).T)


def _xpos_tables(t0):
    import ml_dtypes
    half_idx = np.arange(0, 64, 2, dtype=np.float64)
    inv = 1.0 / 10000.0 ** (half_idx / 64.0)
    base = (half_idx + 0.4 * 64.0) / (1.4 * 64.0)

    def mk(tpos, kind):
        W = len(tpos)
        ct = np.zeros((128, W), np.float32)
        st = np.zeros((128, W), np.float32)
        power = (tpos - 4096.0) / 256.0
        for c in range(64):
            jj = c % 32
            ang = tpos * inv[jj]
            sc = base[jj] ** power
            s = sc if kind == "q" else 1.0 / sc
            cv = (np.cos(ang) * s).astype(np.float32)
            sv = (np.sin(ang) * s).astype(np.float32)
            sgn = -1.0 if c < 32 else 1.0
            for rep in range(2):
                ct[rep * 64 + c] = cv
                st[rep * 64 + c] = sgn * sv
        return ct.astype(ml_dtypes.bfloat16), st.astype(ml_dtypes.bfloat16)

    tq = np.arange(t0, t0 + T, dtype=np.float64)
    tk = np.arange(t0 - 512, t0 + T, dtype=np.float64)
    cq, sq = mk(tq, "q")
    ck, sk = mk(tk, "k")
    return cq, sq, ck, sk


def _mask4(hf):
    m = np.zeros((4, 128, 640), np.float32)
    for qt in range(4):
        for j in range(5):
            mchunk = qt + j - 4
            tkk = mchunk * 128 + np.arange(128)[None, :]
            tqq = qt * 128 + np.arange(128)[:, None]
            ok = (tkk <= tqq) & (tkk >= tqq - 512)
            if hf == 0:
                ok = ok & (tkk >= 0)
            m[qt, :, j * 128:(j + 1) * 128] = np.where(ok, 0.0, MASKV)
    return m


def _rope_tables(t0):
    tpos = np.arange(t0, t0 + T, dtype=np.float64)
    inv = 1.0 / 10000.0 ** (np.arange(0, 64, 2, dtype=np.float64) / 64.0)
    cosr = np.zeros((128, T), np.float32)
    sinr = np.zeros((128, T), np.float32)
    for c in range(64):
        jj = c % 32
        ang = tpos * inv[jj]
        cv = np.cos(ang).astype(np.float32)
        sv = np.sin(ang).astype(np.float32)
        sgn = -1.0 if c < 32 else 1.0
        for rep in range(2):
            cosr[rep * 64 + c] = cv
            sinr[rep * 64 + c] = sgn * sv
    return cosr, sinr


def prep_inputs(inputs):
    x = np.asarray(inputs["L0_attn_x"], dtype=np.float32)
    ipw = np.asarray(inputs["L0_attn_in_proj_w"], dtype=np.float32)
    w_dt = np.ascontiguousarray(ipw[:, 0:D])
    w_v = np.ascontiguousarray(ipw[:, D:2 * D])
    perm = np.array([(f // 64) * 64 + ((f % 64) + 32) % 64 for f in range(D)])
    w_vs = np.ascontiguousarray(w_v[:, perm])
    w_gate = np.ascontiguousarray(ipw[:, 2 * D:3 * D])

    hm = np.asarray(inputs["L0_attn_head_mix"], dtype=np.float32)
    HM = np.zeros((D, D), np.float32)
    HM4 = HM.reshape(16, 64, 16, 64)
    for c in range(64):
        HM4[:, c, :, c] = hm

    common = {
        "w_dt": w_dt, "w_v": w_v, "w_vs": w_vs, "w_gate": w_gate,
        "hm_dense": HM,
        "outproj": np.ascontiguousarray(np.asarray(inputs["L0_attn_out_proj_w"], dtype=np.float32)),
        "convw": np.ascontiguousarray(np.asarray(inputs["L0_attn_conv_w"], dtype=np.float32)[:, 0, :]),
        "convb": np.asarray(inputs["L0_attn_conv_b"], dtype=np.float32).reshape(D, 1),
        "nrmsc": np.asarray(inputs["L0_attn_norm_scale"], dtype=np.float32).reshape(D, 1),
        "dtbias": np.asarray(inputs["L0_attn_dt_bias"], dtype=np.float32).reshape(D, 1),
    }

    wqkv = np.asarray(inputs["L0_ffn_wqkv"], dtype=np.float32)
    permI = np.array([(f // 64) * 64 + ((f % 64) + 32) % 64 for f in range(512)])
    for l in range(2):
        common[f"f{l}_wq"] = np.ascontiguousarray(wqkv[l][:, 0:512])
        common[f"f{l}_wqs"] = np.ascontiguousarray(wqkv[l][:, 0:512][:, permI])
        common[f"f{l}_wk"] = np.ascontiguousarray(wqkv[l][:, 512:1024])
        common[f"f{l}_wks"] = np.ascontiguousarray(wqkv[l][:, 512:1024][:, permI])
        common[f"f{l}_wv"] = np.ascontiguousarray(wqkv[l][:, 1024:1536])
        common[f"f{l}_wo"] = np.ascontiguousarray(np.asarray(inputs["L0_ffn_wo"], dtype=np.float32)[l])
        w1 = np.asarray(inputs["L0_ffn_w1"], dtype=np.float32)[l]
        w1a = np.zeros((D, HIDP), np.float32); w1a[:, :2730] = w1[:, :2730]
        w1g = np.zeros((D, HIDP), np.float32); w1g[:, :2730] = w1[:, 2730:]
        common[f"f{l}_w1a"] = w1a
        common[f"f{l}_w1g"] = w1g
        b1 = np.asarray(inputs["L0_ffn_b1"], dtype=np.float32)[l]
        b1a = np.zeros((HIDP, 1), np.float32); b1a[:2730, 0] = b1[:2730]
        b1g = np.zeros((HIDP, 1), np.float32); b1g[:2730, 0] = b1[2730:]
        common[f"f{l}_b1a"] = b1a
        common[f"f{l}_b1g"] = b1g
        w2 = np.asarray(inputs["L0_ffn_w2"], dtype=np.float32)[l]
        w2p = np.zeros((HIDP, D), np.float32); w2p[:2730] = w2
        common[f"f{l}_w2"] = w2p
        common[f"f{l}_b2"] = np.asarray(inputs["L0_ffn_b2"], dtype=np.float32)[l].reshape(D, 1)
        common[f"f{l}_lg"] = np.asarray(inputs["L0_ffn_ln_g"], dtype=np.float32)[l].reshape(1, D)
        common[f"f{l}_lb"] = np.asarray(inputs["L0_ffn_ln_b"], dtype=np.float32)[l].reshape(1, D)
    common["f_qscale"] = np.ascontiguousarray(np.asarray(inputs["L0_ffn_q_scale"], dtype=np.float32).reshape(2, 64, 1))
    common["f_kscale"] = np.ascontiguousarray(np.asarray(inputs["L0_ffn_k_scale"], dtype=np.float32).reshape(2, 64, 1))

    pp = np.arange(128)[:, None]
    cc_ = np.arange(128)[None, :]
    common["f_m0"] = np.where(cc_ >= pp, 0.0, MASKV).astype(np.float32)
    common["f_m4"] = np.where(cc_ <= pp, 0.0, MASKV).astype(np.float32)

    for nm in ("wq", "wk", "wv", "wo"):
        common["a_" + nm] = np.ascontiguousarray(np.asarray(inputs["L1_attn_" + nm], dtype=np.float32))
    for nm in ("bq", "bk", "bo", "b2"):
        common["a_" + nm] = np.asarray(inputs["L1_attn_" + nm], dtype=np.float32).reshape(D, 1)
    common["a_bv"] = np.asarray(inputs["L1_attn_bv"], dtype=np.float32).reshape(1, D)
    common["a_b1"] = np.asarray(inputs["L1_attn_b1"], dtype=np.float32).reshape(4096, 1)
    common["a_w1"] = np.ascontiguousarray(np.asarray(inputs["L1_attn_w1"], dtype=np.float32))
    common["a_w2"] = np.ascontiguousarray(np.asarray(inputs["L1_attn_w2"], dtype=np.float32))
    for nm in ("ln1_g", "ln1_b", "ln2_g", "ln2_b"):
        common["a_" + nm] = np.asarray(inputs["L1_attn_" + nm], dtype=np.float32).reshape(1, D)
    common["a_proj"] = np.ascontiguousarray(np.asarray(inputs["L1_attn_proj_mat"], dtype=np.float32))
    common["z_wT"] = np.ascontiguousarray(np.asarray(inputs["L1_ffn_weight"], dtype=np.float32).T)

    in_maps = []
    for c in range(8):
        b, hf = c // 2, c % 2
        t0 = hf * T
        shard = _fm(x[b, t0:t0 + T, :])
        halo = _fm(x[b, t0 - XH:t0, :]) if hf else np.zeros((D, XH), np.float32)
        cosr, sinr = _rope_tables(t0)
        m = dict(common)
        m["xin"] = np.ascontiguousarray(np.concatenate([halo, shard], axis=1))
        m["ropecos"] = cosr
        m["ropesin"] = sinr
        eb = np.zeros((128, 2), np.float32); eb[0:64, 0] = 1.0; eb[64:128, 1] = 1.0
        ebT = np.zeros((2, 128), np.float32); ebT[0, 0:64] = 1.0; ebT[1, 64:128] = 1.0
        m["c_eblk"] = eb; m["c_eblkT"] = ebT
        m["flg_take"] = np.full((128, 1), float(hf), np.float32)
        m["flg_send"] = np.full((128, 1), float(1 - hf), np.float32)
        m["flg_negbig"] = np.full((128, 1), float((1 - hf) * NEGBIG), np.float32)
        cq, sq, ck, sk = _xpos_tables(t0)
        m["f_cq"] = cq; m["f_sq"] = sq; m["f_ck"] = ck; m["f_sk"] = sk
        m["f_mask4"] = _mask4(hf)
        in_maps.append(m)
    return in_maps


class Scope:
    _n = [0]

    def __init__(self, nc):
        self.nc = nc
        self.guards = []

    def _u(self, nm):
        Scope._n[0] += 1
        return "%s_u%d" % (nm, Scope._n[0])

    def sb(self, nm, shp, dt=F32):
        g = self.nc.sbuf_tensor(self._u(nm), shp, dt)
        t = g.__enter__()
        self.guards.append(g)
        return t

    def ps(self, nm, shp, dt=F32):
        g = self.nc.psum_tensor(self._u(nm), shp, dt)
        t = g.__enter__()
        self.guards.append(g)
        return t

    def close(self):
        for g in reversed(self.guards):
            g.__exit__(None, None, None)
        self.guards = []


class Prog:
    def __init__(self):
        self.nc = bass.Bass("TRN2", target_bir_lowering=False, debug=False)
        self.sems = {}
        self.cnt = {}
        self.blk = None
        self._bar = 0

    def sem(self, name):
        h = self.nc.semaphore(name).__enter__()
        self.sems[name] = h
        self.cnt[name] = 0

    def inc(self, instr, name, k=1):
        instr.then_inc(self.sems[name], k)
        self.cnt[name] += k

    def w(self, eng, name, val=None):
        eng.wait_ge(self.sems[name], self.cnt[name] if val is None else val)

    def barrier(self):
        self._bar += 1
        tgt = 5 * self._bar
        bs = self.sems["bar"]

        def mk(has_drain):
            def f(e):
                if has_drain:
                    e.drain()
                e.sem_inc(bs, 1)
                e.wait_ge(bs, tgt)
            return f
        self.blk.sync(mk(False))
        self.blk.tensor(mk(True))
        self.blk.vector(mk(True))
        self.blk.scalar(mk(True))
        self.blk.gpsimd(mk(True))
        self.cnt["bar"] = tgt


class U:
    pass


def build_utils(p, E):
    nc, blk = p.nc, p.blk
    u = U()
    sc = Scope(nc)
    u.scope = sc
    u.ones_col = sc.sb("ones_col", [128, 1])
    u.ones_row = sc.sb("ones_row", [1, 128])
    u.onesN = sc.sb("onesN", [1, 512])
    u.neg20w = sc.sb("neg20w", [128, CH])
    u.ftk = sc.sb("ftk", [128, 1])
    u.fsd = sc.sb("fsd", [128, 1])
    u.fnb = sc.sb("fnb", [128, 1])
    u.ident = sc.sb("ident", [128, 128], BF16)
    u.eblk = sc.sb("eblk", [128, 2])
    u.eblkT = sc.sb("eblkT", [2, 128])
    u.iota_f = sc.sb("iota_f", [128, 128])
    u.iota_p = sc.sb("iota_p", [128, 1])
    u.PS0 = sc.ps("PS0", [128, 1024])
    u.PS1 = sc.ps("PS1", [128, 1024])

    def sec_sync(e):
        p.inc(e.dma_start(out=u.ftk[:, :], in_=E["flg_take"][:, :]), "ld", 16)
        p.inc(e.dma_start(out=u.fsd[:, :], in_=E["flg_send"][:, :]), "ld", 16)
        p.inc(e.dma_start(out=u.fnb[:, :], in_=E["flg_negbig"][:, :]), "ld", 16)
        p.inc(e.dma_start(out=u.eblk[:, :], in_=E["c_eblk"][:, :]), "ld", 16)
        p.inc(e.dma_start(out=u.eblkT[:, :], in_=E["c_eblkT"][:, :]), "ld", 16)
    blk.sync(sec_sync)

    def sec_gp(e):
        e.iota(u.iota_f[:, :], pattern=[[1, 128]], base=0, channel_multiplier=0,
               allow_small_or_imprecise_dtypes=True)
        e.iota(u.iota_p[:, :], pattern=[[0, 1]], base=0, channel_multiplier=1,
               allow_small_or_imprecise_dtypes=True)
        e.drain()
        e.sem_inc(p.sems["gp"], 1)
    blk.gpsimd(sec_gp)
    p.cnt["gp"] += 1

    def sec_v(e):
        e.memset(u.ones_col[:, :], 1.0)
        e.memset(u.ones_row[:, :], 1.0)
        e.memset(u.onesN[:, :], 1.0)
        e.memset(u.neg20w[:, :], -20.0)
        p.w(e, "gp")
        e.drain()
        e.tensor_scalar(u.ident[:, :], u.iota_f[:, :], u.iota_p[:, 0:1], 1.0, OP.is_equal, OP.mult)
        e.drain()
    blk.vector(sec_v)

    p.barrier()
    return u


def build_l0a(p, E, u):
    nc, blk = p.nc, p.blk
    xin = E["xin"]
    ones_col, ones_row = u.ones_col, u.ones_row
    neg20w, ftk, fsd, fnb = u.neg20w, u.ftk, u.fsd, u.fnb
    PS0, PS1 = u.PS0, u.PS1

    per = Scope(nc)
    xc_sb = per.sb("xc_sb", [128, G * T], BF16)
    fin_sb = per.sb("fin_sb", [128, G * T], BF16)
    convb_sb = per.sb("convb_sb", [128, G])
    nrmsc_sb = per.sb("nrmsc_sb", [128, G])
    dtb_sb = per.sb("dtb_sb", [128, G])
    convw_sb = per.sb("convw_sb", [128, 4 * G])

    def sec_sync(e):
        for g in range(G):
            p.inc(e.dma_start(out=convb_sb[:, g:g + 1], in_=E["convb"][g * 128:(g + 1) * 128, :]), "ld", 16)
            p.inc(e.dma_start(out=nrmsc_sb[:, g:g + 1], in_=E["nrmsc"][g * 128:(g + 1) * 128, :]), "ld", 16)
            p.inc(e.dma_start(out=dtb_sb[:, g:g + 1], in_=E["dtbias"][g * 128:(g + 1) * 128, :]), "ld", 16)
            p.inc(e.dma_start(out=convw_sb[:, 4 * g:4 * g + 4], in_=E["convw"][g * 128:(g + 1) * 128, :]), "ld", 16)
    blk.sync(sec_sync)

    p.barrier()

    Wd = TT + XH
    b1 = Scope(nc)
    BA = b1.sb("BA", [128, G * Wd])
    BB = b1.sb("BB", [128, G * Wd])
    BC = b1.sb("BC", [128, G * Wd])
    rstat = b1.sb("rstat", [1, Wd])
    rstat2 = b1.sb("rstat2", [1, Wd])
    PS2 = b1.ps("PS2b1", [128, 1024])
    PS3 = b1.ps("PS3b1", [128, 1024])

    def bsl(buf, g, wid=Wd):
        return buf[:, g * Wd:g * Wd + wid]

    for tt in range(NT):
        c0 = tt * TT

        def s_load(e, c0=c0):
            for g in range(G):
                p.inc(e.dma_start(out=bsl(BA, g), in_=xin[g * 128:(g + 1) * 128, c0:c0 + Wd]), "ld", 16)
        blk.sync(s_load)

        def s_sq(e):
            p.w(e, "ld")
            e.square(BB[:, :], BA[:, :])
            e.drain()
            e.sem_inc(p.sems["ex"], 1)
        blk.scalar(s_sq)
        p.cnt["ex"] += 1

        def s_mm1(e):
            p.w(e, "ex")
            for (a, b2) in ((0, 512), (512, Wd)):
                for g in range(G):
                    ins = nc.tensor.matmul(PS0[0:1, a:b2], ones_col[:, :], BB[:, g * Wd + a:g * Wd + b2],
                                           start=(g == 0), stop=(g == G - 1))
                    if g == G - 1:
                        p.inc(ins, "mm", 1)
            for (a, b2) in ((0, 512), (512, Wd)):
                for g in range(G):
                    ins = nc.tensor.matmul(PS1[0:1, a:b2], ones_col[:, :], BA[:, g * Wd + a:g * Wd + b2],
                                           start=(g == 0), stop=(g == G - 1))
                    if g == G - 1:
                        p.inc(ins, "mm", 1)
        blk.tensor(s_mm1)

        def s_stat(e):
            p.w(e, "mm")
            e.tensor_scalar(rstat2[0:1, 0:Wd], PS1[0:1, 0:Wd], 1.0 / 1024.0, None, OP.mult)
            e.drain()
            e.tensor_mul(rstat[0:1, 0:Wd], rstat2[0:1, 0:Wd], rstat2[0:1, 0:Wd])
            e.drain()
            e.scalar_tensor_tensor(out=rstat[0:1, 0:Wd], in0=PS0[0:1, 0:Wd], scalar=1.0 / 1024.0,
                                   in1=rstat[0:1, 0:Wd], op0=OP.mult, op1=OP.subtract)
            e.drain()
            e.tensor_scalar(rstat[0:1, 0:Wd], rstat[0:1, 0:Wd], 1e-5, None, OP.add)
            e.drain()
            e.reciprocal(rstat[0:1, 0:Wd], rstat[0:1, 0:Wd])
            e.drain()
            e.sem_inc(p.sems["ev"], 1)
        blk.vector(s_stat)
        p.cnt["ev"] += 1

        def s_sqrt(e):
            p.w(e, "ev")
            e.sqrt(rstat[0:1, 0:Wd], rstat[0:1, 0:Wd])
            e.drain()
            e.sem_inc(p.sems["ex"], 1)
        blk.scalar(s_sqrt)
        p.cnt["ex"] += 1

        def s_bneg(e):
            p.w(e, "ex")
            e.scalar_tensor_tensor(out=rstat2[0:1, 0:Wd], in0=rstat2[0:1, 0:Wd], scalar=-1.0,
                                   in1=rstat[0:1, 0:Wd], op0=OP.mult, op1=OP.mult)
            e.drain()
            e.sem_inc(p.sems["ev"], 1)
        blk.vector(s_bneg)
        p.cnt["ev"] += 1

        def s_mm2(e):
            p.w(e, "ev")
            for (a, b2) in ((0, 512), (512, Wd)):
                p.inc(nc.tensor.matmul(PS2[:, a:b2], ones_row[:, :], rstat[0:1, a:b2],
                                       start=True, stop=True), "mm", 1)
                p.inc(nc.tensor.matmul(PS3[:, a:b2], ones_row[:, :], rstat2[0:1, a:b2],
                                       start=True, stop=True), "mm", 1)
        blk.tensor(s_mm2)

        def s_xln(e, c0=c0):
            p.w(e, "mm")
            for g in range(G):
                e.scalar_tensor_tensor(out=bsl(BB, g), in0=bsl(BA, g), scalar=1.0,
                                       in1=PS2[:, 0:Wd], op0=OP.mult, op1=OP.mult)
            e.drain()
            for g in range(G):
                e.tensor_add(bsl(BB, g), bsl(BB, g), PS3[:, 0:Wd])
            e.drain()
            e.sem_inc(p.sems["ev"], 1)
        blk.vector(s_xln)
        p.cnt["ev"] += 1

        def s_stln(e, c0=c0):
            p.w(e, "ev")
            for g in range(G):
                p.inc(e.dma_start(out=E["xln_d"][g * 128:(g + 1) * 128, c0:c0 + TT],
                                  in_=BB[:, g * Wd + XH:g * Wd + XH + TT]), "st", 16)
        blk.sync(s_stln)

        def s_sq2(e):
            p.w(e, "ev")
            e.square(BA[:, :], BB[:, :])
            e.drain()
            e.sem_inc(p.sems["ex"], 1)
        blk.scalar(s_sq2)
        p.cnt["ex"] += 1

        def s_mm3(e):
            p.w(e, "ex")
            for (a, b2) in ((0, 512), (512, Wd)):
                for g in range(G):
                    ins = nc.tensor.matmul(PS0[0:1, a:b2], ones_col[:, :], BA[:, g * Wd + a:g * Wd + b2],
                                           start=(g == 0), stop=(g == G - 1))
                    if g == G - 1:
                        p.inc(ins, "mm", 1)
        blk.tensor(s_mm3)

        def s_stat2(e):
            p.w(e, "mm")
            e.tensor_scalar(rstat[0:1, 0:Wd], PS0[0:1, 0:Wd], 1.0 / 1024.0, 1e-6, OP.mult, OP.add)
            e.drain()
            e.reciprocal(rstat[0:1, 0:Wd], rstat[0:1, 0:Wd])
            e.drain()
            e.sem_inc(p.sems["ev"], 1)
        blk.vector(s_stat2)
        p.cnt["ev"] += 1

        def s_sqrt2(e):
            p.w(e, "ev")
            e.sqrt(rstat[0:1, 0:Wd], rstat[0:1, 0:Wd])
            e.drain()
            e.sem_inc(p.sems["ex"], 1)
        blk.scalar(s_sqrt2)
        p.cnt["ex"] += 1

        def s_mm4(e):
            p.w(e, "ex")
            for (a, b2) in ((0, 512), (512, Wd)):
                p.inc(nc.tensor.matmul(PS2[:, a:b2], ones_row[:, :], rstat[0:1, a:b2],
                                       start=True, stop=True), "mm", 1)
        blk.tensor(s_mm4)

        def s_xn(e):
            p.w(e, "mm")
            p.w(e, "st")
            for g in range(G):
                e.scalar_tensor_tensor(out=bsl(BC, g), in0=bsl(BB, g),
                                       scalar=nrmsc_sb[:, g:g + 1], in1=PS2[:, 0:Wd],
                                       op0=OP.mult, op1=OP.mult)
            e.drain()
            for g in range(G):
                e.tensor_scalar(bsl(BC, g), bsl(BC, g), 60000.0, -60000.0, OP.min, OP.max)
            e.drain()
            for g in range(G):
                e.tensor_scalar(bsl(BA, g, TT), BC[:, g * Wd + XH:g * Wd + XH + TT],
                                convw_sb[:, 4 * g + 3:4 * g + 4], None, OP.mult)
            e.drain()
            cur = BA
            for k in (2, 1, 0):
                nxt = BB if cur is BA else BA
                off = XH - 3 + k
                for g in range(G):
                    e.scalar_tensor_tensor(out=bsl(nxt, g, TT),
                                           in0=BC[:, g * Wd + off:g * Wd + off + TT],
                                           scalar=convw_sb[:, 4 * g + k:4 * g + k + 1],
                                           in1=bsl(cur, g, TT), op0=OP.mult, op1=OP.add)
                e.drain()
                cur = nxt
            e.sem_inc(p.sems["ev"], 1)
        blk.vector(s_xn)
        p.cnt["ev"] += 1

        def s_silu(e, tt=tt):
            p.w(e, "ev")
            for g in range(G):
                e.activation(out=xc_sb[:, g * T + tt * TT:g * T + (tt + 1) * TT],
                             in_=bsl(BB, g, TT), func=AF.Silu,
                             bias=convb_sb[:, g:g + 1], scale=1.0)
            e.drain()
            e.sem_inc(p.sems["ex"], 1)
        blk.scalar(s_silu)
        p.cnt["ex"] += 1

        p.barrier()

    b1.close()

    b2 = Scope(nc)
    cos_sb = b2.sb("cos_sb", [128, T])
    sin_sb = b2.sb("sin_sb", [128, T])
    W3 = b2.sb("W3", [128, G * 384], BF16)
    SA = b2.sb("SA", [128, SEG]); SB = b2.sb("SB", [128, SEG]); SC = b2.sb("SC", [128, SEG])
    SD = b2.sb("SD", [128, SEG]); SE = b2.sb("SE", [128, SEG])
    bdec = b2.sb("bdec", [128, G * NCH]); bout = b2.sb("bout", [128, G * NCH])
    st_a = b2.sb("st_a", [128, NCH]); st_b = b2.sb("st_b", [128, NCH])
    st_c = b2.sb("st_c", [128, NCH]); st_d = b2.sb("st_d", [128, NCH])
    st_e = b2.sb("st_e", [128, NCH])
    sv1 = b2.sb("sv1", [128, 1]); sv2 = b2.sb("sv2", [128, 1]); sv3 = b2.sb("sv3", [128, 1])
    snd_sb = b2.sb("snd_sb", [128, 24]); rcv_sb = b2.sb("rcv_sb", [128, 24])
    cc_sb = b2.sb("cc_sb", [128, G * NCH])
    PS2 = b2.ps("PS2b2", [128, 1024])

    def s_tbl(e):
        p.inc(e.dma_start(out=cos_sb[:, :], in_=E["ropecos"][:, :]), "ld", 16)
        p.inc(e.dma_start(out=sin_sb[:, :], in_=E["ropesin"][:, :]), "ld", 16)
    blk.sync(s_tbl)

    for g in range(G):
        def s_w3(e, g=g):
            for kk in range(G):
                for j, w in enumerate((E["w_dt"], E["w_v"], E["w_vs"])):
                    p.inc(e.dma_start(out=W3[:, kk * 384 + j * 128:kk * 384 + (j + 1) * 128],
                                      in_=w[kk * 128:(kk + 1) * 128, g * 128:(g + 1) * 128]), "ld", 16)
        blk.gpsimd(s_w3)

        for sg in range(NSEG):
            seg0 = sg * SEG

            def s_mm(e, g=g, seg0=seg0):
                p.w(e, "ld")
                PSL = [(PS0, 0), (PS0, 512), (PS1, 0), (PS1, 512), (PS2, 0), (PS2, 512)]
                for j in range(3):
                    for t2 in range(2):
                        pb, po = PSL[j * 2 + t2]
                        for kk in range(G):
                            ins = nc.tensor.matmul(
                                pb[:, po:po + 512],
                                W3[:, kk * 384 + j * 128:kk * 384 + (j + 1) * 128],
                                xc_sb[:, kk * T + seg0 + t2 * 512:kk * T + seg0 + (t2 + 1) * 512],
                                start=(kk == 0), stop=(kk == G - 1))
                            if kk == G - 1:
                                p.inc(ins, "mm", 1)
            blk.tensor(s_mm)
            mmn = p.cnt["mm"]

            def s_ev1(e, g=g, mmn=mmn):
                e.wait_ge(p.sems["mm"], mmn - 4)
                e.tensor_scalar(SA[:, 0:512], PS0[:, 0:512], dtb_sb[:, g:g + 1], 30.0, OP.add, OP.min)
                e.tensor_scalar(SA[:, 512:1024], PS0[:, 512:1024], dtb_sb[:, g:g + 1], 30.0, OP.add, OP.min)
                e.drain()
                e.sem_inc(p.sems["ev"], 1)
                e.wait_ge(p.sems["mm"], mmn - 2)
                e.tensor_copy(SB[:, 0:512], PS1[:, 0:512])
                e.tensor_copy(SB[:, 512:1024], PS1[:, 512:1024])
                e.drain()
                e.wait_ge(p.sems["mm"], mmn)
                e.tensor_copy(SC[:, 0:512], PS2[:, 0:512])
                e.tensor_copy(SC[:, 512:1024], PS2[:, 512:1024])
                e.drain()
                e.sem_inc(p.sems["ev"], 1)
            blk.vector(s_ev1)
            p.cnt["ev"] += 2
            evb = p.cnt["ev"]

            def s_sp(e, evb=evb):
                e.wait_ge(p.sems["ev"], evb - 1)
                e.activation(out=SD[:, :], in_=SA[:, :], func=AF.Exp)
                e.drain()
                e.activation(out=SA[:, :], in_=SD[:, :], func=AF.Ln, bias=1.0)
                e.drain()
                e.sem_inc(p.sems["ex"], 1)
            blk.scalar(s_sp)
            p.cnt["ex"] += 1

            def s_scan(e, g=g, sg=sg, seg0=seg0):
                p.w(e, "ex")
                e.tensor_scalar(SA[:, :], SA[:, :], 2.0, 0.001, OP.min, OP.max)
                e.drain()
                e.tensor_mul(SD[:, :], SB[:, :], cos_sb[:, seg0:seg0 + SEG])
                e.drain()
                e.tensor_mul(SE[:, :], SC[:, :], sin_sb[:, seg0:seg0 + SEG])
                e.drain()
                e.tensor_add(SC[:, :], SD[:, :], SE[:, :])
                e.drain()
                e.tensor_mul(SE[:, :], SC[:, :], SA[:, :])
                e.drain()
                e.tensor_scalar(SD[:, :], SA[:, :], -1.0, None, OP.mult)
                e.drain()
                for cc in range(SEG // CH):
                    e.tensor_tensor_scan(out=SC[:, cc * CH:(cc + 1) * CH],
                                         data0=SD[:, cc * CH:(cc + 1) * CH],
                                         data1=neg20w[:, :], initial=0.0,
                                         op0=OP.add, op1=OP.max)
                e.drain()
                e.tensor_copy(bdec[:, g * NCH + sg * 4:g * NCH + sg * 4 + 4], SC[:, CH - 1::CH])
                for cc in range(SEG // CH):
                    a = cc * CH
                    e.tensor_copy(SD[:, a:a + 1], SC[:, a:a + 1])
                    e.tensor_sub(SD[:, a + 1:a + CH], SC[:, a + 1:a + CH], SC[:, a:a + CH - 1])
                e.drain()
                e.sem_inc(p.sems["ev"], 1)
            blk.vector(s_scan)
            p.cnt["ev"] += 1

            def s_expg(e):
                p.w(e, "ev")
                e.activation(out=SA[:, :], in_=SD[:, :], func=AF.Exp)
                e.drain()
                e.sem_inc(p.sems["ex"], 1)
            blk.scalar(s_expg)
            p.cnt["ex"] += 1

            def s_h0(e, g=g, sg=sg):
                p.w(e, "ex")
                for cc in range(SEG // CH):
                    a = cc * CH
                    e.tensor_tensor_scan(out=SB[:, a:a + CH], data0=SA[:, a:a + CH],
                                         data1=SE[:, a:a + CH], initial=0.0,
                                         op0=OP.mult, op1=OP.add)
                e.drain()
                e.tensor_copy(bout[:, g * NCH + sg * 4:g * NCH + sg * 4 + 4], SB[:, CH - 1::CH])
                e.drain()
                e.sem_inc(p.sems["ev"], 1)
            blk.vector(s_h0)
            p.cnt["ev"] += 1

            def s_store(e, g=g, seg0=seg0):
                p.w(e, "ev")
                p.inc(e.dma_start(out=E["g_d"][g * 128:(g + 1) * 128, seg0:seg0 + SEG], in_=SA[:, :]), "st", 16)
                p.inc(e.dma_start(out=E["vin_d"][g * 128:(g + 1) * 128, seg0:seg0 + SEG], in_=SE[:, :]), "st", 16)
            blk.sync(s_store)

            p.barrier()

    if hasattr(p, 'marks'): p.marks.append(('l0a.b2a', nc.n_instructions()))
    for g in range(G):
        def v_stats(e, g=g):
            sl = slice(g * NCH, (g + 1) * NCH)
            e.tensor_tensor_scan(out=st_a[:, :], data0=bdec[:, sl], data1=neg20w[:, 0:NCH],
                                 initial=0.0, op0=OP.add, op1=OP.bypass)
            e.drain()
            e.tensor_scalar(st_b[:, :], st_a[:, :], -80.0, 0.0, OP.max, OP.min)
            e.drain()
            e.reduce_max(sv1[:, :], st_b[:, :], axis=AX.X)
            e.drain()
            e.tensor_scalar(st_c[:, :], st_b[:, :], sv1[:, 0:1], -20.0, OP.subtract, OP.max)
            e.drain()
            e.tensor_scalar(st_c[:, :], st_c[:, :], 0.0, None, OP.min)
            e.drain()
            e.sem_inc(p.sems["ev"], 1)
        blk.vector(v_stats)
        p.cnt["ev"] += 1

        def s_eseed(e):
            p.w(e, "ev")
            e.activation(out=st_d[:, :], in_=st_c[:, :], func=AF.Exp, scale=-1.0)
            e.drain()
            e.sem_inc(p.sems["ex"], 1)
        blk.scalar(s_eseed)
        p.cnt["ex"] += 1

        def v_send(e, g=g):
            p.w(e, "ex")
            sl = slice(g * NCH, (g + 1) * NCH)
            e.tensor_mul(st_d[:, :], bout[:, sl], st_d[:, :])
            e.drain()
            e.reduce_sum(sv2[:, :], st_d[:, :], axis=AX.X)
            e.drain()
            e.tensor_scalar(snd_sb[:, 3 * g:3 * g + 1], st_a[:, NCH - 1:NCH], fsd[:, 0:1], None, OP.mult)
            e.tensor_scalar(snd_sb[:, 3 * g + 1:3 * g + 2], sv1[:, 0:1], fsd[:, 0:1], None, OP.mult)
            e.tensor_scalar(snd_sb[:, 3 * g + 2:3 * g + 3], sv2[:, 0:1], fsd[:, 0:1], None, OP.mult)
            e.drain()
            e.sem_inc(p.sems["ev"], 1)
        blk.vector(v_send)
        p.cnt["ev"] += 1

    def g_coll(e):
        p.w(e, "ev")
        p.inc(e.dma_start(out=E["snd_d"][:, :], in_=snd_sb[:, :]), "st", 16)
        p.w(e, "st")
        p.inc(e.collective_compute(
            "AllReduce", OP.add,
            replica_groups=[[0, 1], [2, 3], [4, 5], [6, 7]],
            ins=[E["snd_d"].opt()],
            outs=[E["rcv_d"].opt()],
        ), "cc", 1)
        p.w(e, "cc")
        p.inc(e.dma_start(out=rcv_sb[:, :], in_=E["rcv_d"][:, :]), "ld", 16)
    blk.gpsimd(g_coll)

    p.barrier()

    for g in range(G):
        def v_cc1(e, g=g):
            sl = slice(g * NCH, (g + 1) * NCH)
            p.w(e, "ld")
            e.tensor_scalar(sv1[:, :], rcv_sb[:, 3 * g:3 * g + 1], ftk[:, 0:1], None, OP.mult)
            e.tensor_scalar(sv2[:, :], rcv_sb[:, 3 * g + 1:3 * g + 2], ftk[:, 0:1], fnb[:, 0:1], OP.mult, OP.add)
            e.drain()
            e.tensor_tensor_scan(out=st_a[:, :], data0=bdec[:, sl], data1=neg20w[:, 0:NCH],
                                 initial=0.0, op0=OP.add, op1=OP.bypass)
            e.drain()
            e.tensor_scalar(st_b[:, :], st_a[:, :], sv1[:, 0:1], -80.0, OP.add, OP.max)
            e.drain()
            e.tensor_scalar(st_b[:, :], st_b[:, :], 0.0, None, OP.min)
            e.drain()
            e.reduce_max(sv3[:, :], st_b[:, :], axis=AX.X)
            e.drain()
            e.tensor_max(sv3[:, :], sv3[:, :], sv2[:, :])
            e.drain()
            e.tensor_scalar(st_c[:, :], st_b[:, :], sv3[:, 0:1], -20.0, OP.subtract, OP.max)
            e.drain()
            e.tensor_scalar(st_c[:, :], st_c[:, :], 0.0, None, OP.min)
            e.drain()
            e.sem_inc(p.sems["ev"], 1)
        blk.vector(v_cc1)
        p.cnt["ev"] += 1

        def s_cc1(e):
            p.w(e, "ev")
            e.activation(out=st_d[:, :], in_=st_c[:, :], func=AF.Exp, scale=-1.0)
            e.drain()
            e.activation(out=st_e[:, :], in_=st_c[:, :], func=AF.Exp)
            e.drain()
            e.sem_inc(p.sems["ex"], 1)
        blk.scalar(s_cc1)
        p.cnt["ex"] += 1

        def v_cc2(e, g=g):
            p.w(e, "ex")
            sl = slice(g * NCH, (g + 1) * NCH)
            e.tensor_mul(st_d[:, :], bout[:, sl], st_d[:, :])
            e.drain()
            e.tensor_scalar(st_a[:, 0:1], rcv_sb[:, 3 * g + 2:3 * g + 3], ftk[:, 0:1], None, OP.mult)
            e.tensor_copy(st_a[:, 1:NCH], st_d[:, 0:NCH - 1])
            e.drain()
            e.tensor_tensor_scan(out=st_b[:, :], data0=st_a[:, :], data1=neg20w[:, 0:NCH],
                                 initial=0.0, op0=OP.add, op1=OP.bypass)
            e.drain()
            e.tensor_mul(cc_sb[:, sl], st_b[:, :], st_e[:, :])
            e.drain()
            e.sem_inc(p.sems["ev"], 1)
        blk.vector(v_cc2)
        p.cnt["ev"] += 1

    p.barrier()

    for g in range(G):
        for sg in range(NSEG):
            seg0 = sg * SEG

            def s_ld2(e, g=g, seg0=seg0):
                p.inc(e.dma_start(out=SA[:, :], in_=E["g_d"][g * 128:(g + 1) * 128, seg0:seg0 + SEG]), "ld", 16)
                p.inc(e.dma_start(out=SE[:, :], in_=E["vin_d"][g * 128:(g + 1) * 128, seg0:seg0 + SEG]), "ld", 16)
            blk.sync(s_ld2)

            def v_hs(e, g=g, sg=sg, seg0=seg0):
                p.w(e, "ld")
                for cc in range(SEG // CH):
                    a = cc * CH
                    ci = g * NCH + sg * 4 + cc
                    e.tensor_tensor_scan(out=SB[:, a:a + CH], data0=SA[:, a:a + CH],
                                         data1=SE[:, a:a + CH], initial=cc_sb[:, ci:ci + 1],
                                         op0=OP.mult, op1=OP.add)
                e.drain()
                e.tensor_copy(fin_sb[:, g * T + seg0:g * T + seg0 + SEG], SB[:, :])
                e.drain()
                e.sem_inc(p.sems["ev"], 1)
            blk.vector(v_hs)
            p.cnt["ev"] += 1

            p.barrier()

    tap_aps = E.get("_taps", {})
    if "cc" in tap_aps:
        def g_tapcc(e):
            p.inc(e.dma_start(out=tap_aps["cc"][0:128, 0:G * NCH], in_=cc_sb[:, :]), "st", 16)
            p.inc(e.dma_start(out=tap_aps["cc"][128:256, 0:G * NCH], in_=bdec[:, :]), "st", 16)
            p.inc(e.dma_start(out=tap_aps["cc"][256:384, 0:G * NCH], in_=bout[:, :]), "st", 16)
            p.inc(e.dma_start(out=tap_aps["cc"][384:512, 0:24], in_=rcv_sb[:, :]), "st", 16)
            p.w(e, "st")
        blk.gpsimd(g_tapcc)
        p.barrier()

    b2.close()

    if hasattr(p, 'marks'): p.marks.append(('l0a.b2b', nc.n_instructions()))
    T3 = 256
    NT3 = T // T3
    b3 = Scope(nc)
    HMW = b3.sb("HMW", [128, G * D], BF16)
    WGT = b3.sb("WGT", [128, G * D], BF16)
    WOP = b3.sb("WOP", [128, G * D], BF16)
    gt_sb = b3.sb("gt_sb", [128, T3], BF16)
    mg_sb = b3.sb("mg_sb", [128, G * T3], BF16)
    xr_sb = b3.sb("xr_sb", [128, G * T3])
    xl_sb = b3.sb("xl_sb", [128, G * T3])
    PS2 = b3.ps("PS2b3", [128, 1024])

    def g_wb3(e):
        for g in range(G):
            p.inc(e.dma_start(out=HMW[:, g * D:(g + 1) * D], in_=E["hm_dense"][g * 128:(g + 1) * 128, :]), "ld", 16)
            p.inc(e.dma_start(out=WGT[:, g * D:(g + 1) * D], in_=E["w_gate"][g * 128:(g + 1) * 128, :]), "ld", 16)
            p.inc(e.dma_start(out=WOP[:, g * D:(g + 1) * D], in_=E["outproj"][g * 128:(g + 1) * 128, :]), "ld", 16)
    blk.gpsimd(g_wb3)
    p.barrier()

    for tt in range(NT3):
        c0 = tt * T3

        def s_ldx(e, c0=c0):
            p.w(e, "st")
            for g in range(G):
                p.inc(e.dma_start(out=xr_sb[:, g * T3:(g + 1) * T3],
                                  in_=xin[g * 128:(g + 1) * 128, XH + c0:XH + c0 + T3]), "ld", 16)
                p.inc(e.dma_start(out=xl_sb[:, g * T3:(g + 1) * T3],
                                  in_=E["xln_d"][g * 128:(g + 1) * 128, c0:c0 + T3]), "ld", 16)
        blk.sync(s_ldx)

        def v_pre(e):
            p.w(e, "ld")
            for g in range(G):
                e.tensor_add(xr_sb[:, g * T3:(g + 1) * T3], xr_sb[:, g * T3:(g + 1) * T3],
                             xl_sb[:, g * T3:(g + 1) * T3])
            e.drain()
            e.sem_inc(p.sems["ev"], 1)
        blk.vector(v_pre)
        p.cnt["ev"] += 1

        for oc in range(G):
            def s_mm3(e, oc=oc, c0=c0):
                p.w(e, "ld")
                for kk in range(G):
                    ins = nc.tensor.matmul(PS0[:, 0:T3], HMW[:, kk * D + oc * 128:kk * D + (oc + 1) * 128],
                                           fin_sb[:, kk * T + c0:kk * T + c0 + T3],
                                           start=(kk == 0), stop=(kk == G - 1))
                    if kk == G - 1:
                        p.inc(ins, "mm", 1)
                for kk in range(G):
                    ins = nc.tensor.matmul(PS1[:, 0:T3], WGT[:, kk * D + oc * 128:kk * D + (oc + 1) * 128],
                                           xc_sb[:, kk * T + c0:kk * T + c0 + T3],
                                           start=(kk == 0), stop=(kk == G - 1))
                    if kk == G - 1:
                        p.inc(ins, "mm", 1)
            blk.tensor(s_mm3)

            def s_sig(e):
                p.w(e, "mm")
                e.activation(out=gt_sb[:, :], in_=PS1[:, 0:T3], func=AF.Sigmoid)
                e.drain()
                e.sem_inc(p.sems["ex"], 1)
            blk.scalar(s_sig)
            p.cnt["ex"] += 1

            def s_mg(e, oc=oc):
                p.w(e, "ex")
                e.scalar_tensor_tensor(out=mg_sb[:, oc * T3:(oc + 1) * T3], in0=PS0[:, 0:T3],
                                       scalar=1.0, in1=gt_sb[:, :], op0=OP.mult, op1=OP.mult)
                e.drain()
                e.sem_inc(p.sems["ev"], 1)
            blk.vector(s_mg)
            p.cnt["ev"] += 1

            p.barrier()

        for oc in range(G):
            def s_mm4(e, oc=oc):
                for kk in range(G):
                    ins = nc.tensor.matmul(PS2[:, 0:T3], WOP[:, kk * D + oc * 128:kk * D + (oc + 1) * 128],
                                           mg_sb[:, kk * T3:(kk + 1) * T3],
                                           start=(kk == 0), stop=(kk == G - 1))
                    if kk == G - 1:
                        p.inc(ins, "mm", 1)
            blk.tensor(s_mm4)

            def s_res(e, oc=oc):
                p.w(e, "mm")
                e.scalar_tensor_tensor(out=xr_sb[:, oc * T3:(oc + 1) * T3], in0=PS2[:, 0:T3],
                                       scalar=1.0, in1=xr_sb[:, oc * T3:(oc + 1) * T3],
                                       op0=OP.mult, op1=OP.add)
                e.drain()
                e.sem_inc(p.sems["ev"], 1)
            blk.vector(s_res)
            p.cnt["ev"] += 1

            p.barrier()

        def s_stx(e, c0=c0):
            for g in range(G):
                p.inc(e.dma_start(out=E["x1_d"][g * 128:(g + 1) * 128, c0:c0 + T3],
                                  in_=xr_sb[:, g * T3:(g + 1) * T3]), "st", 16)
        blk.sync(s_stx)

        p.barrier()

    b3.close()

    tap_aps = E.get("_taps", {})
    if tap_aps:
        def g_taps(e):
            for nm, src in (("xc", xc_sb), ("fin", fin_sb)):
                if nm in tap_aps:
                    for g in range(G):
                        p.inc(e.dma_start(out=tap_aps[nm][g * 128:(g + 1) * 128, :],
                                          in_=src[:, g * T:(g + 1) * T]), "st", 16)
        blk.gpsimd(g_taps)
        p.barrier()
    per.close()


def emit_ln_job(p, u, LA, LB, r1, r2, src, scol, dst, dcol, w, PSa=None, PSb=None):
    nc, blk = p.nc, p.blk

    def s_load(e):
        for g in range(G):
            p.inc(e.dma_start(out=LA[:, g * 512:g * 512 + w], in_=src[g * 128:(g + 1) * 128, scol:scol + w]), "ld", 16)
    blk.sync(s_load)

    def s_sq(e):
        p.w(e, "ld")
        for g in range(G):
            e.square(LB[:, g * 512:g * 512 + w], LA[:, g * 512:g * 512 + w])
        e.drain()
        e.sem_inc(p.sems["ex"], 1)
    blk.scalar(s_sq)
    p.cnt["ex"] += 1

    def s_mm(e):
        p.w(e, "ex")
        for g in range(G):
            ins = nc.tensor.matmul(u.PS0[0:1, 0:w], u.ones_col[:, :], LB[:, g * 512:g * 512 + w],
                                   start=(g == 0), stop=(g == G - 1))
            if g == G - 1:
                p.inc(ins, "mm", 1)
        for g in range(G):
            ins = nc.tensor.matmul(u.PS1[0:1, 0:w], u.ones_col[:, :], LA[:, g * 512:g * 512 + w],
                                   start=(g == 0), stop=(g == G - 1))
            if g == G - 1:
                p.inc(ins, "mm", 1)
    blk.tensor(s_mm)

    def s_stat(e):
        p.w(e, "mm")
        e.tensor_scalar(r2[0:1, 0:w], u.PS1[0:1, 0:w], 1.0 / 1024.0, None, OP.mult)
        e.drain()
        e.tensor_mul(r1[0:1, 0:w], r2[0:1, 0:w], r2[0:1, 0:w])
        e.drain()
        e.scalar_tensor_tensor(out=r1[0:1, 0:w], in0=u.PS0[0:1, 0:w], scalar=1.0 / 1024.0,
                               in1=r1[0:1, 0:w], op0=OP.mult, op1=OP.subtract)
        e.drain()
        e.tensor_scalar(r1[0:1, 0:w], r1[0:1, 0:w], 1e-5, None, OP.add)
        e.drain()
        e.reciprocal(r1[0:1, 0:w], r1[0:1, 0:w])
        e.drain()
        e.sem_inc(p.sems["ev"], 1)
    blk.vector(s_stat)
    p.cnt["ev"] += 1

    def s_sqrt(e):
        p.w(e, "ev")
        e.sqrt(r1[0:1, 0:w], r1[0:1, 0:w])
        e.drain()
        e.sem_inc(p.sems["ex"], 1)
    blk.scalar(s_sqrt)
    p.cnt["ex"] += 1

    def s_b(e):
        p.w(e, "ex")
        e.scalar_tensor_tensor(out=r2[0:1, 0:w], in0=r2[0:1, 0:w], scalar=-1.0,
                               in1=r1[0:1, 0:w], op0=OP.mult, op1=OP.mult)
        e.drain()
        e.sem_inc(p.sems["ev"], 1)
    blk.vector(s_b)
    p.cnt["ev"] += 1

    def s_bc(e):
        p.w(e, "ev")
        p.inc(nc.tensor.matmul(PSa[:, 0:w], u.ones_row[:, :], r1[0:1, 0:w], start=True, stop=True), "mm", 1)
        p.inc(nc.tensor.matmul(PSb[:, 0:w], u.ones_row[:, :], r2[0:1, 0:w], start=True, stop=True), "mm", 1)
    blk.tensor(s_bc)

    def s_app(e):
        p.w(e, "mm")
        for g in range(G):
            e.scalar_tensor_tensor(out=LB[:, g * 512:g * 512 + w], in0=LA[:, g * 512:g * 512 + w],
                                   scalar=1.0, in1=PSa[:, 0:w], op0=OP.mult, op1=OP.mult)
        e.drain()
        for g in range(G):
            e.tensor_add(LB[:, g * 512:g * 512 + w], LB[:, g * 512:g * 512 + w], PSb[:, 0:w])
        e.drain()
        e.sem_inc(p.sems["ev"], 1)
    blk.vector(s_app)
    p.cnt["ev"] += 1

    def s_store(e):
        p.w(e, "ev")
        for g in range(G):
            p.inc(e.dma_start(out=dst[g * 128:(g + 1) * 128, dcol:dcol + w], in_=LB[:, g * 512:g * 512 + w]), "st", 16)
    blk.sync(s_store)
    p.barrier()


def emit_exchange(p, u, sc, src, scol, w, snd_d, rcv_d, dst, dcol, cols=None):
    nc, blk = p.nc, p.blk
    HB = sc.sb("HBx%d" % p.cnt["cc"], [128, G * w])

    def s_l(e):
        p.w(e, "st")
        for g in range(G):
            p.inc(e.dma_start(out=HB[:, g * w:(g + 1) * w], in_=src[g * 128:(g + 1) * 128, scol:scol + w]), "ld", 16)
    blk.sync(s_l)

    def s_m(e):
        p.w(e, "ld")
        e.tensor_scalar(HB[:, :], HB[:, :], u.fsd[:, 0:1], None, OP.mult)
        e.drain()
        e.sem_inc(p.sems["ev"], 1)
    blk.vector(s_m)
    p.cnt["ev"] += 1

    def s_c(e):
        p.w(e, "ev")
        for g in range(G):
            p.inc(e.dma_start(out=snd_d[g * 128:(g + 1) * 128, 0:w], in_=HB[:, g * w:(g + 1) * w]), "st", 16)
        p.w(e, "st")
        p.inc(e.collective_compute(
            "AllReduce", OP.add,
            replica_groups=[[0, 1], [2, 3], [4, 5], [6, 7]],
            ins=[snd_d.opt()], outs=[rcv_d.opt()],
        ), "cc", 1)
        p.w(e, "cc")
        for g in range(G):
            p.inc(e.dma_start(out=HB[:, g * w:(g + 1) * w], in_=rcv_d[g * 128:(g + 1) * 128, 0:w]), "ld", 16)
    blk.gpsimd(s_c)

    a, b2 = (0, w) if cols is None else cols

    def s_m2(e):
        p.w(e, "ld")
        e.tensor_scalar(HB[:, :], HB[:, :], u.ftk[:, 0:1], None, OP.mult)
        e.drain()
        e.sem_inc(p.sems["ev"], 1)
    blk.vector(s_m2)
    p.cnt["ev"] += 1

    def s_s(e):
        p.w(e, "ev")
        for g in range(G):
            p.inc(e.dma_start(out=dst[g * 128:(g + 1) * 128, dcol:dcol + (b2 - a)],
                              in_=HB[:, g * w + a:g * w + b2],
                              allow_slow_non_contiguous=True), "st", 16)
    blk.sync(s_s)
    p.barrier()


def build_l0f_layer(p, E, u, l, src, xpd, dst, dst_col0):
    nc, blk = p.nc, p.blk
    PS0, PS1 = u.PS0, u.PS1
    pre = "f%d_" % l

    at = Scope(nc)
    WQKV = at.sb("WQKV", [128, G * 2560], BF16)
    WO = at.sb("WO", [128, 4 * 1024], BF16)
    KROT = at.sb("KROT", [128, 4 * KS], BF16)
    VT = at.sb("VT", [128, NKC * 512], BF16)
    TBC = at.sb("TBC", [128, KS], BF16); TBS = at.sb("TBS", [128, KS], BF16)
    MSK4 = at.sb("MSK4", [128, 4 * 640])
    M0 = at.sb("M0", [128, 128]); M4 = at.sb("M4", [128, 128])
    XIT = at.sb("XIT", [128, G * 513], BF16)
    XPR = at.sb("XPR", [128, G * 512])
    RA = at.sb("RA", [128, 512]); RB = at.sb("RB", [128, 512])
    RC = at.sb("RC", [128, 512]); SQB = at.sb("SQB", [128, 512])
    NRM = at.sb("NRM", [2, 512])
    QROT = at.sb("QROT", [128, 4 * 512], BF16)
    P_sb = at.sb("P_sb", [128, 2 * 640], BF16)
    PT_sb = at.sb("PT_sb", [128, 2 * 640], BF16)
    rowm = at.sb("rowm", [128, 2])
    rsum = at.sb("rsum", [128, 4])
    rinv = at.sb("rinv", [128, 2])
    DIAG = at.sb("DIAG", [128, 2 * 128], BF16)
    o_sb = at.sb("o_sb", [128, 4 * 512], BF16)
    qsc = at.sb("qsc", [128, 1]); qscs = at.sb("qscs", [128, 1])
    ksc = at.sb("ksc", [128, 1]); kscs = at.sb("kscs", [128, 1])

    def s_w(e):
        for kk in range(G):
            for j, nm in enumerate(("wq", "wqs", "wk", "wks", "wv")):
                p.inc(e.dma_start(out=WQKV[:, kk * 2560 + j * 512:kk * 2560 + (j + 1) * 512],
                                  in_=E[pre + nm][kk * 128:(kk + 1) * 128, :]), "ld", 16)
        for c4 in range(4):
            p.inc(e.dma_start(out=WO[:, c4 * 1024:(c4 + 1) * 1024],
                              in_=E[pre + "wo"][c4 * 128:(c4 + 1) * 128, :]), "ld", 16)
        p.inc(e.dma_start(out=TBC[:, :], in_=E["f_ck"][:, :]), "ld", 16)
        p.inc(e.dma_start(out=TBS[:, :], in_=E["f_sk"][:, :]), "ld", 16)
        for qt in range(4):
            p.inc(e.dma_start(out=MSK4[:, qt * 640:(qt + 1) * 640],
                              in_=E["f_mask4"][qt, :, :]), "ld", 16)
        p.inc(e.dma_start(out=M0[:, :], in_=E["f_m0"][:, :]), "ld", 16)
        p.inc(e.dma_start(out=M4[:, :], in_=E["f_m4"][:, :]), "ld", 16)
        qsa = E["f_qscale"][l]
        ksa = E["f_kscale"][l]
        for rep in range(2):
            b0 = rep * 64
            p.inc(e.dma_start(out=qsc[b0:b0 + 64, :], in_=qsa[0:64, :]), "ld", 16)
            p.inc(e.dma_start(out=ksc[b0:b0 + 64, :], in_=ksa[0:64, :]), "ld", 16)
            p.inc(e.dma_start(out=qscs[b0:b0 + 32, :], in_=qsa[32:64, :]), "ld", 16)
            p.inc(e.dma_start(out=qscs[b0 + 32:b0 + 64, :], in_=qsa[0:32, :]), "ld", 16)
            p.inc(e.dma_start(out=kscs[b0:b0 + 32, :], in_=ksa[32:64, :]), "ld", 16)
            p.inc(e.dma_start(out=kscs[b0 + 32:b0 + 64, :], in_=ksa[0:32, :]), "ld", 16)
    blk.gpsimd(s_w)
    p.barrier()

    def shifted(col0, wid, kk):
        off = 1 if kk < 4 else 0
        return XIT[:, kk * 513 + col0 + off:kk * 513 + col0 + off + wid]

    def emit_qk_rot(kind, col0_xi, tok0_tbl, nq, dst_rot, dst_col):
        jW, jWs = (0, 1) if kind == "q" else (2, 3)
        CT, ST = TBC, TBS
        scl, scls = (qsc, qscs) if kind == "q" else (ksc, kscs)

        def s_ld(e):
            p.w(e, "st")
            for kk in range(G):
                p.inc(e.dma_start(out=XIT[:, kk * 513:(kk + 1) * 513],
                                  in_=src[kk * 128:(kk + 1) * 128, col0_xi - 1:col0_xi + 512]), "ld", 16)
        blk.gpsimd(s_ld)

        for qc in range(4):
            def s_mm(e, qc=qc):
                p.w(e, "ld")
                if qc == 0:
                    p.w(e, "ex")
                    p.w(e, "ev")
                for kk in range(G):
                    ins = nc.tensor.matmul(PS0[:, 0:512],
                                           WQKV[:, kk * 2560 + jW * 512 + qc * 128:kk * 2560 + jW * 512 + (qc + 1) * 128],
                                           shifted(0, 512, kk), start=(kk == 0), stop=(kk == G - 1))
                    if kk == G - 1:
                        p.inc(ins, "mm", 1)
                for kk in range(G):
                    ins = nc.tensor.matmul(PS0[:, 512:1024],
                                           WQKV[:, kk * 2560 + jWs * 512 + qc * 128:kk * 2560 + jWs * 512 + (qc + 1) * 128],
                                           shifted(0, 512, kk), start=(kk == 0), stop=(kk == G - 1))
                    if kk == G - 1:
                        p.inc(ins, "mm", 1)
            blk.tensor(s_mm)

            def s_sq(e):
                p.w(e, "mm")
                e.square(SQB[:, :], PS0[:, 0:512])
                e.drain()
                e.sem_inc(p.sems["ex"], 1)
            blk.scalar(s_sq)
            p.cnt["ex"] += 1

            def s_ev(e, qc=qc):
                p.w(e, "ex")
                e.tensor_scalar(RA[:, :], PS0[:, 0:512], scl[:, 0:1], None, OP.mult)
                e.tensor_scalar(RB[:, :], PS0[:, 512:1024], scls[:, 0:1], None, OP.mult)
                e.drain()
                e.sem_inc(p.sems["ev"], 1)
            blk.vector(s_ev)
            p.cnt["ev"] += 1

            def s_nrm(e, qc=qc):
                p.w(e, "ev")
                p.inc(nc.tensor.matmul(PS1[0:2, 0:512], u.eblk[:, :], SQB[:, :],
                                       start=True, stop=True), "mm", 1)
            blk.tensor(s_nrm)

            def s_n2(e):
                p.w(e, "mm")
                e.tensor_copy(NRM[0:2, 0:512], PS1[0:2, 0:512])
                e.drain()
                e.sem_inc(p.sems["ev"], 1)
            blk.vector(s_n2)
            p.cnt["ev"] += 1

            def s_n3(e):
                p.w(e, "ev")
                e.sqrt(NRM[0:2, 0:512], NRM[0:2, 0:512])
                e.drain()
                e.sem_inc(p.sems["ex"], 1)
            blk.scalar(s_n3)
            p.cnt["ex"] += 1

            def s_n4(e):
                p.w(e, "ex")
                e.tensor_scalar(NRM[0:2, 0:512], NRM[0:2, 0:512], 1e-12, None, OP.max)
                e.drain()
                e.reciprocal(NRM[0:2, 0:512], NRM[0:2, 0:512])
                e.drain()
                e.sem_inc(p.sems["ev"], 1)
            blk.vector(s_n4)
            p.cnt["ev"] += 1

            def s_bc(e):
                p.w(e, "ev")
                p.inc(nc.tensor.matmul(PS1[:, 512:1024], u.eblkT[:, :], NRM[0:2, 0:512],
                                       start=True, stop=True), "mm", 1)
            blk.tensor(s_bc)

            def s_rope(e, qc=qc):
                p.w(e, "mm")
                e.tensor_mul(RC[:, :], RA[:, :], CT[:, tok0_tbl:tok0_tbl + 512])
                e.drain()
                e.tensor_mul(RA[:, :], RB[:, :], ST[:, tok0_tbl:tok0_tbl + 512])
                e.drain()
                e.tensor_add(RC[:, :], RC[:, :], RA[:, :])
                e.drain()
                e.tensor_mul(dst_rot[:, dst_col + qc * (KS if kind == "k" else 512):
                                     dst_col + qc * (KS if kind == "k" else 512) + 512],
                             RC[:, :], PS1[:, 512:1024])
                e.drain()
                e.sem_inc(p.sems["ev"], 1)
            blk.vector(s_rope)
            p.cnt["ev"] += 1

            p.barrier()

    ab_sc = Scope(nc)
    PSV = ab_sc.ps("PSV", [128, 1024])
    for kt in range(9):
        col0 = 2 + 512 * kt
        emit_qk_rot("k", col0, 512 * kt, 512, KROT, 512 * kt)
        for vc in range(4):
            def s_vmm(e, kt=kt, vc=vc):
                for kk in range(G):
                    ins = nc.tensor.matmul(PSV[:, 0:512],
                                           shifted(vc * 128, 128, kk),
                                           WQKV[:, kk * 2560 + 4 * 512:kk * 2560 + 5 * 512],
                                           start=(kk == 0), stop=(kk == G - 1))
                    if kk == G - 1:
                        p.inc(ins, "mm", 1)
            blk.tensor(s_vmm)

            def s_vev(e, kt=kt, vc=vc):
                p.w(e, "mm")
                e.tensor_copy(VT[:, (4 * kt + vc) * 512:(4 * kt + vc + 1) * 512], PSV[:, 0:512])
                e.drain()
                e.sem_inc(p.sems["ev"], 1)
            blk.vector(s_vev)
            p.cnt["ev"] += 1

            p.barrier()

    ab_sc.close()

    tapd = E.get("_taps", {})
    if l == 0 and "dbgkrot" in tapd:
        def g_dk(e):
            p.inc(e.dma_start(out=tapd["dbgkrot"][:, :], in_=KROT[:, :]), "st", 16)
            p.inc(e.dma_start(out=tapd["dbgvt"][:, :], in_=VT[:, :]), "st", 16)
        blk.gpsimd(g_dk)
        p.barrier()

    if hasattr(p, 'marks'): p.marks.append(('f%d.keys' % l, nc.n_instructions()))
    c_sc = Scope(nc)
    PSPT = c_sc.ps("PSPT", [128, 1024])
    PSD = c_sc.ps("PSD", [128, 1024])
    def s_qtbl(e):
        p.inc(e.dma_start(out=TBC[:, 0:T], in_=E["f_cq"][:, :]), "ld", 16)
        p.inc(e.dma_start(out=TBS[:, 0:T], in_=E["f_sq"][:, :]), "ld", 16)
    blk.gpsimd(s_qtbl)
    p.barrier()

    for qg in range(8):
        emit_qk_rot("q", KH + 512 * qg, 512 * qg, 512, QROT, 0)

        if l == 0 and qg == 0 and "dbgqrot" in E.get("_taps", {}):
            def g_dq(e, qg=qg):
                p.inc(e.dma_start(out=E["_taps"]["dbgqrot"][:, :], in_=QROT[:, :]), "st", 16)
            blk.gpsimd(g_dq)
            p.barrier()

        sim_mm = {}; ev_max = {}; ex_exp = {}; ev_rinv = {}
        mm_tr = {}; ex_pt = {}; mm_pv = {}; ev_oev = {}
        SIMP = [PS0, PS1]

        def stage2(it, qg=qg):
            h = it % 8
            qtl = it // 8
            qt = qg * 4 + qtl
            pg = it // 2

            def t_tr(e, it=it, h=h):
                e.wait_ge(p.sems["ev"], ev_rinv[it])
                if it - 1 in ex_pt:
                    e.wait_ge(p.sems["ex"], ex_pt[it - 1])
                for j in range(5):
                    slot = PSPT[:, j * 128:(j + 1) * 128]
                    ins = nc.tensor.matmul(slot,
                                           P_sb[:, (it % 2) * 640 + j * 128:(it % 2) * 640 + (j + 1) * 128],
                                           DIAG[:, (it % 2) * 128:(it % 2 + 1) * 128],
                                           start=True, stop=True)
                    if j == 4:
                        p.inc(ins, "mm", 1)
                        mm_tr[it] = p.cnt["mm"]
            blk.tensor(t_tr)

            def a_ev(e, it=it):
                e.wait_ge(p.sems["mm"], mm_tr[it])
                e.activation(out=PT_sb[:, (it % 2) * 640:(it % 2) * 640 + 640],
                             in_=PSPT[:, 0:640], func=AF.Copy)
                e.drain()
                e.sem_inc(p.sems["ex"], 1)
                p.cnt["ex"] += 1
                ex_pt[it] = p.cnt["ex"]
            blk.scalar(a_ev)

            def t_pv(e, it=it, h=h, qt=qt, pg=pg):
                if pg - C2D in ev_oev:
                    e.wait_ge(p.sems["ev"], ev_oev[pg - C2D])
                e.wait_ge(p.sems["ex"], ex_pt[it])
                for j in range(5):
                    pd = PSD[(h % 2) * 64:(h % 2) * 64 + 64,
                             (pg % 2) * 512:(pg % 2) * 512 + 128]
                    ins = nc.tensor.matmul(pd, VT[:, (qt + j) * 512 + h * 64:(qt + j) * 512 + (h + 1) * 64],
                                           PT_sb[:, (it % 2) * 640 + j * 128:(it % 2) * 640 + (j + 1) * 128],
                                           start=(j == 0), stop=(j == 4),
                                           tile_position=(0, (h % 2) * 64))
                    if j == 4:
                        p.inc(ins, "mm", 1)
                        mm_pv[it] = p.cnt["mm"]
            blk.tensor(t_pv)

            if h % 2 == 1:
                def v_oev(e, it=it, h=h, pg=pg, qtl=qtl):
                    e.wait_ge(p.sems["mm"], mm_pv[it])
                    e.wait_ge(p.sems["mm"], mm_pv[it - 1])
                    pd = PSD[:, (pg % 2) * 512:(pg % 2) * 512 + 128]
                    e.tensor_copy(o_sb[:, (h // 2) * 512 + qtl * 128:(h // 2) * 512 + (qtl + 1) * 128], pd)
                    e.drain()
                    e.sem_inc(p.sems["ev"], 1)
                    p.cnt["ev"] += 1
                    ev_oev[pg] = p.cnt["ev"]
                blk.vector(v_oev)

        for it in range(32):
            h = it % 8
            qtl = it // 8
            qt = qg * 4 + qtl
            if C2BAR:
                p.barrier()
            if it >= 2:
                stage2(it - 2)
            if C2BAR:
                p.barrier()

            def t_sim(e, it=it, h=h, qt=qt, qtl=qtl):
                if it - C2D in ex_exp:
                    e.wait_ge(p.sems["ex"], ex_exp[it - C2D])
                lhs = QROT[(h % 2) * 64:(h % 2) * 64 + 64,
                           (h // 2) * 512 + qtl * 128:(h // 2) * 512 + (qtl + 1) * 128]
                rhsbase = (h // 2) * KS + 128 * qt
                pb = SIMP[it % 2]
                nc.tensor.matmul(pb[:, 0:512], lhs, KROT[(h % 2) * 64:(h % 2) * 64 + 64, rhsbase:rhsbase + 512],
                                 start=True, stop=True)
                p.inc(nc.tensor.matmul(pb[:, 512:640], lhs,
                                       KROT[(h % 2) * 64:(h % 2) * 64 + 64, rhsbase + 512:rhsbase + 640],
                                       start=True, stop=True), "mm", 1)
                sim_mm[it] = p.cnt["mm"]
            blk.tensor(t_sim)

            def v_max(e, it=it, qt=qt):
                e.wait_ge(p.sems["mm"], sim_mm[it])
                pb = SIMP[it % 2]
                if qt < 4:
                    e.tensor_add(pb[:, 0:512], pb[:, 0:512], MSK4[:, (qt % 4) * 640:(qt % 4) * 640 + 512])
                    e.tensor_add(pb[:, 512:640], pb[:, 512:640], MSK4[:, (qt % 4) * 640 + 512:(qt % 4 + 1) * 640])
                else:
                    e.tensor_add(pb[:, 0:128], pb[:, 0:128], M0[:, :])
                    e.tensor_add(pb[:, 512:640], pb[:, 512:640], M4[:, :])
                e.drain()
                e.reduce_max(rowm[:, it % 2:it % 2 + 1], pb[:, 0:640], axis=AX.X)
                e.drain()
                e.tensor_scalar(rowm[:, it % 2:it % 2 + 1], rowm[:, it % 2:it % 2 + 1], -8.0, None, OP.mult)
                e.drain()
                e.sem_inc(p.sems["ev"], 1)
                p.cnt["ev"] += 1
                ev_max[it] = p.cnt["ev"]
            blk.vector(v_max)

            def a_exp(e, it=it):
                e.wait_ge(p.sems["ev"], ev_max[it])
                if it - C2D in mm_tr:
                    e.wait_ge(p.sems["mm"], mm_tr[it - C2D])
                pb = SIMP[it % 2]
                e.activation(out=P_sb[:, (it % 2) * 640:(it % 2) * 640 + 512], in_=pb[:, 0:512],
                             func=AF.Exp, bias=rowm[:, it % 2:it % 2 + 1], scale=8.0,
                             accum_out=rsum[:, (it % 2) * 2:(it % 2) * 2 + 1])
                e.activation(out=P_sb[:, (it % 2) * 640 + 512:(it % 2) * 640 + 640], in_=pb[:, 512:640],
                             func=AF.Exp, bias=rowm[:, it % 2:it % 2 + 1], scale=8.0,
                             accum_out=rsum[:, (it % 2) * 2 + 1:(it % 2) * 2 + 2])
                e.drain()
                e.sem_inc(p.sems["ex"], 1)
                p.cnt["ex"] += 1
                ex_exp[it] = p.cnt["ex"]
            blk.scalar(a_exp)

            def v_rs(e, it=it):
                e.wait_ge(p.sems["ex"], ex_exp[it])
                e.tensor_add(rinv[:, it % 2:it % 2 + 1], rsum[:, (it % 2) * 2:(it % 2) * 2 + 1],
                             rsum[:, (it % 2) * 2 + 1:(it % 2) * 2 + 2])
                e.drain()
                e.reciprocal(rinv[:, it % 2:it % 2 + 1], rinv[:, it % 2:it % 2 + 1])
                e.drain()
                e.tensor_scalar(DIAG[:, (it % 2) * 128:(it % 2 + 1) * 128], u.ident[:, :],
                                rinv[:, it % 2:it % 2 + 1], None, OP.mult)
                e.drain()
                e.sem_inc(p.sems["ev"], 1)
                p.cnt["ev"] += 1
                ev_rinv[it] = p.cnt["ev"]
            blk.vector(v_rs)

        stage2(30)
        stage2(31)

        if l == 0 and qg == 0 and "dbgo" in E.get("_taps", {}):
            p.barrier()
            def g_do(e):
                p.inc(e.dma_start(out=E["_taps"]["dbgo"][:, :], in_=o_sb[:, :]), "st", 16)
                if "dbgp" in E["_taps"]:
                    p.inc(e.dma_start(out=E["_taps"]["dbgp"][:, :], in_=P_sb[:, :]), "st", 16)
            blk.gpsimd(g_do)
            p.barrier()

        def s_ldxi(e, qg=qg):
            p.w(e, "st")
            for kk in range(G):
                p.inc(e.dma_start(out=XPR[:, kk * 512:(kk + 1) * 512],
                                  in_=src[kk * 128:(kk + 1) * 128, KH + qg * 512:KH + (qg + 1) * 512]), "ld", 16)
        blk.sync(s_ldxi)

        ev_xp = {}
        for oc in range(G):
            def s_wo(e, oc=oc):
                if oc == 0:
                    p.w(e, "ld")
                    p.w(e, "ex")
                    p.w(e, "ev")
                if oc - 2 in ev_xp:
                    e.wait_ge(p.sems["ev"], ev_xp[oc - 2])
                for c4 in range(4):
                    ins = nc.tensor.matmul(PS0[:, (oc % 2) * 512:(oc % 2) * 512 + 512],
                                           WO[:, c4 * 1024 + oc * 128:c4 * 1024 + (oc + 1) * 128],
                                           o_sb[:, c4 * 512:(c4 + 1) * 512],
                                           start=(c4 == 0), stop=(c4 == 3))
                    if c4 == 3:
                        p.inc(ins, "mm", 1)
                        ev_xp[(oc, "mm")] = p.cnt["mm"]
            blk.tensor(s_wo)

            def s_xp(e, oc=oc):
                e.wait_ge(p.sems["mm"], ev_xp[(oc, "mm")])
                e.scalar_tensor_tensor(out=XPR[:, oc * 512:(oc + 1) * 512],
                                       in0=PS0[:, (oc % 2) * 512:(oc % 2) * 512 + 512],
                                       scalar=1.0, in1=XPR[:, oc * 512:(oc + 1) * 512],
                                       op0=OP.mult, op1=OP.add)
                e.drain()
                e.sem_inc(p.sems["ev"], 1)
                p.cnt["ev"] += 1
                ev_xp[oc] = p.cnt["ev"]
            blk.vector(s_xp)

        def s_stxp(e, qg=qg):
            p.w(e, "ev")
            for kk in range(G):
                p.inc(e.dma_start(out=xpd[kk * 128:(kk + 1) * 128, 1 + qg * 512:1 + (qg + 1) * 512],
                                  in_=XPR[:, kk * 512:(kk + 1) * 512]), "st", 16)
        blk.sync(s_stxp)
        p.barrier()

        if l == 0 and qg == 0 and "dbgo2" in E.get("_taps", {}):
            def g_do2(e):
                p.inc(e.dma_start(out=E["_taps"]["dbgo2"][:, :], in_=o_sb[:, :]), "st", 16)
                p.inc(e.dma_start(out=E["_taps"]["dbgxpr"][:, 0:G * 512], in_=XPR[:, :]), "st", 16)
            blk.gpsimd(g_do2)
            p.barrier()

    c_sc.close()
    at.close()

    ex_sc = Scope(nc)
    emit_exchange(p, u, ex_sc, xpd, T - 7, 8, E["snd_s"], E["rcv_s"], xpd, 0, cols=(7, 8))
    ex_sc.close()

    if hasattr(p, 'marks'): p.marks.append(('f%d.attn' % l, nc.n_instructions()))
    T3F = 512
    NTF = T // T3F
    ff = Scope(nc)
    W1A = ff.sb("W1A", [128, G * HIDP], BF16)
    W1G = ff.sb("W1G", [128, G * HIDP], BF16)
    W2 = ff.sb("W2", [128, NHC * 1024], BF16)
    XPT = ff.sb("XPT", [128, G * (T3F + 1)])
    SQF = ff.sb("SQF", [128, T3F])
    H3 = ff.sb("H3", [128, G * T3F], BF16)
    HH = ff.sb("HH", [128, NHC * T3F], BF16)
    fr1 = ff.sb("fr1", [1, T3F]); fr2 = ff.sb("fr2", [1, T3F])
    lgrow = ff.sb("lgrow", [1, D]); lbrow = ff.sb("lbrow", [1, D])
    b1a_c = ff.sb("b1a_c", [128, NHC]); b1g_c = ff.sb("b1g_c", [128, NHC])
    b2_c = ff.sb("b2_c", [128, G])
    PS2 = ff.ps("PS2ff", [128, 1024])
    PS3 = ff.ps("PS3ff", [128, 1024])

    def s_wf(e):
        for kk in range(G):
            p.inc(e.dma_start(out=W1A[:, kk * HIDP:(kk + 1) * HIDP],
                              in_=E[pre + "w1a"][kk * 128:(kk + 1) * 128, :]), "ld", 16)
            p.inc(e.dma_start(out=W1G[:, kk * HIDP:(kk + 1) * HIDP],
                              in_=E[pre + "w1g"][kk * 128:(kk + 1) * 128, :]), "ld", 16)
        for hc in range(NHC):
            p.inc(e.dma_start(out=W2[:, hc * 1024:(hc + 1) * 1024],
                              in_=E[pre + "w2"][hc * 128:(hc + 1) * 128, :]), "ld", 16)
            p.inc(e.dma_start(out=b1a_c[:, hc:hc + 1], in_=E[pre + "b1a"][hc * 128:(hc + 1) * 128, :]), "ld", 16)
            p.inc(e.dma_start(out=b1g_c[:, hc:hc + 1], in_=E[pre + "b1g"][hc * 128:(hc + 1) * 128, :]), "ld", 16)
        for g in range(G):
            p.inc(e.dma_start(out=b2_c[:, g:g + 1], in_=E[pre + "b2"][g * 128:(g + 1) * 128, :]), "ld", 16)
        p.inc(e.dma_start(out=lgrow[:, :], in_=E[pre + "lg"][:, :]), "ld", 16)
        p.inc(e.dma_start(out=lbrow[:, :], in_=E[pre + "lb"][:, :]), "ld", 16)
    blk.gpsimd(s_wf)
    p.barrier()

    for tf in range(NTF):
        c0 = tf * T3F

        def fsh(kk, wid=T3F):
            off = 1 if kk < 4 else 0
            return XPT[:, kk * (T3F + 1) + off:kk * (T3F + 1) + off + wid]

        def s_lx(e, c0=c0):
            p.w(e, "st")
            for kk in range(G):
                p.inc(e.dma_start(out=XPT[:, kk * (T3F + 1):(kk + 1) * (T3F + 1)],
                                  in_=xpd[kk * 128:(kk + 1) * 128, c0:c0 + T3F + 1]), "ld", 16)
        blk.sync(s_lx)

        for kk in range(G):
            def s_sqf(e, kk=kk):
                if kk == 0:
                    p.w(e, "ld")
                else:
                    p.w(e, "mm")
                e.square(SQF[:, 0:T3F], fsh(kk))
                e.drain()
                e.sem_inc(p.sems["ex"], 1)
            blk.scalar(s_sqf)
            p.cnt["ex"] += 1

            def s_mmsq(e, kk=kk):
                p.w(e, "ex")
                ins = nc.tensor.matmul(PS0[0:1, 0:T3F], u.ones_col[:, :], SQF[:, 0:T3F],
                                       start=(kk == 0), stop=(kk == G - 1))
                if kk == G - 1:
                    p.inc(ins, "mm", 1)
                else:
                    p.inc(ins, "mm", 1)
            blk.tensor(s_mmsq)

        def s_mmsum(e):
            for kk in range(G):
                ins = nc.tensor.matmul(PS1[0:1, 0:T3F], u.ones_col[:, :], fsh(kk),
                                       start=(kk == 0), stop=(kk == G - 1))
                if kk == G - 1:
                    p.inc(ins, "mm", 1)
        blk.tensor(s_mmsum)

        def s_st2(e):
            p.w(e, "mm")
            e.tensor_scalar(fr2[0:1, :], PS1[0:1, 0:T3F], 1.0 / 1024.0, None, OP.mult)
            e.drain()
            e.tensor_mul(fr1[0:1, :], fr2[0:1, :], fr2[0:1, :])
            e.drain()
            e.scalar_tensor_tensor(out=fr1[0:1, :], in0=PS0[0:1, 0:T3F], scalar=1.0 / 1024.0,
                                   in1=fr1[0:1, :], op0=OP.mult, op1=OP.subtract)
            e.drain()
            e.tensor_scalar(fr1[0:1, :], fr1[0:1, :], 1e-5, None, OP.add)
            e.drain()
            e.reciprocal(fr1[0:1, :], fr1[0:1, :])
            e.drain()
            e.sem_inc(p.sems["ev"], 1)
        blk.vector(s_st2)
        p.cnt["ev"] += 1

        def s_sq3(e):
            p.w(e, "ev")
            e.sqrt(fr1[0:1, :], fr1[0:1, :])
            e.drain()
            e.sem_inc(p.sems["ex"], 1)
        blk.scalar(s_sq3)
        p.cnt["ex"] += 1

        def s_negm(e):
            p.w(e, "ex")
            e.scalar_tensor_tensor(out=fr2[0:1, :], in0=fr2[0:1, :], scalar=-1.0,
                                   in1=fr1[0:1, :], op0=OP.mult, op1=OP.mult)
            e.drain()
            e.sem_inc(p.sems["ev"], 1)
        blk.vector(s_negm)
        p.cnt["ev"] += 1

        for kk in range(G):
            def s_r1(e, kk=kk):
                p.w(e, "ev")
                p.inc(nc.tensor.matmul(PS2[:, 0:T3F], lgrow[0:1, kk * 128:(kk + 1) * 128], fr1[0:1, :],
                                       start=True, stop=True), "mm", 1)
                nc.tensor.matmul(PS3[:, 0:T3F], lbrow[0:1, kk * 128:(kk + 1) * 128], u.onesN[0:1, 0:T3F],
                                 start=True, stop=False)
                p.inc(nc.tensor.matmul(PS3[:, 0:T3F], lgrow[0:1, kk * 128:(kk + 1) * 128], fr2[0:1, :],
                                       start=False, stop=True), "mm", 1)
            blk.tensor(s_r1)

            def s_h3(e, kk=kk):
                p.w(e, "mm")
                e.scalar_tensor_tensor(out=SQF[:, 0:T3F], in0=fsh(kk), scalar=1.0,
                                       in1=PS2[:, 0:T3F], op0=OP.mult, op1=OP.mult)
                e.drain()
                e.tensor_add(H3[:, kk * T3F:(kk + 1) * T3F], SQF[:, 0:T3F], PS3[:, 0:T3F])
                e.drain()
                e.sem_inc(p.sems["ev"], 1)
            blk.vector(s_h3)
            p.cnt["ev"] += 1
            p.barrier()

        for hc in range(NHC):
            def s_w1(e, hc=hc):
                for kk in range(G):
                    ins = nc.tensor.matmul(PS0[:, (hc % 2) * 512:(hc % 2) * 512 + T3F],
                                           W1A[:, kk * HIDP + hc * 128:kk * HIDP + (hc + 1) * 128],
                                           H3[:, kk * T3F:(kk + 1) * T3F],
                                           start=(kk == 0), stop=(kk == G - 1))
                    if kk == G - 1:
                        p.inc(ins, "mm", 1)
                for kk in range(G):
                    ins = nc.tensor.matmul(PS1[:, (hc % 2) * 512:(hc % 2) * 512 + T3F],
                                           W1G[:, kk * HIDP + hc * 128:kk * HIDP + (hc + 1) * 128],
                                           H3[:, kk * T3F:(kk + 1) * T3F],
                                           start=(kk == 0), stop=(kk == G - 1))
                    if kk == G - 1:
                        p.inc(ins, "mm", 1)
            blk.tensor(s_w1)

            def s_gelu(e, hc=hc):
                p.w(e, "mm")
                e.activation(out=SQF[:, 0:T3F], in_=PS1[:, (hc % 2) * 512:(hc % 2) * 512 + T3F],
                             func=AF.Gelu, bias=b1g_c[:, hc:hc + 1], scale=1.0)
                e.drain()
                e.sem_inc(p.sems["ex"], 1)
            blk.scalar(s_gelu)
            p.cnt["ex"] += 1

            def s_hh(e, hc=hc):
                p.w(e, "ex")
                e.scalar_tensor_tensor(out=HH[:, hc * T3F:(hc + 1) * T3F],
                                       in0=PS0[:, (hc % 2) * 512:(hc % 2) * 512 + T3F],
                                       scalar=b1a_c[:, hc:hc + 1], in1=SQF[:, 0:T3F],
                                       op0=OP.add, op1=OP.mult)
                e.drain()
                e.sem_inc(p.sems["ev"], 1)
            blk.vector(s_hh)
            p.cnt["ev"] += 1
            p.barrier()

        for oc in range(G):
            def s_w2(e, oc=oc):
                for hc in range(NHC):
                    ins = nc.tensor.matmul(PS2[:, (oc % 2) * 512:(oc % 2) * 512 + T3F],
                                           W2[:, hc * 1024 + oc * 128:hc * 1024 + (oc + 1) * 128],
                                           HH[:, hc * T3F:(hc + 1) * T3F],
                                           start=(hc == 0), stop=(hc == NHC - 1))
                    if hc == NHC - 1:
                        p.inc(ins, "mm", 1)
            blk.tensor(s_w2)

            def s_out(e, oc=oc):
                p.w(e, "mm")
                e.scalar_tensor_tensor(out=XPT[:, oc * (T3F + 1) + 1:oc * (T3F + 1) + 1 + T3F],
                                       in0=PS2[:, (oc % 2) * 512:(oc % 2) * 512 + T3F],
                                       scalar=b2_c[:, oc:oc + 1],
                                       in1=XPT[:, oc * (T3F + 1) + 1:oc * (T3F + 1) + 1 + T3F],
                                       op0=OP.add, op1=OP.add)
                e.drain()
                e.sem_inc(p.sems["ev"], 1)
            blk.vector(s_out)
            p.cnt["ev"] += 1
            p.barrier()

        def s_sto(e, c0=c0):
            for kk in range(G):
                p.inc(e.dma_start(out=dst[kk * 128:(kk + 1) * 128, dst_col0 + c0:dst_col0 + c0 + T3F],
                                  in_=XPT[:, kk * (T3F + 1) + 1:kk * (T3F + 1) + 1 + T3F]), "st", 16)
        blk.sync(s_sto)
        p.barrier()

    ff.close()
    if hasattr(p, 'marks'): p.marks.append(('f%d.ffn' % l, nc.n_instructions()))


def build_l0f(p, E, u):
    nc, blk = p.nc, p.blk

    sc = Scope(nc)
    emit_exchange(p, u, sc, E["x1_d"], T - KH, KH, E["snd_h"], E["rcv_h"], E["hx1_d"], 0)
    sc.close()

    lnsc = Scope(nc)
    LA = lnsc.sb("LA", [128, G * 512])
    LB = lnsc.sb("LB", [128, G * 512])
    r1 = lnsc.sb("r1", [1, 512]); r2 = lnsc.sb("r2", [1, 512])
    PSa = lnsc.ps("PSaln", [128, 1024]); PSb = lnsc.ps("PSbln", [128, 1024])
    emit_ln_job(p, u, LA, LB, r1, r2, E["hx1_d"], 0, E["xi0_d"], 0, 512, PSa, PSb)
    emit_ln_job(p, u, LA, LB, r1, r2, E["hx1_d"], 512, E["xi0_d"], 512, 2, PSa, PSb)
    for i in range(8):
        emit_ln_job(p, u, LA, LB, r1, r2, E["x1_d"], i * 512, E["xi0_d"], KH + i * 512, 512, PSa, PSb)
    lnsc.close()

    build_l0f_layer(p, E, u, 0, E["xi0_d"], E["xp0_d"], E["xi1_d"], KH)
    sc = Scope(nc)
    emit_exchange(p, u, sc, E["xi1_d"], KH + T - KH, KH, E["snd_h"], E["rcv_h"], E["xi1_d"], 0)
    sc.close()
    build_l0f_layer(p, E, u, 1, E["xi1_d"], E["xp1_d"], E["xi2_d"], 0)

    sc = Scope(nc)
    XA = sc.sb("XA", [128, G * 512])
    XB = sc.sb("XB", [128, G * 512])
    for i in range(8):
        def s_l(e, i=i):
            p.w(e, "st")
            for g in range(G):
                p.inc(e.dma_start(out=XA[:, g * 512:(g + 1) * 512],
                                  in_=E["x1_d"][g * 128:(g + 1) * 128, i * 512:(i + 1) * 512]), "ld", 16)
                p.inc(e.dma_start(out=XB[:, g * 512:(g + 1) * 512],
                                  in_=E["xi2_d"][g * 128:(g + 1) * 128, i * 512:(i + 1) * 512]), "ld", 16)
        blk.sync(s_l)

        def s_a(e):
            p.w(e, "ld")
            e.tensor_add(XA[:, :], XA[:, :], XB[:, :])
            e.drain()
            e.sem_inc(p.sems["ev"], 1)
        blk.vector(s_a)
        p.cnt["ev"] += 1

        def s_s(e, i=i):
            p.w(e, "ev")
            for g in range(G):
                p.inc(e.dma_start(out=E["x2_d"][g * 128:(g + 1) * 128, i * 512:(i + 1) * 512],
                                  in_=XA[:, g * 512:(g + 1) * 512]), "st", 16)
        blk.sync(s_s)
        p.barrier()
    sc.close()


def emit_ln_sb(p, u, XT, W, OUT, grow, brow, fr1, fr2, SQ1, PSa, PSb, eps=1e-5):
    nc, blk = p.nc, p.blk
    for kk in range(G):
        def s_sq(e, kk=kk):
            if kk > 0:
                p.w(e, "mm")
            else:
                p.w(e, "ld")
            e.square(SQ1[:, 0:W], XT[:, kk * W:(kk + 1) * W])
            e.drain()
            e.sem_inc(p.sems["ex"], 1)
        blk.scalar(s_sq)
        p.cnt["ex"] += 1

        def s_mm(e, kk=kk):
            p.w(e, "ex")
            p.inc(nc.tensor.matmul(u.PS0[0:1, 0:W], u.ones_col[:, :], SQ1[:, 0:W],
                                   start=(kk == 0), stop=(kk == G - 1)), "mm", 1)
        blk.tensor(s_mm)

    def s_mmsum(e):
        for kk in range(G):
            ins = nc.tensor.matmul(u.PS1[0:1, 0:W], u.ones_col[:, :], XT[:, kk * W:(kk + 1) * W],
                                   start=(kk == 0), stop=(kk == G - 1))
            if kk == G - 1:
                p.inc(ins, "mm", 1)
    blk.tensor(s_mmsum)

    def s_st(e):
        p.w(e, "mm")
        e.tensor_scalar(fr2[0:1, 0:W], u.PS1[0:1, 0:W], 1.0 / 1024.0, None, OP.mult)
        e.drain()
        e.tensor_mul(fr1[0:1, 0:W], fr2[0:1, 0:W], fr2[0:1, 0:W])
        e.drain()
        e.scalar_tensor_tensor(out=fr1[0:1, 0:W], in0=u.PS0[0:1, 0:W], scalar=1.0 / 1024.0,
                               in1=fr1[0:1, 0:W], op0=OP.mult, op1=OP.subtract)
        e.drain()
        e.tensor_scalar(fr1[0:1, 0:W], fr1[0:1, 0:W], eps, None, OP.add)
        e.drain()
        e.reciprocal(fr1[0:1, 0:W], fr1[0:1, 0:W])
        e.drain()
        e.sem_inc(p.sems["ev"], 1)
    blk.vector(s_st)
    p.cnt["ev"] += 1

    def s_sqr(e):
        p.w(e, "ev")
        e.sqrt(fr1[0:1, 0:W], fr1[0:1, 0:W])
        e.drain()
        e.sem_inc(p.sems["ex"], 1)
    blk.scalar(s_sqr)
    p.cnt["ex"] += 1

    def s_nm(e):
        p.w(e, "ex")
        e.scalar_tensor_tensor(out=fr2[0:1, 0:W], in0=fr2[0:1, 0:W], scalar=-1.0,
                               in1=fr1[0:1, 0:W], op0=OP.mult, op1=OP.mult)
        e.drain()
        e.sem_inc(p.sems["ev"], 1)
    blk.vector(s_nm)
    p.cnt["ev"] += 1

    for kk in range(G):
        def s_r1(e, kk=kk):
            p.w(e, "ev")
            if grow is not None:
                p.inc(nc.tensor.matmul(PSa[:, 0:W], grow[0:1, kk * 128:(kk + 1) * 128], fr1[0:1, 0:W],
                                       start=True, stop=True), "mm", 1)
                nc.tensor.matmul(PSb[:, 0:W], brow[0:1, kk * 128:(kk + 1) * 128], u.onesN[0:1, 0:W],
                                 start=True, stop=False)
                p.inc(nc.tensor.matmul(PSb[:, 0:W], grow[0:1, kk * 128:(kk + 1) * 128], fr2[0:1, 0:W],
                                       start=False, stop=True), "mm", 1)
            else:
                p.inc(nc.tensor.matmul(PSa[:, 0:W], u.ones_row[:, :], fr1[0:1, 0:W],
                                       start=True, stop=True), "mm", 1)
                p.inc(nc.tensor.matmul(PSb[:, 0:W], u.ones_row[:, :], fr2[0:1, 0:W],
                                       start=True, stop=True), "mm", 1)
        blk.tensor(s_r1)

        def s_ap(e, kk=kk):
            p.w(e, "mm")
            e.scalar_tensor_tensor(out=SQ1[:, 0:W], in0=XT[:, kk * W:(kk + 1) * W], scalar=1.0,
                                   in1=PSa[:, 0:W], op0=OP.mult, op1=OP.mult)
            e.drain()
            e.tensor_add(OUT[:, kk * W:(kk + 1) * W], SQ1[:, 0:W], PSb[:, 0:W])
            e.drain()
            e.sem_inc(p.sems["ev"], 1)
        blk.vector(s_ap)
        p.cnt["ev"] += 1
        p.barrier()


def emit_trig(p, u, PIN, pofs, POUT, cofs, sofs, W, TMP1, TMP2):
    nc, blk = p.nc, p.blk
    TWO_PI = float(2.0 * math.pi)
    MAGIC = 12582912.0

    def s_t(e):
        p.w(e, "mm")
        p.w(e, "ex")
        e.tensor_scalar(TMP1[:, 0:W], PIN, 0.125, None, OP.mult)
        e.drain()
        e.tensor_scalar(TMP2[:, 0:W], TMP1[:, 0:W], 1.0 / TWO_PI, MAGIC, OP.mult, OP.add)
        e.drain()
        e.tensor_scalar(TMP2[:, 0:W], TMP2[:, 0:W], MAGIC, None, OP.subtract)
        e.drain()
        e.scalar_tensor_tensor(out=TMP2[:, 0:W], in0=TMP2[:, 0:W], scalar=-TWO_PI,
                               in1=TMP1[:, 0:W], op0=OP.mult, op1=OP.add)
        e.drain()
        e.sem_inc(p.sems["ev"], 1)
    blk.vector(s_t)
    p.cnt["ev"] += 1

    def s_sin(e):
        p.w(e, "ev")
        e.activation(out=POUT[:, sofs:sofs + W], in_=TMP2[:, 0:W], func=AF.Sin)
        e.drain()
        e.sem_inc(p.sems["ex"], 1)
    blk.scalar(s_sin)
    p.cnt["ex"] += 1

    def s_t2(e):
        p.w(e, "ex")
        e.tensor_scalar(TMP2[:, 0:W], TMP1[:, 0:W], math.pi / 2.0, None, OP.add)
        e.drain()
        e.tensor_scalar(TMP1[:, 0:W], TMP2[:, 0:W], 1.0 / TWO_PI, MAGIC, OP.mult, OP.add)
        e.drain()
        e.tensor_scalar(TMP1[:, 0:W], TMP1[:, 0:W], MAGIC, None, OP.subtract)
        e.drain()
        e.scalar_tensor_tensor(out=TMP1[:, 0:W], in0=TMP1[:, 0:W], scalar=-TWO_PI,
                               in1=TMP2[:, 0:W], op0=OP.mult, op1=OP.add)
        e.drain()
        e.sem_inc(p.sems["ev"], 1)
    blk.vector(s_t2)
    p.cnt["ev"] += 1

    def s_cos(e):
        p.w(e, "ev")
        e.activation(out=POUT[:, cofs:cofs + W], in_=TMP1[:, 0:W], func=AF.Sin)
        e.drain()
        e.sem_inc(p.sems["ex"], 1)
    blk.scalar(s_cos)
    p.cnt["ex"] += 1


def build_l1(p, E, u):
    nc, blk = p.nc, p.blk
    PS0, PS1 = u.PS0, u.PS1

    pk = Scope(nc)
    WK = pk.sb("WK", [128, G * 1024], BF16)
    WV = pk.sb("WV", [128, G * 1024], BF16)
    PROJ = pk.sb("PROJ", [128, 128], BF16)
    XT = pk.sb("XT", [128, G * 512])
    XLN = pk.sb("XLN", [128, G * 512], BF16)
    KF = pk.sb("KF", [128, G * 512], BF16)
    VTK = pk.sb("VTK", [128, 4 * 1024], BF16)
    KP = pk.sb("KP", [128, 2 * 256], BF16)
    TMP1 = pk.sb("TMP1", [128, 256]); TMP2 = pk.sb("TMP2", [128, 256])
    SQ1 = pk.sb("SQ1", [128, 512])
    fr1 = pk.sb("fr1", [1, 512]); fr2 = pk.sb("fr2", [1, 512])
    g1row = pk.sb("g1row", [1, D]); b1row = pk.sb("b1row", [1, D])
    bkc = pk.sb("bkc", [128, G])
    BVP = pk.sb("BVP", [128, 1024])
    bvrow = pk.sb("bvrow", [1, D])
    KVS = pk.sb("KVS", [128, 16 * 128])
    KVACC = pk.sb("KVACC", [128, 16 * 128])
    PSkv0 = pk.ps("PSkv0", [128, 1024])
    PSkv1 = pk.ps("PSkv1", [128, 1024])
    PSa = u.PS0
    PSb = u.PS1

    def s_wk(e):
        for kk in range(G):
            p.inc(e.dma_start(out=WK[:, kk * 1024:(kk + 1) * 1024],
                              in_=E["a_wk"][kk * 128:(kk + 1) * 128, :]), "ld", 16)
            p.inc(e.dma_start(out=WV[:, kk * 1024:(kk + 1) * 1024],
                              in_=E["a_wv"][kk * 128:(kk + 1) * 128, :]), "ld", 16)
            p.inc(e.dma_start(out=bkc[:, kk:kk + 1], in_=E["a_bk"][kk * 128:(kk + 1) * 128, :]), "ld", 16)
        p.inc(e.dma_start(out=PROJ[0:64, :], in_=E["a_proj"][:, :]), "ld", 16)
        p.inc(e.dma_start(out=PROJ[64:128, :], in_=E["a_proj"][:, :]), "ld", 16)
    blk.gpsimd(s_wk)

    def s_row(e):
        p.inc(e.dma_start(out=g1row[:, :], in_=E["a_ln1_g"][:, :]), "ld", 16)
        p.inc(e.dma_start(out=b1row[:, :], in_=E["a_ln1_b"][:, :]), "ld", 16)
        p.inc(e.dma_start(out=bvrow[:, :], in_=E["a_bv"][:, :]), "ld", 16)
    blk.sync(s_row)
    p.barrier()

    def s_bvp(e):
        p.inc(nc.tensor.matmul(PSa[:, 0:512], u.ones_row[:, :], bvrow[0:1, 0:512],
                               start=True, stop=True), "mm", 1)
        p.inc(nc.tensor.matmul(PSa[:, 512:1024], u.ones_row[:, :], bvrow[0:1, 512:1024],
                               start=True, stop=True), "mm", 1)
    blk.tensor(s_bvp)

    def s_bvp2(e):
        p.w(e, "mm")
        e.tensor_copy(BVP[:, :], PSa[:, 0:1024])
        e.drain()
        e.memset(KVACC[:, :], 0.0)
        e.drain()
        e.sem_inc(p.sems["ev"], 1)
    blk.vector(s_bvp2)
    p.cnt["ev"] += 1
    p.barrier()

    for tt in range(NT):
        c0 = tt * TT

        def s_lx(e, c0=c0):
            for kk in range(G):
                p.inc(e.dma_start(out=XT[:, kk * 512:(kk + 1) * 512],
                                  in_=E["x2_d"][kk * 128:(kk + 1) * 128, c0:c0 + 512]), "ld", 16)
        blk.sync(s_lx)

        def s_wait(e):
            p.w(e, "ld")
            e.engine_nop()
            e.drain()
            e.sem_inc(p.sems["ev"], 1)
        blk.vector(s_wait)
        p.cnt["ev"] += 1
        p.barrier()

        emit_ln_sb(p, u, XT, 512, XLN, g1row, b1row, fr1, fr2, SQ1, u.PS0[:, 512:1024], u.PS1[:, 512:1024])

        def s_stln(e, c0=c0):
            for kk in range(G):
                p.inc(e.dma_start(out=E["x2ln_d"][kk * 128:(kk + 1) * 128, c0:c0 + 512],
                                  in_=XLN[:, kk * 512:(kk + 1) * 512]), "st", 16)
        blk.gpsimd(s_stln)

        for oc in range(G):
            def s_kmm(e, oc=oc):
                for kk in range(G):
                    ins = nc.tensor.matmul(PSa[:, (oc % 2) * 512:(oc % 2) * 512 + 512],
                                           WK[:, kk * 1024 + oc * 128:kk * 1024 + (oc + 1) * 128],
                                           XLN[:, kk * 512:(kk + 1) * 512],
                                           start=(kk == 0), stop=(kk == G - 1))
                    if kk == G - 1:
                        p.inc(ins, "mm", 1)
            blk.tensor(s_kmm)

            def s_kev(e, oc=oc):
                p.w(e, "mm")
                e.tensor_scalar(KF[:, oc * 512:(oc + 1) * 512], PSa[:, (oc % 2) * 512:(oc % 2) * 512 + 512],
                                bkc[:, oc:oc + 1], None, OP.add)
                e.drain()
                e.sem_inc(p.sems["ev"], 1)
            blk.vector(s_kev)
            p.cnt["ev"] += 1
            p.barrier()

        for vc in range(4):
            def s_vmm(e, vc=vc):
                for kk in range(G):
                    ins = nc.tensor.matmul(PSb[:, 0:512],
                                           XLN[:, kk * 512 + vc * 128:kk * 512 + (vc + 1) * 128],
                                           WV[:, kk * 1024:kk * 1024 + 512],
                                           start=(kk == 0), stop=(kk == G - 1))
                    ins = nc.tensor.matmul(PSb[:, 512:1024],
                                           XLN[:, kk * 512 + vc * 128:kk * 512 + (vc + 1) * 128],
                                           WV[:, kk * 1024 + 512:(kk + 1) * 1024],
                                           start=(kk == 0), stop=(kk == G - 1))
                    if kk == G - 1:
                        p.inc(ins, "mm", 1)
            blk.tensor(s_vmm)

            def s_vev(e, vc=vc):
                p.w(e, "mm")
                e.tensor_add(VTK[:, vc * 1024:(vc + 1) * 1024], PSb[:, 0:1024], BVP[:, :])
                e.drain()
                e.sem_inc(p.sems["ev"], 1)
            blk.vector(s_vev)
            p.cnt["ev"] += 1
            p.barrier()

        for sub in range(4):
            for hp in range(8):
                def s_pmm(e, sub=sub, hp=hp):
                    for hh in range(2):
                        b0 = hh * 64
                        ins = nc.tensor.matmul(PSb[:, hh * 512:hh * 512 + 128],
                                               KF[b0:b0 + 64, hp * 512 + sub * 128:hp * 512 + (sub + 1) * 128],
                                               PROJ[b0:b0 + 64, :],
                                               start=True, stop=True)
                        p.inc(ins, "mm", 1)
                blk.tensor(s_pmm)

                for hh in range(2):
                    emit_trig(p, u, PSb[:, hh * 512:hh * 512 + 128], 0,
                              KP, hh * 256, hh * 256 + 128, 128, TMP1, TMP2)

                def s_kvmm(e, sub=sub, hp=hp):
                    p.w(e, "ex")
                    for hh in range(2):
                        h = hp * 2 + hh
                        pskv = PSkv0 if h < 8 else PSkv1
                        hb = h % 8
                        for mc in range(2):
                            ins = nc.tensor.matmul(
                                pskv[:, hb * 128 + mc * 64:hb * 128 + (mc + 1) * 64],
                                KP[:, hh * 256 + mc * 128:hh * 256 + (mc + 1) * 128],
                                VTK[:, sub * 1024 + h * 64:sub * 1024 + (h + 1) * 64],
                                start=True, stop=True)
                            if hp == 7 and hh == 1 and mc == 1:
                                p.inc(ins, "mm", 1)
                blk.tensor(s_kvmm)
                p.barrier()

            def s_kvacc(e, sub=sub):
                e.tensor_add(KVACC[:, 0:1024], KVACC[:, 0:1024], PSkv0[:, 0:1024])
                e.drain()
                e.tensor_add(KVACC[:, 1024:2048], KVACC[:, 1024:2048], PSkv1[:, 0:1024])
                e.drain()
                e.sem_inc(p.sems["ev"], 1)
            blk.vector(s_kvacc)
            p.cnt["ev"] += 1
            p.barrier()

    def s_kvc(e):
        p.inc(e.dma_start(out=E["kv_snd"][:, :], in_=KVACC[:, :]), "st", 16)
        p.w(e, "st")
        p.inc(e.collective_compute(
            "AllReduce", OP.add,
            replica_groups=[[0, 1], [2, 3], [4, 5], [6, 7]],
            ins=[E["kv_snd"].opt()], outs=[E["kv_rcv"].opt()],
        ), "cc", 1)
        p.w(e, "cc")
        p.inc(e.dma_start(out=KVS[:, :], in_=E["kv_rcv"][:, :]), "ld", 16)
    blk.gpsimd(s_kvc)
    p.barrier()

    def s_kvb(e):
        p.w(e, "ld")
        e.tensor_copy(XLN[:, 0:2048], KVS[:, :])
        e.drain()
        e.sem_inc(p.sems["ev"], 1)
    blk.vector(s_kvb)
    p.cnt["ev"] += 1

    def s_kvs(e):
        p.w(e, "ev")
        p.inc(e.dma_start(out=E["kvb_d"][:, :], in_=XLN[:, 0:2048]), "st", 16)
    blk.gpsimd(s_kvs)
    p.barrier()
    pk.close()

    if hasattr(p, 'marks'): p.marks.append(('l1.K', nc.n_instructions()))
    qa = Scope(nc)
    WQ = qa.sb("WQ", [128, G * 1024], BF16)
    WO1 = qa.sb("WO1", [128, G * 1024], BF16)
    PROJ2 = qa.sb("PROJ2", [128, 128], BF16)
    KVB = qa.sb("KVB", [128, 2048], BF16)
    XLN2 = qa.sb("XLN2", [128, G * 512], BF16)
    QF = qa.sb("QF", [128, G * 512], BF16)
    QP = qa.sb("QP", [128, 2 * 512], BF16)
    ATT = qa.sb("ATT", [128, G * 512], BF16)
    XR = qa.sb("XR", [128, G * 512])
    TMP1q = qa.sb("TMP1q", [128, 512]); TMP2q = qa.sb("TMP2q", [128, 512])
    bqc = qa.sb("bqc", [128, G]); boc = qa.sb("boc", [128, G])
    PSq = qa.ps("PSq", [128, 1024])
    PSat = qa.ps("PSat", [128, 1024])

    def s_wq(e):
        for kk in range(G):
            p.inc(e.dma_start(out=WQ[:, kk * 1024:(kk + 1) * 1024],
                              in_=E["a_wq"][kk * 128:(kk + 1) * 128, :]), "ld", 16)
            p.inc(e.dma_start(out=WO1[:, kk * 1024:(kk + 1) * 1024],
                              in_=E["a_wo"][kk * 128:(kk + 1) * 128, :]), "ld", 16)
            p.inc(e.dma_start(out=bqc[:, kk:kk + 1], in_=E["a_bq"][kk * 128:(kk + 1) * 128, :]), "ld", 16)
            p.inc(e.dma_start(out=boc[:, kk:kk + 1], in_=E["a_bo"][kk * 128:(kk + 1) * 128, :]), "ld", 16)
        p.inc(e.dma_start(out=PROJ2[0:64, :], in_=E["a_proj"][:, :]), "ld", 16)
        p.inc(e.dma_start(out=PROJ2[64:128, :], in_=E["a_proj"][:, :]), "ld", 16)
        p.inc(e.dma_start(out=KVB[:, :], in_=E["kvb_d"][:, :]), "ld", 16)
    blk.gpsimd(s_wq)
    p.barrier()

    for tt in range(NT):
        c0 = tt * TT

        def s_lx(e, c0=c0):
            p.w(e, "st")
            for kk in range(G):
                p.inc(e.dma_start(out=XLN2[:, kk * 512:(kk + 1) * 512],
                                  in_=E["x2ln_d"][kk * 128:(kk + 1) * 128, c0:c0 + 512]), "ld", 16)
                p.inc(e.dma_start(out=XR[:, kk * 512:(kk + 1) * 512],
                                  in_=E["x2_d"][kk * 128:(kk + 1) * 128, c0:c0 + 512]), "ld", 16)
        blk.sync(s_lx)

        for oc in range(G):
            def s_qmm(e, oc=oc):
                p.w(e, "ld")
                for kk in range(G):
                    ins = nc.tensor.matmul(PSq[:, (oc % 2) * 512:(oc % 2) * 512 + 512],
                                           WQ[:, kk * 1024 + oc * 128:kk * 1024 + (oc + 1) * 128],
                                           XLN2[:, kk * 512:(kk + 1) * 512],
                                           start=(kk == 0), stop=(kk == G - 1))
                    if kk == G - 1:
                        p.inc(ins, "mm", 1)
            blk.tensor(s_qmm)

            def s_qev(e, oc=oc):
                p.w(e, "mm")
                e.tensor_scalar(QF[:, oc * 512:(oc + 1) * 512], PSq[:, (oc % 2) * 512:(oc % 2) * 512 + 512],
                                bqc[:, oc:oc + 1], None, OP.add)
                e.drain()
                e.sem_inc(p.sems["ev"], 1)
            blk.vector(s_qev)
            p.cnt["ev"] += 1
            p.barrier()

        for h in range(16):
            def s_pf(e, h=h):
                b0 = (h % 2) * 64
                p.inc(nc.tensor.matmul(PSq[:, 0:512],
                                       PROJ2[b0:b0 + 64, :],
                                       QF[b0:b0 + 64, (h // 2) * 512:(h // 2 + 1) * 512],
                                       start=True, stop=True), "mm", 1)
            blk.tensor(s_pf)

            emit_trig(p, u, PSq[:, 0:512], 0, QP, 0, 512, 512, TMP1q, TMP2q)

            def s_att(e, h=h):
                p.w(e, "ex")
                for mc in range(2):
                    ins = nc.tensor.matmul(PSat[((h % 2)) * 64:((h % 2)) * 64 + 64,
                                                ((h // 2) % 2) * 512:((h // 2) % 2) * 512 + 512],
                                           KVB[:, h * 128 + mc * 64:h * 128 + (mc + 1) * 64],
                                           QP[:, mc * 512:(mc + 1) * 512],
                                           start=(mc == 0), stop=(mc == 1),
                                           tile_position=(0, (h % 2) * 64))
                    if mc == 1:
                        p.inc(ins, "mm", 1)
            blk.tensor(s_att)

            if h % 4 == 3:
                def s_atev(e, h=h):
                    p.w(e, "mm")
                    pair0 = h // 2 - 1
                    pair1 = h // 2
                    e.tensor_scalar(ATT[:, pair0 * 512:pair0 * 512 + 512],
                                    PSat[:, 0:512], 1.0 / 128.0, None, OP.mult)
                    e.drain()
                    e.tensor_scalar(ATT[:, pair1 * 512:pair1 * 512 + 512],
                                    PSat[:, 512:1024], 1.0 / 128.0, None, OP.mult)
                    e.drain()
                    e.sem_inc(p.sems["ev"], 1)
                blk.vector(s_atev)
                p.cnt["ev"] += 1
            p.barrier()

        for oc in range(G):
            def s_womm(e, oc=oc):
                for kk in range(G):
                    ins = nc.tensor.matmul(PSq[:, (oc % 2) * 512:(oc % 2) * 512 + 512],
                                           WO1[:, kk * 1024 + oc * 128:kk * 1024 + (oc + 1) * 128],
                                           ATT[:, kk * 512:(kk + 1) * 512],
                                           start=(kk == 0), stop=(kk == G - 1))
                    if kk == G - 1:
                        p.inc(ins, "mm", 1)
            blk.tensor(s_womm)

            def s_res(e, oc=oc):
                p.w(e, "mm")
                e.scalar_tensor_tensor(out=XR[:, oc * 512:(oc + 1) * 512],
                                       in0=PSq[:, (oc % 2) * 512:(oc % 2) * 512 + 512],
                                       scalar=boc[:, oc:oc + 1], in1=XR[:, oc * 512:(oc + 1) * 512],
                                       op0=OP.add, op1=OP.add)
                e.drain()
                e.sem_inc(p.sems["ev"], 1)
            blk.vector(s_res)
            p.cnt["ev"] += 1
            p.barrier()

        def s_sx(e, c0=c0):
            for kk in range(G):
                p.inc(e.dma_start(out=E["xq_d"][kk * 128:(kk + 1) * 128, c0:c0 + 512],
                                  in_=XR[:, kk * 512:(kk + 1) * 512]), "st", 16)
        blk.sync(s_sx)
        p.barrier()

        if tt == 0 and "dbgatt" in E.get("_taps", {}):
            def g_da(e):
                p.inc(e.dma_start(out=E["_taps"]["dbgatt"][:, :], in_=ATT[:, :]), "st", 16)
                p.inc(e.dma_start(out=E["_taps"]["dbgqf"][:, :], in_=QF[:, :]), "st", 16)
                p.inc(e.dma_start(out=E["_taps"]["dbgqp"][:, :], in_=QP[:, :]), "st", 16)
            blk.gpsimd(g_da)
            p.barrier()
    qa.close()

    if hasattr(p, 'marks'): p.marks.append(('l1.QA', nc.n_instructions()))
    qf = Scope(nc)
    W1L = qf.sb("W1L", [128, G * 4096], BF16)
    W2L = qf.sb("W2L", [128, 32 * 1024], BF16)
    XT2 = qf.sb("XT2", [128, G * 512])
    XB2 = qf.sb("XB2", [128, G * 512], BF16)
    HH2 = qf.sb("HH2", [128, 32 * 512], BF16)
    SQ2 = qf.sb("SQ2", [128, 512])
    fr1b = qf.sb("fr1b", [1, 512]); fr2b = qf.sb("fr2b", [1, 512])
    g2row = qf.sb("g2row", [1, D]); b2row = qf.sb("b2row", [1, D])
    b1c = qf.sb("b1c", [128, 32]); b2c2 = qf.sb("b2c2", [128, G])
    PSc = qf.ps("PSc", [128, 1024])
    PSd = qf.ps("PSd", [128, 1024])

    def s_wf(e):
        for kk in range(G):
            p.inc(e.dma_start(out=W1L[:, kk * 4096:(kk + 1) * 4096],
                              in_=E["a_w1"][kk * 128:(kk + 1) * 128, :]), "ld", 16)
        for hc in range(32):
            p.inc(e.dma_start(out=W2L[:, hc * 1024:(hc + 1) * 1024],
                              in_=E["a_w2"][hc * 128:(hc + 1) * 128, :]), "ld", 16)
            p.inc(e.dma_start(out=b1c[:, hc:hc + 1], in_=E["a_b1"][hc * 128:(hc + 1) * 128, :]), "ld", 16)
        for kk in range(G):
            p.inc(e.dma_start(out=b2c2[:, kk:kk + 1], in_=E["a_b2"][kk * 128:(kk + 1) * 128, :]), "ld", 16)
    blk.gpsimd(s_wf)

    def s_row2(e):
        p.inc(e.dma_start(out=g2row[:, :], in_=E["a_ln2_g"][:, :]), "ld", 16)
        p.inc(e.dma_start(out=b2row[:, :], in_=E["a_ln2_b"][:, :]), "ld", 16)
    blk.sync(s_row2)
    p.barrier()

    for tt in range(NT):
        c0 = tt * TT

        def s_lx(e, c0=c0):
            p.w(e, "st")
            for kk in range(G):
                p.inc(e.dma_start(out=XT2[:, kk * 512:(kk + 1) * 512],
                                  in_=E["xq_d"][kk * 128:(kk + 1) * 128, c0:c0 + 512]), "ld", 16)
        blk.sync(s_lx)

        def s_wt(e):
            p.w(e, "ld")
            e.engine_nop()
            e.drain()
            e.sem_inc(p.sems["ev"], 1)
        blk.vector(s_wt)
        p.cnt["ev"] += 1
        p.barrier()

        emit_ln_sb(p, u, XT2, 512, XB2, g2row, b2row, fr1b, fr2b, SQ2, u.PS0[:, 512:1024], u.PS1[:, 512:1024])

        for hc in range(32):
            def s_w1(e, hc=hc):
                for kk in range(G):
                    ins = nc.tensor.matmul(PSc[:, (hc % 2) * 512:(hc % 2) * 512 + 512],
                                           W1L[:, kk * 4096 + hc * 128:kk * 4096 + (hc + 1) * 128],
                                           XB2[:, kk * 512:(kk + 1) * 512],
                                           start=(kk == 0), stop=(kk == G - 1))
                    if kk == G - 1:
                        p.inc(ins, "mm", 1)
            blk.tensor(s_w1)

            def s_rl(e, hc=hc):
                p.w(e, "mm")
                e.activation(out=HH2[:, hc * 512:(hc + 1) * 512],
                             in_=PSc[:, (hc % 2) * 512:(hc % 2) * 512 + 512],
                             func=AF.Relu, bias=b1c[:, hc:hc + 1], scale=1.0)
                e.drain()
                e.sem_inc(p.sems["ex"], 1)
            blk.scalar(s_rl)
            p.cnt["ex"] += 1
            p.barrier()

        for oc in range(G):
            def s_w2(e, oc=oc):
                for hc in range(32):
                    ins = nc.tensor.matmul(PSd[:, (oc % 2) * 512:(oc % 2) * 512 + 512],
                                           W2L[:, hc * 1024 + oc * 128:hc * 1024 + (oc + 1) * 128],
                                           HH2[:, hc * 512:(hc + 1) * 512],
                                           start=(hc == 0), stop=(hc == 31))
                    if hc == 31:
                        p.inc(ins, "mm", 1)
            blk.tensor(s_w2)

            def s_o(e, oc=oc):
                p.w(e, "mm")
                e.scalar_tensor_tensor(out=XT2[:, oc * 512:(oc + 1) * 512],
                                       in0=PSd[:, (oc % 2) * 512:(oc % 2) * 512 + 512],
                                       scalar=b2c2[:, oc:oc + 1], in1=XT2[:, oc * 512:(oc + 1) * 512],
                                       op0=OP.add, op1=OP.add)
                e.drain()
                e.sem_inc(p.sems["ev"], 1)
            blk.vector(s_o)
            p.cnt["ev"] += 1
            p.barrier()

        def s_sx(e, c0=c0):
            for kk in range(G):
                p.inc(e.dma_start(out=E["x3_d"][kk * 128:(kk + 1) * 128, c0:c0 + 512],
                                  in_=XT2[:, kk * 512:(kk + 1) * 512]), "st", 16)
        blk.sync(s_sx)
        p.barrier()
    qf.close()


def build_l1f(p, E, u, out_fm):
    nc, blk = p.nc, p.blk
    zf = Scope(nc)
    WS = zf.sb("WS", [128, G * 1024])
    SGN = zf.sb("SGN", [128, G * 1024], BF16)
    XT3 = zf.sb("XT3", [128, G * 512])
    XB3 = zf.sb("XB3", [128, G * 512], BF16)
    SQ3 = zf.sb("SQ3", [128, 512])
    fr1c = zf.sb("fr1c", [1, 512]); fr2c = zf.sb("fr2c", [1, 512])
    PSe = zf.ps("PSe", [128, 1024])
    PSf = zf.ps("PSf", [128, 1024])

    def s_w(e):
        for kk in range(G):
            p.inc(e.dma_start(out=WS[:, kk * 1024:(kk + 1) * 1024],
                              in_=E["z_wT"][kk * 128:(kk + 1) * 128, :]), "ld", 16)
    blk.sync(s_w)

    def s_sgn(e):
        p.w(e, "ld")
        e.sign(SGN[:, :], WS[:, :])
        e.drain()
        e.sem_inc(p.sems["ex"], 1)
    blk.scalar(s_sgn)
    p.cnt["ex"] += 1
    p.barrier()

    for tt in range(NT):
        c0 = tt * TT

        def s_lx(e, c0=c0):
            p.w(e, "st")
            for kk in range(G):
                p.inc(e.dma_start(out=XT3[:, kk * 512:(kk + 1) * 512],
                                  in_=E["x3_d"][kk * 128:(kk + 1) * 128, c0:c0 + 512]), "ld", 16)
        blk.sync(s_lx)

        def s_wt(e):
            p.w(e, "ld")
            e.engine_nop()
            e.drain()
            e.sem_inc(p.sems["ev"], 1)
        blk.vector(s_wt)
        p.cnt["ev"] += 1
        p.barrier()

        emit_ln_sb(p, u, XT3, 512, XB3, None, None, fr1c, fr2c, SQ3, u.PS0[:, 512:1024], u.PS1[:, 512:1024])

        for oc in range(G):
            def s_mm(e, oc=oc):
                for kk in range(G):
                    ins = nc.tensor.matmul(PSe[:, (oc % 2) * 512:(oc % 2) * 512 + 512],
                                           SGN[:, kk * 1024 + oc * 128:kk * 1024 + (oc + 1) * 128],
                                           XB3[:, kk * 512:(kk + 1) * 512],
                                           start=(kk == 0), stop=(kk == G - 1))
                    if kk == G - 1:
                        p.inc(ins, "mm", 1)
            blk.tensor(s_mm)

            def s_o(e, oc=oc):
                p.w(e, "mm")
                e.scalar_tensor_tensor(out=XT3[:, oc * 512:(oc + 1) * 512],
                                       in0=PSe[:, (oc % 2) * 512:(oc % 2) * 512 + 512],
                                       scalar=1.0, in1=XT3[:, oc * 512:(oc + 1) * 512],
                                       op0=OP.mult, op1=OP.add)
                e.drain()
                e.sem_inc(p.sems["ev"], 1)
            blk.vector(s_o)
            p.cnt["ev"] += 1
            p.barrier()

        def s_sx(e, c0=c0):
            for kk in range(G):
                p.inc(e.dma_start(out=out_fm[kk * 128:(kk + 1) * 128, c0:c0 + 512],
                                  in_=XT3[:, kk * 512:(kk + 1) * 512]), "st", 16)
        blk.sync(s_sx)
        p.barrier()
    zf.close()


def build_program(taps=()):
    p = Prog()
    nc = p.nc

    ein = lambda nm, shp, dt=F32: nc.dram_tensor(nm, shp, dt, kind="ExternalInput").ap()
    dscr = lambda nm, shp, dt=F32: nc.dram_tensor(nm, shp, dt).ap()

    E = {}
    E["xin"] = ein("xin", [D, XH + T])
    for nm in ("w_dt", "w_v", "w_vs", "w_gate", "hm_dense", "outproj"):
        E[nm] = ein(nm, [D, D])
    E["convw"] = ein("convw", [D, 4])
    for nm in ("convb", "nrmsc", "dtbias"):
        E[nm] = ein(nm, [D, 1])
    E["ropecos"] = ein("ropecos", [128, T])
    E["ropesin"] = ein("ropesin", [128, T])
    for nm in ("flg_take", "flg_send", "flg_negbig"):
        E[nm] = ein(nm, [128, 1])
    E["c_eblk"] = ein("c_eblk", [128, 2])
    E["c_eblkT"] = ein("c_eblkT", [2, 128])

    E["g_d"] = dscr("g_d", [D, T])
    E["vin_d"] = dscr("vin_d", [D, T])
    E["x1_d"] = dscr("x1_d", [D, T])
    E["xln_d"] = dscr("xln_d", [D, T])
    E["snd_d"] = dscr("snd_d", [128, 24])
    E["rcv_d"] = dscr("rcv_d", [128, 24])

    for l in range(2):
        pre = "f%d_" % l
        for nm, shp in (("wq", [D, 512]), ("wqs", [D, 512]), ("wk", [D, 512]),
                        ("wks", [D, 512]), ("wv", [D, 512]), ("wo", [512, D]),
                        ("w1a", [D, HIDP]), ("w1g", [D, HIDP]), ("w2", [HIDP, D]),
                        ("b1a", [HIDP, 1]), ("b1g", [HIDP, 1]), ("b2", [D, 1]),
                        ("lg", [1, D]), ("lb", [1, D])):
            E[pre + nm] = ein(pre + nm, shp)
    E["f_qscale"] = ein("f_qscale", [2, 64, 1])
    E["f_kscale"] = ein("f_kscale", [2, 64, 1])
    E["f_cq"] = ein("f_cq", [128, T], BF16)
    E["f_sq"] = ein("f_sq", [128, T], BF16)
    E["f_ck"] = ein("f_ck", [128, KS], BF16)
    E["f_sk"] = ein("f_sk", [128, KS], BF16)
    E["f_mask4"] = ein("f_mask4", [4, 128, 640])
    E["f_m0"] = ein("f_m0", [128, 128])
    E["f_m4"] = ein("f_m4", [128, 128])

    E["hx1_d"] = dscr("hx1_d", [D, KH])
    E["xi0_d"] = dscr("xi0_d", [D, 2 + KS])
    E["xi1_d"] = dscr("xi1_d", [D, 2 + KS])
    E["xi2_d"] = dscr("xi2_d", [D, T])
    E["xp0_d"] = dscr("xp0_d", [D, 1 + T])
    E["xp1_d"] = dscr("xp1_d", [D, 1 + T])
    E["x2_d"] = dscr("x2_d", [D, T])
    E["snd_h"] = dscr("snd_h", [D, KH])
    E["rcv_h"] = dscr("rcv_h", [D, KH])
    E["snd_s"] = dscr("snd_s", [D, 8])
    E["rcv_s"] = dscr("rcv_s", [D, 8])

    for nm, shp in (("a_wq", [D, D]), ("a_wk", [D, D]), ("a_wv", [D, D]), ("a_wo", [D, D]),
                    ("a_bq", [D, 1]), ("a_bk", [D, 1]), ("a_bo", [D, 1]), ("a_b2", [D, 1]),
                    ("a_bv", [1, D]), ("a_b1", [4096, 1]),
                    ("a_w1", [D, 4096]), ("a_w2", [4096, D]),
                    ("a_ln1_g", [1, D]), ("a_ln1_b", [1, D]), ("a_ln2_g", [1, D]), ("a_ln2_b", [1, D]),
                    ("a_proj", [64, 128]), ("z_wT", [D, D])):
        E[nm] = ein(nm, shp)

    E["x2ln_d"] = dscr("x2ln_d", [D, T], BF16)
    E["kv_snd"] = dscr("kv_snd", [128, 2048])
    E["kv_rcv"] = dscr("kv_rcv", [128, 2048])
    E["kvb_d"] = dscr("kvb_d", [128, 2048], BF16)
    E["xq_d"] = dscr("xq_d", [D, T])
    E["x3_d"] = dscr("x3_d", [D, T])

    out_fm = nc.dram_tensor("out_fm", [D, T], F32, kind="ExternalOutput").ap()
    tap_aps = {}
    _tshapes = {"dbgkrot": [128, 4 * KS], "dbgvt": [128, NKC * 512],
                "dbgqrot": [128, 4 * 512], "dbgo": [128, 4 * 512], "dbgp": [128, 2 * 640],
                "dbgatt": [128, G * 512], "dbgqf": [128, G * 512], "dbgqp": [128, 2 * 512],
                "dbgo2": [128, 4 * 512], "dbgxpr": [128, G * 512]}
    for tname in taps:
        shp = list(E[tname].tensor.shape) if tname in E else _tshapes.get(tname, [D, T])
        tap_aps[tname] = nc.dram_tensor("tap_" + tname, shp, F32, kind="ExternalOutput").ap()

    with nc.Block() as blk:
        p.blk = blk
        for s in ("ld", "mm", "ev", "ex", "st", "bar", "cc", "gp", "tp"):
            p.sem(s)

        E["_taps"] = tap_aps
        u = build_utils(p, E)
        p.marks = [("start", nc.n_instructions())]
        build_l0a(p, E, u)
        p.marks.append(("l0a", nc.n_instructions()))
        build_l0f(p, E, u)
        p.marks.append(("l0f", nc.n_instructions()))
        build_l1(p, E, u)
        p.marks.append(("l1a", nc.n_instructions()))
        build_l1f(p, E, u, out_fm)
        p.marks.append(("l1f", nc.n_instructions()))

        def s_out(e):
            for tname, ap in tap_aps.items():
                if tname in E:
                    p.inc(e.dma_start(out=ap[:, :], in_=E[tname][:, :]), "st", 16)
            p.w(e, "st")
        blk.gpsimd(s_out)

    return p, tap_aps


def kernel(**inputs):
    in_maps = prep_inputs(inputs)
    p, _ = build_program()
    res = run_bass_kernel_spmd(p.nc, in_maps, core_ids=list(range(8)))
    out = np.zeros((B, N, D), np.float32)
    for c in range(8):
        b, hf = c // 2, c % 2
        out[b, hf * T:(hf + 1) * T, :] = res.results[c]["out_fm"].T
    return out


# version 03
# speedup vs baseline: 1.1360x; 1.1360x over previous
import math
import numpy as np
import concourse.bass as bass
import concourse.mybir as mybir
from concourse.bass_utils import run_bass_kernel_spmd

F32 = mybir.dt.float32
BF16 = mybir.dt.bfloat16
AF = mybir.ActivationFunctionType
OP = mybir.AluOpType
AX = mybir.AxisListType

B, N, D = 4, 8192, 1024
T = 4096
XH = 4
CH = 256
NCH = T // CH
SEG = 1024
NSEG = T // SEG
TT = 512
NT = T // TT
G = 8
NEGBIG = -3.0e38
MASKV = -1.0e30
C2_SERIAL = False
C2D = 2
C2BAR = False

KH = 514
KS = 4608
NKC = KS // 128
HIDP = 2816
NHC = HIDP // 128


def _fm(a):
    return np.ascontiguousarray(np.asarray(a, dtype=np.float32).T)


def _xpos_tables(t0):
    import ml_dtypes
    half_idx = np.arange(0, 64, 2, dtype=np.float64)
    inv = 1.0 / 10000.0 ** (half_idx / 64.0)
    base = (half_idx + 0.4 * 64.0) / (1.4 * 64.0)

    def mk(tpos, kind):
        W = len(tpos)
        ct = np.zeros((128, W), np.float32)
        st = np.zeros((128, W), np.float32)
        power = (tpos - 4096.0) / 256.0
        for c in range(64):
            jj = c % 32
            ang = tpos * inv[jj]
            sc = base[jj] ** power
            s = sc if kind == "q" else 1.0 / sc
            cv = (np.cos(ang) * s).astype(np.float32)
            sv = (np.sin(ang) * s).astype(np.float32)
            sgn = -1.0 if c < 32 else 1.0
            for rep in range(2):
                ct[rep * 64 + c] = cv
                st[rep * 64 + c] = sgn * sv
        return ct.astype(ml_dtypes.bfloat16), st.astype(ml_dtypes.bfloat16)

    tq = np.arange(t0, t0 + T, dtype=np.float64)
    tk = np.arange(t0 - 512, t0 + T, dtype=np.float64)
    cq, sq = mk(tq, "q")
    ck, sk = mk(tk, "k")
    return cq, sq, ck, sk


def _mask4(hf):
    m = np.zeros((4, 128, 640), np.float32)
    for qt in range(4):
        for j in range(5):
            mchunk = qt + j - 4
            tkk = mchunk * 128 + np.arange(128)[None, :]
            tqq = qt * 128 + np.arange(128)[:, None]
            ok = (tkk <= tqq) & (tkk >= tqq - 512)
            if hf == 0:
                ok = ok & (tkk >= 0)
            m[qt, :, j * 128:(j + 1) * 128] = np.where(ok, 0.0, MASKV)
    return m


def _rope_tables(t0):
    tpos = np.arange(t0, t0 + T, dtype=np.float64)
    inv = 1.0 / 10000.0 ** (np.arange(0, 64, 2, dtype=np.float64) / 64.0)
    cosr = np.zeros((128, T), np.float32)
    sinr = np.zeros((128, T), np.float32)
    for c in range(64):
        jj = c % 32
        ang = tpos * inv[jj]
        cv = np.cos(ang).astype(np.float32)
        sv = np.sin(ang).astype(np.float32)
        sgn = -1.0 if c < 32 else 1.0
        for rep in range(2):
            cosr[rep * 64 + c] = cv
            sinr[rep * 64 + c] = sgn * sv
    return cosr, sinr


def prep_inputs(inputs):
    x = np.asarray(inputs["L0_attn_x"], dtype=np.float32)
    ipw = np.asarray(inputs["L0_attn_in_proj_w"], dtype=np.float32)
    w_dt = np.ascontiguousarray(ipw[:, 0:D])
    w_v = np.ascontiguousarray(ipw[:, D:2 * D])
    perm = np.array([(f // 64) * 64 + ((f % 64) + 32) % 64 for f in range(D)])
    w_vs = np.ascontiguousarray(w_v[:, perm])
    w_gate = np.ascontiguousarray(ipw[:, 2 * D:3 * D])

    hm = np.asarray(inputs["L0_attn_head_mix"], dtype=np.float32)
    HM = np.zeros((D, D), np.float32)
    HM4 = HM.reshape(16, 64, 16, 64)
    for c in range(64):
        HM4[:, c, :, c] = hm

    common = {
        "w_dt": w_dt, "w_v": w_v, "w_vs": w_vs, "w_gate": w_gate,
        "hm_dense": HM,
        "outproj": np.ascontiguousarray(np.asarray(inputs["L0_attn_out_proj_w"], dtype=np.float32)),
        "convw": np.ascontiguousarray(np.asarray(inputs["L0_attn_conv_w"], dtype=np.float32)[:, 0, :]),
        "convb": np.asarray(inputs["L0_attn_conv_b"], dtype=np.float32).reshape(D, 1),
        "nrmsc": np.asarray(inputs["L0_attn_norm_scale"], dtype=np.float32).reshape(D, 1),
        "dtbias": np.asarray(inputs["L0_attn_dt_bias"], dtype=np.float32).reshape(D, 1),
    }

    wqkv = np.asarray(inputs["L0_ffn_wqkv"], dtype=np.float32)
    permI = np.array([(f // 64) * 64 + ((f % 64) + 32) % 64 for f in range(512)])
    for l in range(2):
        common[f"f{l}_wq"] = np.ascontiguousarray(wqkv[l][:, 0:512])
        common[f"f{l}_wqs"] = np.ascontiguousarray(wqkv[l][:, 0:512][:, permI])
        common[f"f{l}_wk"] = np.ascontiguousarray(wqkv[l][:, 512:1024])
        common[f"f{l}_wks"] = np.ascontiguousarray(wqkv[l][:, 512:1024][:, permI])
        common[f"f{l}_wv"] = np.ascontiguousarray(wqkv[l][:, 1024:1536])
        common[f"f{l}_wo"] = np.ascontiguousarray(np.asarray(inputs["L0_ffn_wo"], dtype=np.float32)[l])
        w1 = np.asarray(inputs["L0_ffn_w1"], dtype=np.float32)[l]
        w1a = np.zeros((D, HIDP), np.float32); w1a[:, :2730] = w1[:, :2730]
        w1g = np.zeros((D, HIDP), np.float32); w1g[:, :2730] = w1[:, 2730:]
        common[f"f{l}_w1a"] = w1a
        common[f"f{l}_w1g"] = w1g
        b1 = np.asarray(inputs["L0_ffn_b1"], dtype=np.float32)[l]
        b1a = np.zeros((HIDP, 1), np.float32); b1a[:2730, 0] = b1[:2730]
        b1g = np.zeros((HIDP, 1), np.float32); b1g[:2730, 0] = b1[2730:]
        common[f"f{l}_b1a"] = b1a
        common[f"f{l}_b1g"] = b1g
        w2 = np.asarray(inputs["L0_ffn_w2"], dtype=np.float32)[l]
        w2p = np.zeros((HIDP, D), np.float32); w2p[:2730] = w2
        common[f"f{l}_w2"] = w2p
        common[f"f{l}_b2"] = np.asarray(inputs["L0_ffn_b2"], dtype=np.float32)[l].reshape(D, 1)
        common[f"f{l}_lg"] = np.asarray(inputs["L0_ffn_ln_g"], dtype=np.float32)[l].reshape(1, D)
        common[f"f{l}_lb"] = np.asarray(inputs["L0_ffn_ln_b"], dtype=np.float32)[l].reshape(1, D)
    common["f_qscale"] = np.ascontiguousarray(np.asarray(inputs["L0_ffn_q_scale"], dtype=np.float32).reshape(2, 64, 1))
    common["f_kscale"] = np.ascontiguousarray(np.asarray(inputs["L0_ffn_k_scale"], dtype=np.float32).reshape(2, 64, 1))

    pp = np.arange(128)[:, None]
    cc_ = np.arange(128)[None, :]
    common["f_m0"] = np.where(cc_ >= pp, 0.0, MASKV).astype(np.float32)
    common["f_m4"] = np.where(cc_ <= pp, 0.0, MASKV).astype(np.float32)

    for nm in ("wq", "wk", "wv", "wo"):
        common["a_" + nm] = np.ascontiguousarray(np.asarray(inputs["L1_attn_" + nm], dtype=np.float32))
    for nm in ("bq", "bk", "bo", "b2"):
        common["a_" + nm] = np.asarray(inputs["L1_attn_" + nm], dtype=np.float32).reshape(D, 1)
    common["a_bv"] = np.asarray(inputs["L1_attn_bv"], dtype=np.float32).reshape(1, D)
    common["a_b1"] = np.asarray(inputs["L1_attn_b1"], dtype=np.float32).reshape(4096, 1)
    common["a_w1"] = np.ascontiguousarray(np.asarray(inputs["L1_attn_w1"], dtype=np.float32))
    common["a_w2"] = np.ascontiguousarray(np.asarray(inputs["L1_attn_w2"], dtype=np.float32))
    for nm in ("ln1_g", "ln1_b", "ln2_g", "ln2_b"):
        common["a_" + nm] = np.asarray(inputs["L1_attn_" + nm], dtype=np.float32).reshape(1, D)
    common["a_proj"] = np.ascontiguousarray(np.asarray(inputs["L1_attn_proj_mat"], dtype=np.float32))
    common["z_wT"] = np.ascontiguousarray(np.asarray(inputs["L1_ffn_weight"], dtype=np.float32).T)

    in_maps = []
    for c in range(8):
        b, hf = c // 2, c % 2
        t0 = hf * T
        shard = _fm(x[b, t0:t0 + T, :])
        halo = _fm(x[b, t0 - XH:t0, :]) if hf else np.zeros((D, XH), np.float32)
        cosr, sinr = _rope_tables(t0)
        m = dict(common)
        m["xin"] = np.ascontiguousarray(np.concatenate([halo, shard], axis=1))
        m["ropecos"] = cosr
        m["ropesin"] = sinr
        eb = np.zeros((128, 2), np.float32); eb[0:64, 0] = 1.0; eb[64:128, 1] = 1.0
        ebT = np.zeros((2, 128), np.float32); ebT[0, 0:64] = 1.0; ebT[1, 64:128] = 1.0
        m["c_eblk"] = eb; m["c_eblkT"] = ebT
        m["flg_take"] = np.full((128, 1), float(hf), np.float32)
        m["flg_send"] = np.full((128, 1), float(1 - hf), np.float32)
        m["flg_negbig"] = np.full((128, 1), float((1 - hf) * NEGBIG), np.float32)
        cq, sq, ck, sk = _xpos_tables(t0)
        m["f_cq"] = cq; m["f_sq"] = sq; m["f_ck"] = ck; m["f_sk"] = sk
        m["f_mask4"] = _mask4(hf)
        in_maps.append(m)
    return in_maps


class Scope:
    _n = [0]

    def __init__(self, nc):
        self.nc = nc
        self.guards = []

    def _u(self, nm):
        Scope._n[0] += 1
        return "%s_u%d" % (nm, Scope._n[0])

    def sb(self, nm, shp, dt=F32):
        g = self.nc.sbuf_tensor(self._u(nm), shp, dt)
        t = g.__enter__()
        self.guards.append(g)
        return t

    def ps(self, nm, shp, dt=F32):
        g = self.nc.psum_tensor(self._u(nm), shp, dt)
        t = g.__enter__()
        self.guards.append(g)
        return t

    def close(self):
        for g in reversed(self.guards):
            g.__exit__(None, None, None)
        self.guards = []


class Prog:
    def __init__(self):
        self.nc = bass.Bass("TRN2", target_bir_lowering=False, debug=False)
        self.sems = {}
        self.cnt = {}
        self.blk = None
        self._bar = 0

    def sem(self, name):
        h = self.nc.semaphore(name).__enter__()
        self.sems[name] = h
        self.cnt[name] = 0

    def inc(self, instr, name, k=1):
        instr.then_inc(self.sems[name], k)
        self.cnt[name] += k

    def w(self, eng, name, val=None):
        eng.wait_ge(self.sems[name], self.cnt[name] if val is None else val)

    def barrier(self):
        self._bar += 1
        tgt = 5 * self._bar
        bs = self.sems["bar"]

        def mk(has_drain):
            def f(e):
                if has_drain:
                    e.drain()
                e.sem_inc(bs, 1)
                e.wait_ge(bs, tgt)
            return f
        self.blk.sync(mk(False))
        self.blk.tensor(mk(True))
        self.blk.vector(mk(True))
        self.blk.scalar(mk(True))
        self.blk.gpsimd(mk(True))
        self.cnt["bar"] = tgt


class U:
    pass


def build_utils(p, E):
    nc, blk = p.nc, p.blk
    u = U()
    sc = Scope(nc)
    u.scope = sc
    u.ones_col = sc.sb("ones_col", [128, 1])
    u.ones_row = sc.sb("ones_row", [1, 128])
    u.onesN = sc.sb("onesN", [1, 512])
    u.neg20w = sc.sb("neg20w", [128, CH])
    u.ftk = sc.sb("ftk", [128, 1])
    u.fsd = sc.sb("fsd", [128, 1])
    u.fnb = sc.sb("fnb", [128, 1])
    u.ident = sc.sb("ident", [128, 128], BF16)
    u.eblk = sc.sb("eblk", [128, 2])
    u.eblkT = sc.sb("eblkT", [2, 128])
    u.iota_f = sc.sb("iota_f", [128, 128])
    u.iota_p = sc.sb("iota_p", [128, 1])
    u.PS0 = sc.ps("PS0", [128, 1024])
    u.PS1 = sc.ps("PS1", [128, 1024])

    def sec_sync(e):
        p.inc(e.dma_start(out=u.ftk[:, :], in_=E["flg_take"][:, :]), "ld", 16)
        p.inc(e.dma_start(out=u.fsd[:, :], in_=E["flg_send"][:, :]), "ld", 16)
        p.inc(e.dma_start(out=u.fnb[:, :], in_=E["flg_negbig"][:, :]), "ld", 16)
        p.inc(e.dma_start(out=u.eblk[:, :], in_=E["c_eblk"][:, :]), "ld", 16)
        p.inc(e.dma_start(out=u.eblkT[:, :], in_=E["c_eblkT"][:, :]), "ld", 16)
    blk.sync(sec_sync)

    def sec_gp(e):
        e.iota(u.iota_f[:, :], pattern=[[1, 128]], base=0, channel_multiplier=0,
               allow_small_or_imprecise_dtypes=True)
        e.iota(u.iota_p[:, :], pattern=[[0, 1]], base=0, channel_multiplier=1,
               allow_small_or_imprecise_dtypes=True)
        e.drain()
        e.sem_inc(p.sems["gp"], 1)
    blk.gpsimd(sec_gp)
    p.cnt["gp"] += 1

    def sec_v(e):
        e.memset(u.ones_col[:, :], 1.0)
        e.memset(u.ones_row[:, :], 1.0)
        e.memset(u.onesN[:, :], 1.0)
        e.memset(u.neg20w[:, :], -20.0)
        p.w(e, "gp")
        e.drain()
        e.tensor_scalar(u.ident[:, :], u.iota_f[:, :], u.iota_p[:, 0:1], 1.0, OP.is_equal, OP.mult)
        e.drain()
    blk.vector(sec_v)

    p.barrier()
    return u


def build_l0a(p, E, u):
    nc, blk = p.nc, p.blk
    xin = E["xin"]
    ones_col, ones_row = u.ones_col, u.ones_row
    neg20w, ftk, fsd, fnb = u.neg20w, u.ftk, u.fsd, u.fnb
    PS0, PS1 = u.PS0, u.PS1

    per = Scope(nc)
    xc_sb = per.sb("xc_sb", [128, G * T], BF16)
    fin_sb = per.sb("fin_sb", [128, G * T], BF16)
    convb_sb = per.sb("convb_sb", [128, G])
    nrmsc_sb = per.sb("nrmsc_sb", [128, G])
    dtb_sb = per.sb("dtb_sb", [128, G])
    convw_sb = per.sb("convw_sb", [128, 4 * G])

    def sec_sync(e):
        for g in range(G):
            p.inc(e.dma_start(out=convb_sb[:, g:g + 1], in_=E["convb"][g * 128:(g + 1) * 128, :]), "ld", 16)
            p.inc(e.dma_start(out=nrmsc_sb[:, g:g + 1], in_=E["nrmsc"][g * 128:(g + 1) * 128, :]), "ld", 16)
            p.inc(e.dma_start(out=dtb_sb[:, g:g + 1], in_=E["dtbias"][g * 128:(g + 1) * 128, :]), "ld", 16)
            p.inc(e.dma_start(out=convw_sb[:, 4 * g:4 * g + 4], in_=E["convw"][g * 128:(g + 1) * 128, :]), "ld", 16)
    blk.sync(sec_sync)

    p.barrier()

    Wd = TT + XH
    b1 = Scope(nc)
    BA = b1.sb("BA", [128, G * Wd])
    BB = b1.sb("BB", [128, G * Wd])
    BC = b1.sb("BC", [128, G * Wd])
    rstat = b1.sb("rstat", [1, Wd])
    rstat2 = b1.sb("rstat2", [1, Wd])
    PS2 = b1.ps("PS2b1", [128, 1024])
    PS3 = b1.ps("PS3b1", [128, 1024])

    def bsl(buf, g, wid=Wd):
        return buf[:, g * Wd:g * Wd + wid]

    for tt in range(NT):
        c0 = tt * TT

        def s_load(e, c0=c0):
            for g in range(G):
                p.inc(e.dma_start(out=bsl(BA, g), in_=xin[g * 128:(g + 1) * 128, c0:c0 + Wd]), "ld", 16)
        blk.sync(s_load)

        def s_sq(e):
            p.w(e, "ld")
            e.square(BB[:, :], BA[:, :])
            e.drain()
            e.sem_inc(p.sems["ex"], 1)
        blk.scalar(s_sq)
        p.cnt["ex"] += 1

        def s_mm1(e):
            p.w(e, "ex")
            for (a, b2) in ((0, 512), (512, Wd)):
                for g in range(G):
                    ins = nc.tensor.matmul(PS0[0:1, a:b2], ones_col[:, :], BB[:, g * Wd + a:g * Wd + b2],
                                           start=(g == 0), stop=(g == G - 1))
                    if g == G - 1:
                        p.inc(ins, "mm", 1)
            for (a, b2) in ((0, 512), (512, Wd)):
                for g in range(G):
                    ins = nc.tensor.matmul(PS1[0:1, a:b2], ones_col[:, :], BA[:, g * Wd + a:g * Wd + b2],
                                           start=(g == 0), stop=(g == G - 1))
                    if g == G - 1:
                        p.inc(ins, "mm", 1)
        blk.tensor(s_mm1)

        def s_stat(e):
            p.w(e, "mm")
            e.tensor_scalar(rstat2[0:1, 0:Wd], PS1[0:1, 0:Wd], 1.0 / 1024.0, None, OP.mult)
            e.drain()
            e.tensor_mul(rstat[0:1, 0:Wd], rstat2[0:1, 0:Wd], rstat2[0:1, 0:Wd])
            e.drain()
            e.scalar_tensor_tensor(out=rstat[0:1, 0:Wd], in0=PS0[0:1, 0:Wd], scalar=1.0 / 1024.0,
                                   in1=rstat[0:1, 0:Wd], op0=OP.mult, op1=OP.subtract)
            e.drain()
            e.tensor_scalar(rstat[0:1, 0:Wd], rstat[0:1, 0:Wd], 1e-5, None, OP.add)
            e.drain()
            e.reciprocal(rstat[0:1, 0:Wd], rstat[0:1, 0:Wd])
            e.drain()
            e.sem_inc(p.sems["ev"], 1)
        blk.vector(s_stat)
        p.cnt["ev"] += 1

        def s_sqrt(e):
            p.w(e, "ev")
            e.sqrt(rstat[0:1, 0:Wd], rstat[0:1, 0:Wd])
            e.drain()
            e.sem_inc(p.sems["ex"], 1)
        blk.scalar(s_sqrt)
        p.cnt["ex"] += 1

        def s_bneg(e):
            p.w(e, "ex")
            e.scalar_tensor_tensor(out=rstat2[0:1, 0:Wd], in0=rstat2[0:1, 0:Wd], scalar=-1.0,
                                   in1=rstat[0:1, 0:Wd], op0=OP.mult, op1=OP.mult)
            e.drain()
            e.sem_inc(p.sems["ev"], 1)
        blk.vector(s_bneg)
        p.cnt["ev"] += 1

        def s_mm2(e):
            p.w(e, "ev")
            for (a, b2) in ((0, 512), (512, Wd)):
                p.inc(nc.tensor.matmul(PS2[:, a:b2], ones_row[:, :], rstat[0:1, a:b2],
                                       start=True, stop=True), "mm", 1)
                p.inc(nc.tensor.matmul(PS3[:, a:b2], ones_row[:, :], rstat2[0:1, a:b2],
                                       start=True, stop=True), "mm", 1)
        blk.tensor(s_mm2)

        def s_xln(e, c0=c0):
            p.w(e, "mm")
            for g in range(G):
                e.scalar_tensor_tensor(out=bsl(BB, g), in0=bsl(BA, g), scalar=1.0,
                                       in1=PS2[:, 0:Wd], op0=OP.mult, op1=OP.mult)
            e.drain()
            for g in range(G):
                e.tensor_add(bsl(BB, g), bsl(BB, g), PS3[:, 0:Wd])
            e.drain()
            e.sem_inc(p.sems["ev"], 1)
        blk.vector(s_xln)
        p.cnt["ev"] += 1

        def s_stln(e, c0=c0):
            p.w(e, "ev")
            for g in range(G):
                p.inc(e.dma_start(out=E["xln_d"][g * 128:(g + 1) * 128, c0:c0 + TT],
                                  in_=BB[:, g * Wd + XH:g * Wd + XH + TT]), "st", 16)
        blk.sync(s_stln)

        def s_sq2(e):
            p.w(e, "ev")
            e.square(BA[:, :], BB[:, :])
            e.drain()
            e.sem_inc(p.sems["ex"], 1)
        blk.scalar(s_sq2)
        p.cnt["ex"] += 1

        def s_mm3(e):
            p.w(e, "ex")
            for (a, b2) in ((0, 512), (512, Wd)):
                for g in range(G):
                    ins = nc.tensor.matmul(PS0[0:1, a:b2], ones_col[:, :], BA[:, g * Wd + a:g * Wd + b2],
                                           start=(g == 0), stop=(g == G - 1))
                    if g == G - 1:
                        p.inc(ins, "mm", 1)
        blk.tensor(s_mm3)

        def s_stat2(e):
            p.w(e, "mm")
            e.tensor_scalar(rstat[0:1, 0:Wd], PS0[0:1, 0:Wd], 1.0 / 1024.0, 1e-6, OP.mult, OP.add)
            e.drain()
            e.reciprocal(rstat[0:1, 0:Wd], rstat[0:1, 0:Wd])
            e.drain()
            e.sem_inc(p.sems["ev"], 1)
        blk.vector(s_stat2)
        p.cnt["ev"] += 1

        def s_sqrt2(e):
            p.w(e, "ev")
            e.sqrt(rstat[0:1, 0:Wd], rstat[0:1, 0:Wd])
            e.drain()
            e.sem_inc(p.sems["ex"], 1)
        blk.scalar(s_sqrt2)
        p.cnt["ex"] += 1

        def s_mm4(e):
            p.w(e, "ex")
            for (a, b2) in ((0, 512), (512, Wd)):
                p.inc(nc.tensor.matmul(PS2[:, a:b2], ones_row[:, :], rstat[0:1, a:b2],
                                       start=True, stop=True), "mm", 1)
        blk.tensor(s_mm4)

        def s_xn(e):
            p.w(e, "mm")
            p.w(e, "st")
            for g in range(G):
                e.scalar_tensor_tensor(out=bsl(BC, g), in0=bsl(BB, g),
                                       scalar=nrmsc_sb[:, g:g + 1], in1=PS2[:, 0:Wd],
                                       op0=OP.mult, op1=OP.mult)
            e.drain()
            for g in range(G):
                e.tensor_scalar(bsl(BC, g), bsl(BC, g), 60000.0, -60000.0, OP.min, OP.max)
            e.drain()
            for g in range(G):
                e.tensor_scalar(bsl(BA, g, TT), BC[:, g * Wd + XH:g * Wd + XH + TT],
                                convw_sb[:, 4 * g + 3:4 * g + 4], None, OP.mult)
            e.drain()
            cur = BA
            for k in (2, 1, 0):
                nxt = BB if cur is BA else BA
                off = XH - 3 + k
                for g in range(G):
                    e.scalar_tensor_tensor(out=bsl(nxt, g, TT),
                                           in0=BC[:, g * Wd + off:g * Wd + off + TT],
                                           scalar=convw_sb[:, 4 * g + k:4 * g + k + 1],
                                           in1=bsl(cur, g, TT), op0=OP.mult, op1=OP.add)
                e.drain()
                cur = nxt
            e.sem_inc(p.sems["ev"], 1)
        blk.vector(s_xn)
        p.cnt["ev"] += 1

        def s_silu(e, tt=tt):
            p.w(e, "ev")
            for g in range(G):
                e.activation(out=xc_sb[:, g * T + tt * TT:g * T + (tt + 1) * TT],
                             in_=bsl(BB, g, TT), func=AF.Silu,
                             bias=convb_sb[:, g:g + 1], scale=1.0)
            e.drain()
            e.sem_inc(p.sems["ex"], 1)
        blk.scalar(s_silu)
        p.cnt["ex"] += 1

        p.barrier()

    b1.close()

    b2 = Scope(nc)
    cos_sb = b2.sb("cos_sb", [128, T])
    sin_sb = b2.sb("sin_sb", [128, T])
    W3 = b2.sb("W3", [128, G * 384], BF16)
    SA = b2.sb("SA", [128, SEG]); SB = b2.sb("SB", [128, SEG]); SC = b2.sb("SC", [128, SEG])
    SD = b2.sb("SD", [128, SEG]); SE = b2.sb("SE", [128, SEG])
    bdec = b2.sb("bdec", [128, G * NCH]); bout = b2.sb("bout", [128, G * NCH])
    st_a = b2.sb("st_a", [128, NCH]); st_b = b2.sb("st_b", [128, NCH])
    st_c = b2.sb("st_c", [128, NCH]); st_d = b2.sb("st_d", [128, NCH])
    st_e = b2.sb("st_e", [128, NCH])
    sv1 = b2.sb("sv1", [128, 1]); sv2 = b2.sb("sv2", [128, 1]); sv3 = b2.sb("sv3", [128, 1])
    snd_sb = b2.sb("snd_sb", [128, 24]); rcv_sb = b2.sb("rcv_sb", [128, 24])
    cc_sb = b2.sb("cc_sb", [128, G * NCH])
    PS2 = b2.ps("PS2b2", [128, 1024])

    def s_tbl(e):
        p.inc(e.dma_start(out=cos_sb[:, :], in_=E["ropecos"][:, :]), "ld", 16)
        p.inc(e.dma_start(out=sin_sb[:, :], in_=E["ropesin"][:, :]), "ld", 16)
    blk.sync(s_tbl)

    for g in range(G):
        def s_w3(e, g=g):
            for kk in range(G):
                for j, w in enumerate((E["w_dt"], E["w_v"], E["w_vs"])):
                    p.inc(e.dma_start(out=W3[:, kk * 384 + j * 128:kk * 384 + (j + 1) * 128],
                                      in_=w[kk * 128:(kk + 1) * 128, g * 128:(g + 1) * 128]), "ld", 16)
        blk.gpsimd(s_w3)

        for sg in range(NSEG):
            seg0 = sg * SEG

            def s_mm(e, g=g, seg0=seg0):
                p.w(e, "ld")
                PSL = [(PS0, 0), (PS0, 512), (PS1, 0), (PS1, 512), (PS2, 0), (PS2, 512)]
                for j in range(3):
                    for t2 in range(2):
                        pb, po = PSL[j * 2 + t2]
                        for kk in range(G):
                            ins = nc.tensor.matmul(
                                pb[:, po:po + 512],
                                W3[:, kk * 384 + j * 128:kk * 384 + (j + 1) * 128],
                                xc_sb[:, kk * T + seg0 + t2 * 512:kk * T + seg0 + (t2 + 1) * 512],
                                start=(kk == 0), stop=(kk == G - 1))
                            if kk == G - 1:
                                p.inc(ins, "mm", 1)
            blk.tensor(s_mm)
            mmn = p.cnt["mm"]

            def s_ev1(e, g=g, mmn=mmn):
                e.wait_ge(p.sems["mm"], mmn - 4)
                e.tensor_scalar(SA[:, 0:512], PS0[:, 0:512], dtb_sb[:, g:g + 1], 30.0, OP.add, OP.min)
                e.tensor_scalar(SA[:, 512:1024], PS0[:, 512:1024], dtb_sb[:, g:g + 1], 30.0, OP.add, OP.min)
                e.drain()
                e.sem_inc(p.sems["ev"], 1)
                e.wait_ge(p.sems["mm"], mmn - 2)
                e.tensor_copy(SB[:, 0:512], PS1[:, 0:512])
                e.tensor_copy(SB[:, 512:1024], PS1[:, 512:1024])
                e.drain()
                e.wait_ge(p.sems["mm"], mmn)
                e.tensor_copy(SC[:, 0:512], PS2[:, 0:512])
                e.tensor_copy(SC[:, 512:1024], PS2[:, 512:1024])
                e.drain()
                e.sem_inc(p.sems["ev"], 1)
            blk.vector(s_ev1)
            p.cnt["ev"] += 2
            evb = p.cnt["ev"]

            def s_sp(e, evb=evb):
                e.wait_ge(p.sems["ev"], evb - 1)
                e.activation(out=SD[:, :], in_=SA[:, :], func=AF.Exp)
                e.drain()
                e.activation(out=SA[:, :], in_=SD[:, :], func=AF.Ln, bias=1.0)
                e.drain()
                e.sem_inc(p.sems["ex"], 1)
            blk.scalar(s_sp)
            p.cnt["ex"] += 1

            def s_scan(e, g=g, sg=sg, seg0=seg0):
                p.w(e, "ex")
                e.tensor_scalar(SA[:, :], SA[:, :], 2.0, 0.001, OP.min, OP.max)
                e.drain()
                e.tensor_mul(SD[:, :], SB[:, :], cos_sb[:, seg0:seg0 + SEG])
                e.drain()
                e.tensor_mul(SE[:, :], SC[:, :], sin_sb[:, seg0:seg0 + SEG])
                e.drain()
                e.tensor_add(SC[:, :], SD[:, :], SE[:, :])
                e.drain()
                e.tensor_mul(SE[:, :], SC[:, :], SA[:, :])
                e.drain()
                e.tensor_scalar(SD[:, :], SA[:, :], -1.0, None, OP.mult)
                e.drain()
                for cc in range(SEG // CH):
                    e.tensor_tensor_scan(out=SC[:, cc * CH:(cc + 1) * CH],
                                         data0=SD[:, cc * CH:(cc + 1) * CH],
                                         data1=neg20w[:, :], initial=0.0,
                                         op0=OP.add, op1=OP.max)
                e.drain()
                e.tensor_copy(bdec[:, g * NCH + sg * 4:g * NCH + sg * 4 + 4], SC[:, CH - 1::CH])
                for cc in range(SEG // CH):
                    a = cc * CH
                    e.tensor_copy(SD[:, a:a + 1], SC[:, a:a + 1])
                    e.tensor_sub(SD[:, a + 1:a + CH], SC[:, a + 1:a + CH], SC[:, a:a + CH - 1])
                e.drain()
                e.sem_inc(p.sems["ev"], 1)
            blk.vector(s_scan)
            p.cnt["ev"] += 1

            def s_expg(e):
                p.w(e, "ev")
                e.activation(out=SA[:, :], in_=SD[:, :], func=AF.Exp)
                e.drain()
                e.sem_inc(p.sems["ex"], 1)
            blk.scalar(s_expg)
            p.cnt["ex"] += 1

            def s_h0(e, g=g, sg=sg):
                p.w(e, "ex")
                for cc in range(SEG // CH):
                    a = cc * CH
                    e.tensor_tensor_scan(out=SB[:, a:a + CH], data0=SA[:, a:a + CH],
                                         data1=SE[:, a:a + CH], initial=0.0,
                                         op0=OP.mult, op1=OP.add)
                e.drain()
                e.tensor_copy(bout[:, g * NCH + sg * 4:g * NCH + sg * 4 + 4], SB[:, CH - 1::CH])
                e.drain()
                e.sem_inc(p.sems["ev"], 1)
            blk.vector(s_h0)
            p.cnt["ev"] += 1

            def s_store(e, g=g, seg0=seg0):
                p.w(e, "ev")
                p.inc(e.dma_start(out=E["g_d"][g * 128:(g + 1) * 128, seg0:seg0 + SEG], in_=SA[:, :]), "st", 16)
                p.inc(e.dma_start(out=E["vin_d"][g * 128:(g + 1) * 128, seg0:seg0 + SEG], in_=SE[:, :]), "st", 16)
            blk.sync(s_store)

            p.barrier()

    if hasattr(p, 'marks'): p.marks.append(('l0a.b2a', nc.n_instructions()))
    for g in range(G):
        def v_stats(e, g=g):
            sl = slice(g * NCH, (g + 1) * NCH)
            e.tensor_tensor_scan(out=st_a[:, :], data0=bdec[:, sl], data1=neg20w[:, 0:NCH],
                                 initial=0.0, op0=OP.add, op1=OP.bypass)
            e.drain()
            e.tensor_scalar(st_b[:, :], st_a[:, :], -80.0, 0.0, OP.max, OP.min)
            e.drain()
            e.reduce_max(sv1[:, :], st_b[:, :], axis=AX.X)
            e.drain()
            e.tensor_scalar(st_c[:, :], st_b[:, :], sv1[:, 0:1], -20.0, OP.subtract, OP.max)
            e.drain()
            e.tensor_scalar(st_c[:, :], st_c[:, :], 0.0, None, OP.min)
            e.drain()
            e.sem_inc(p.sems["ev"], 1)
        blk.vector(v_stats)
        p.cnt["ev"] += 1

        def s_eseed(e):
            p.w(e, "ev")
            e.activation(out=st_d[:, :], in_=st_c[:, :], func=AF.Exp, scale=-1.0)
            e.drain()
            e.sem_inc(p.sems["ex"], 1)
        blk.scalar(s_eseed)
        p.cnt["ex"] += 1

        def v_send(e, g=g):
            p.w(e, "ex")
            sl = slice(g * NCH, (g + 1) * NCH)
            e.tensor_mul(st_d[:, :], bout[:, sl], st_d[:, :])
            e.drain()
            e.reduce_sum(sv2[:, :], st_d[:, :], axis=AX.X)
            e.drain()
            e.tensor_scalar(snd_sb[:, 3 * g:3 * g + 1], st_a[:, NCH - 1:NCH], fsd[:, 0:1], None, OP.mult)
            e.tensor_scalar(snd_sb[:, 3 * g + 1:3 * g + 2], sv1[:, 0:1], fsd[:, 0:1], None, OP.mult)
            e.tensor_scalar(snd_sb[:, 3 * g + 2:3 * g + 3], sv2[:, 0:1], fsd[:, 0:1], None, OP.mult)
            e.drain()
            e.sem_inc(p.sems["ev"], 1)
        blk.vector(v_send)
        p.cnt["ev"] += 1

    def g_coll(e):
        p.w(e, "ev")
        p.inc(e.dma_start(out=E["snd_d"][:, :], in_=snd_sb[:, :]), "st", 16)
        p.w(e, "st")
        p.inc(e.collective_compute(
            "AllReduce", OP.add,
            replica_groups=[[0, 1], [2, 3], [4, 5], [6, 7]],
            ins=[E["snd_d"].opt()],
            outs=[E["rcv_d"].opt()],
        ), "cc", 1)
        p.w(e, "cc")
        p.inc(e.dma_start(out=rcv_sb[:, :], in_=E["rcv_d"][:, :]), "ld", 16)
    blk.gpsimd(g_coll)

    p.barrier()

    for g in range(G):
        def v_cc1(e, g=g):
            sl = slice(g * NCH, (g + 1) * NCH)
            p.w(e, "ld")
            e.tensor_scalar(sv1[:, :], rcv_sb[:, 3 * g:3 * g + 1], ftk[:, 0:1], None, OP.mult)
            e.tensor_scalar(sv2[:, :], rcv_sb[:, 3 * g + 1:3 * g + 2], ftk[:, 0:1], fnb[:, 0:1], OP.mult, OP.add)
            e.drain()
            e.tensor_tensor_scan(out=st_a[:, :], data0=bdec[:, sl], data1=neg20w[:, 0:NCH],
                                 initial=0.0, op0=OP.add, op1=OP.bypass)
            e.drain()
            e.tensor_scalar(st_b[:, :], st_a[:, :], sv1[:, 0:1], -80.0, OP.add, OP.max)
            e.drain()
            e.tensor_scalar(st_b[:, :], st_b[:, :], 0.0, None, OP.min)
            e.drain()
            e.reduce_max(sv3[:, :], st_b[:, :], axis=AX.X)
            e.drain()
            e.tensor_max(sv3[:, :], sv3[:, :], sv2[:, :])
            e.drain()
            e.tensor_scalar(st_c[:, :], st_b[:, :], sv3[:, 0:1], -20.0, OP.subtract, OP.max)
            e.drain()
            e.tensor_scalar(st_c[:, :], st_c[:, :], 0.0, None, OP.min)
            e.drain()
            e.sem_inc(p.sems["ev"], 1)
        blk.vector(v_cc1)
        p.cnt["ev"] += 1

        def s_cc1(e):
            p.w(e, "ev")
            e.activation(out=st_d[:, :], in_=st_c[:, :], func=AF.Exp, scale=-1.0)
            e.drain()
            e.activation(out=st_e[:, :], in_=st_c[:, :], func=AF.Exp)
            e.drain()
            e.sem_inc(p.sems["ex"], 1)
        blk.scalar(s_cc1)
        p.cnt["ex"] += 1

        def v_cc2(e, g=g):
            p.w(e, "ex")
            sl = slice(g * NCH, (g + 1) * NCH)
            e.tensor_mul(st_d[:, :], bout[:, sl], st_d[:, :])
            e.drain()
            e.tensor_scalar(st_a[:, 0:1], rcv_sb[:, 3 * g + 2:3 * g + 3], ftk[:, 0:1], None, OP.mult)
            e.tensor_copy(st_a[:, 1:NCH], st_d[:, 0:NCH - 1])
            e.drain()
            e.tensor_tensor_scan(out=st_b[:, :], data0=st_a[:, :], data1=neg20w[:, 0:NCH],
                                 initial=0.0, op0=OP.add, op1=OP.bypass)
            e.drain()
            e.tensor_mul(cc_sb[:, sl], st_b[:, :], st_e[:, :])
            e.drain()
            e.sem_inc(p.sems["ev"], 1)
        blk.vector(v_cc2)
        p.cnt["ev"] += 1

    p.barrier()

    for g in range(G):
        for sg in range(NSEG):
            seg0 = sg * SEG

            def s_ld2(e, g=g, seg0=seg0):
                p.inc(e.dma_start(out=SA[:, :], in_=E["g_d"][g * 128:(g + 1) * 128, seg0:seg0 + SEG]), "ld", 16)
                p.inc(e.dma_start(out=SE[:, :], in_=E["vin_d"][g * 128:(g + 1) * 128, seg0:seg0 + SEG]), "ld", 16)
            blk.sync(s_ld2)

            def v_hs(e, g=g, sg=sg, seg0=seg0):
                p.w(e, "ld")
                for cc in range(SEG // CH):
                    a = cc * CH
                    ci = g * NCH + sg * 4 + cc
                    e.tensor_tensor_scan(out=SB[:, a:a + CH], data0=SA[:, a:a + CH],
                                         data1=SE[:, a:a + CH], initial=cc_sb[:, ci:ci + 1],
                                         op0=OP.mult, op1=OP.add)
                e.drain()
                e.tensor_copy(fin_sb[:, g * T + seg0:g * T + seg0 + SEG], SB[:, :])
                e.drain()
                e.sem_inc(p.sems["ev"], 1)
            blk.vector(v_hs)
            p.cnt["ev"] += 1

            p.barrier()

    tap_aps = E.get("_taps", {})
    if "cc" in tap_aps:
        def g_tapcc(e):
            p.inc(e.dma_start(out=tap_aps["cc"][0:128, 0:G * NCH], in_=cc_sb[:, :]), "st", 16)
            p.inc(e.dma_start(out=tap_aps["cc"][128:256, 0:G * NCH], in_=bdec[:, :]), "st", 16)
            p.inc(e.dma_start(out=tap_aps["cc"][256:384, 0:G * NCH], in_=bout[:, :]), "st", 16)
            p.inc(e.dma_start(out=tap_aps["cc"][384:512, 0:24], in_=rcv_sb[:, :]), "st", 16)
            p.w(e, "st")
        blk.gpsimd(g_tapcc)
        p.barrier()

    b2.close()

    if hasattr(p, 'marks'): p.marks.append(('l0a.b2b', nc.n_instructions()))
    T3 = 256
    NT3 = T // T3
    b3 = Scope(nc)
    HMW = b3.sb("HMW", [128, G * D], BF16)
    WGT = b3.sb("WGT", [128, G * D], BF16)
    WOP = b3.sb("WOP", [128, G * D], BF16)
    gt_sb = b3.sb("gt_sb", [128, T3], BF16)
    mg_sb = b3.sb("mg_sb", [128, G * T3], BF16)
    xr_sb = b3.sb("xr_sb", [128, G * T3])
    xl_sb = b3.sb("xl_sb", [128, G * T3])
    PS2 = b3.ps("PS2b3", [128, 1024])

    def g_wb3(e):
        for g in range(G):
            p.inc(e.dma_start(out=HMW[:, g * D:(g + 1) * D], in_=E["hm_dense"][g * 128:(g + 1) * 128, :]), "ld", 16)
            p.inc(e.dma_start(out=WGT[:, g * D:(g + 1) * D], in_=E["w_gate"][g * 128:(g + 1) * 128, :]), "ld", 16)
            p.inc(e.dma_start(out=WOP[:, g * D:(g + 1) * D], in_=E["outproj"][g * 128:(g + 1) * 128, :]), "ld", 16)
    blk.gpsimd(g_wb3)
    p.barrier()

    for tt in range(NT3):
        c0 = tt * T3

        def s_ldx(e, c0=c0):
            p.w(e, "st")
            for g in range(G):
                p.inc(e.dma_start(out=xr_sb[:, g * T3:(g + 1) * T3],
                                  in_=xin[g * 128:(g + 1) * 128, XH + c0:XH + c0 + T3]), "ld", 16)
                p.inc(e.dma_start(out=xl_sb[:, g * T3:(g + 1) * T3],
                                  in_=E["xln_d"][g * 128:(g + 1) * 128, c0:c0 + T3]), "ld", 16)
        blk.sync(s_ldx)

        def v_pre(e):
            p.w(e, "ld")
            for g in range(G):
                e.tensor_add(xr_sb[:, g * T3:(g + 1) * T3], xr_sb[:, g * T3:(g + 1) * T3],
                             xl_sb[:, g * T3:(g + 1) * T3])
            e.drain()
            e.sem_inc(p.sems["ev"], 1)
        blk.vector(v_pre)
        p.cnt["ev"] += 1

        for oc in range(G):
            def s_mm3(e, oc=oc, c0=c0):
                p.w(e, "ld")
                for kk in range(G):
                    ins = nc.tensor.matmul(PS0[:, 0:T3], HMW[:, kk * D + oc * 128:kk * D + (oc + 1) * 128],
                                           fin_sb[:, kk * T + c0:kk * T + c0 + T3],
                                           start=(kk == 0), stop=(kk == G - 1))
                    if kk == G - 1:
                        p.inc(ins, "mm", 1)
                for kk in range(G):
                    ins = nc.tensor.matmul(PS1[:, 0:T3], WGT[:, kk * D + oc * 128:kk * D + (oc + 1) * 128],
                                           xc_sb[:, kk * T + c0:kk * T + c0 + T3],
                                           start=(kk == 0), stop=(kk == G - 1))
                    if kk == G - 1:
                        p.inc(ins, "mm", 1)
            blk.tensor(s_mm3)

            def s_sig(e):
                p.w(e, "mm")
                e.activation(out=gt_sb[:, :], in_=PS1[:, 0:T3], func=AF.Sigmoid)
                e.drain()
                e.sem_inc(p.sems["ex"], 1)
            blk.scalar(s_sig)
            p.cnt["ex"] += 1

            def s_mg(e, oc=oc):
                p.w(e, "ex")
                e.scalar_tensor_tensor(out=mg_sb[:, oc * T3:(oc + 1) * T3], in0=PS0[:, 0:T3],
                                       scalar=1.0, in1=gt_sb[:, :], op0=OP.mult, op1=OP.mult)
                e.drain()
                e.sem_inc(p.sems["ev"], 1)
            blk.vector(s_mg)
            p.cnt["ev"] += 1

            p.barrier()

        for oc in range(G):
            def s_mm4(e, oc=oc):
                for kk in range(G):
                    ins = nc.tensor.matmul(PS2[:, 0:T3], WOP[:, kk * D + oc * 128:kk * D + (oc + 1) * 128],
                                           mg_sb[:, kk * T3:(kk + 1) * T3],
                                           start=(kk == 0), stop=(kk == G - 1))
                    if kk == G - 1:
                        p.inc(ins, "mm", 1)
            blk.tensor(s_mm4)

            def s_res(e, oc=oc):
                p.w(e, "mm")
                e.scalar_tensor_tensor(out=xr_sb[:, oc * T3:(oc + 1) * T3], in0=PS2[:, 0:T3],
                                       scalar=1.0, in1=xr_sb[:, oc * T3:(oc + 1) * T3],
                                       op0=OP.mult, op1=OP.add)
                e.drain()
                e.sem_inc(p.sems["ev"], 1)
            blk.vector(s_res)
            p.cnt["ev"] += 1

            p.barrier()

        def s_stx(e, c0=c0):
            for g in range(G):
                p.inc(e.dma_start(out=E["x1_d"][g * 128:(g + 1) * 128, c0:c0 + T3],
                                  in_=xr_sb[:, g * T3:(g + 1) * T3]), "st", 16)
        blk.sync(s_stx)

        p.barrier()

    b3.close()

    tap_aps = E.get("_taps", {})
    if tap_aps:
        def g_taps(e):
            for nm, src in (("xc", xc_sb), ("fin", fin_sb)):
                if nm in tap_aps:
                    for g in range(G):
                        p.inc(e.dma_start(out=tap_aps[nm][g * 128:(g + 1) * 128, :],
                                          in_=src[:, g * T:(g + 1) * T]), "st", 16)
        blk.gpsimd(g_taps)
        p.barrier()
    per.close()


def emit_ln_job(p, u, LA, LB, r1, r2, src, scol, dst, dcol, w, PSa=None, PSb=None):
    nc, blk = p.nc, p.blk

    def s_load(e):
        for g in range(G):
            p.inc(e.dma_start(out=LA[:, g * 512:g * 512 + w], in_=src[g * 128:(g + 1) * 128, scol:scol + w]), "ld", 16)
    blk.sync(s_load)

    def s_sq(e):
        p.w(e, "ld")
        for g in range(G):
            e.square(LB[:, g * 512:g * 512 + w], LA[:, g * 512:g * 512 + w])
        e.drain()
        e.sem_inc(p.sems["ex"], 1)
    blk.scalar(s_sq)
    p.cnt["ex"] += 1

    def s_mm(e):
        p.w(e, "ex")
        for g in range(G):
            ins = nc.tensor.matmul(u.PS0[0:1, 0:w], u.ones_col[:, :], LB[:, g * 512:g * 512 + w],
                                   start=(g == 0), stop=(g == G - 1))
            if g == G - 1:
                p.inc(ins, "mm", 1)
        for g in range(G):
            ins = nc.tensor.matmul(u.PS1[0:1, 0:w], u.ones_col[:, :], LA[:, g * 512:g * 512 + w],
                                   start=(g == 0), stop=(g == G - 1))
            if g == G - 1:
                p.inc(ins, "mm", 1)
    blk.tensor(s_mm)

    def s_stat(e):
        p.w(e, "mm")
        e.tensor_scalar(r2[0:1, 0:w], u.PS1[0:1, 0:w], 1.0 / 1024.0, None, OP.mult)
        e.drain()
        e.tensor_mul(r1[0:1, 0:w], r2[0:1, 0:w], r2[0:1, 0:w])
        e.drain()
        e.scalar_tensor_tensor(out=r1[0:1, 0:w], in0=u.PS0[0:1, 0:w], scalar=1.0 / 1024.0,
                               in1=r1[0:1, 0:w], op0=OP.mult, op1=OP.subtract)
        e.drain()
        e.tensor_scalar(r1[0:1, 0:w], r1[0:1, 0:w], 1e-5, None, OP.add)
        e.drain()
        e.reciprocal(r1[0:1, 0:w], r1[0:1, 0:w])
        e.drain()
        e.sem_inc(p.sems["ev"], 1)
    blk.vector(s_stat)
    p.cnt["ev"] += 1

    def s_sqrt(e):
        p.w(e, "ev")
        e.sqrt(r1[0:1, 0:w], r1[0:1, 0:w])
        e.drain()
        e.sem_inc(p.sems["ex"], 1)
    blk.scalar(s_sqrt)
    p.cnt["ex"] += 1

    def s_b(e):
        p.w(e, "ex")
        e.scalar_tensor_tensor(out=r2[0:1, 0:w], in0=r2[0:1, 0:w], scalar=-1.0,
                               in1=r1[0:1, 0:w], op0=OP.mult, op1=OP.mult)
        e.drain()
        e.sem_inc(p.sems["ev"], 1)
    blk.vector(s_b)
    p.cnt["ev"] += 1

    def s_bc(e):
        p.w(e, "ev")
        p.inc(nc.tensor.matmul(PSa[:, 0:w], u.ones_row[:, :], r1[0:1, 0:w], start=True, stop=True), "mm", 1)
        p.inc(nc.tensor.matmul(PSb[:, 0:w], u.ones_row[:, :], r2[0:1, 0:w], start=True, stop=True), "mm", 1)
    blk.tensor(s_bc)

    def s_app(e):
        p.w(e, "mm")
        for g in range(G):
            e.scalar_tensor_tensor(out=LB[:, g * 512:g * 512 + w], in0=LA[:, g * 512:g * 512 + w],
                                   scalar=1.0, in1=PSa[:, 0:w], op0=OP.mult, op1=OP.mult)
        e.drain()
        for g in range(G):
            e.tensor_add(LB[:, g * 512:g * 512 + w], LB[:, g * 512:g * 512 + w], PSb[:, 0:w])
        e.drain()
        e.sem_inc(p.sems["ev"], 1)
    blk.vector(s_app)
    p.cnt["ev"] += 1

    def s_store(e):
        p.w(e, "ev")
        for g in range(G):
            p.inc(e.dma_start(out=dst[g * 128:(g + 1) * 128, dcol:dcol + w], in_=LB[:, g * 512:g * 512 + w]), "st", 16)
    blk.sync(s_store)
    p.barrier()


def emit_exchange(p, u, sc, src, scol, w, snd_d, rcv_d, dst, dcol, cols=None):
    nc, blk = p.nc, p.blk
    HB = sc.sb("HBx%d" % p.cnt["cc"], [128, G * w])

    def s_l(e):
        p.w(e, "st")
        for g in range(G):
            p.inc(e.dma_start(out=HB[:, g * w:(g + 1) * w], in_=src[g * 128:(g + 1) * 128, scol:scol + w]), "ld", 16)
    blk.sync(s_l)

    def s_m(e):
        p.w(e, "ld")
        e.tensor_scalar(HB[:, :], HB[:, :], u.fsd[:, 0:1], None, OP.mult)
        e.drain()
        e.sem_inc(p.sems["ev"], 1)
    blk.vector(s_m)
    p.cnt["ev"] += 1

    def s_c(e):
        p.w(e, "ev")
        for g in range(G):
            p.inc(e.dma_start(out=snd_d[g * 128:(g + 1) * 128, 0:w], in_=HB[:, g * w:(g + 1) * w]), "st", 16)
        p.w(e, "st")
        p.inc(e.collective_compute(
            "AllReduce", OP.add,
            replica_groups=[[0, 1], [2, 3], [4, 5], [6, 7]],
            ins=[snd_d.opt()], outs=[rcv_d.opt()],
        ), "cc", 1)
        p.w(e, "cc")
        for g in range(G):
            p.inc(e.dma_start(out=HB[:, g * w:(g + 1) * w], in_=rcv_d[g * 128:(g + 1) * 128, 0:w]), "ld", 16)
    blk.gpsimd(s_c)

    a, b2 = (0, w) if cols is None else cols

    def s_m2(e):
        p.w(e, "ld")
        e.tensor_scalar(HB[:, :], HB[:, :], u.ftk[:, 0:1], None, OP.mult)
        e.drain()
        e.sem_inc(p.sems["ev"], 1)
    blk.vector(s_m2)
    p.cnt["ev"] += 1

    def s_s(e):
        p.w(e, "ev")
        for g in range(G):
            p.inc(e.dma_start(out=dst[g * 128:(g + 1) * 128, dcol:dcol + (b2 - a)],
                              in_=HB[:, g * w + a:g * w + b2],
                              allow_slow_non_contiguous=True), "st", 16)
    blk.sync(s_s)
    p.barrier()


def build_l0f_layer(p, E, u, l, src, xpd, dst, dst_col0):
    nc, blk = p.nc, p.blk
    PS0, PS1 = u.PS0, u.PS1
    pre = "f%d_" % l

    at = Scope(nc)
    WQKV = at.sb("WQKV", [128, G * 2560], BF16)
    WO = at.sb("WO", [128, 4 * 1024], BF16)
    KROT = at.sb("KROT", [128, 4 * KS], BF16)
    VT = at.sb("VT", [128, NKC * 512], BF16)
    TBC = at.sb("TBC", [128, KS], BF16); TBS = at.sb("TBS", [128, KS], BF16)
    MSK4 = at.sb("MSK4", [128, 4 * 640])
    M0 = at.sb("M0", [128, 128]); M4 = at.sb("M4", [128, 128])
    XIT = at.sb("XIT", [128, G * 513], BF16)
    XPR = at.sb("XPR", [128, G * 512])
    RA = at.sb("RA", [128, 512]); RB = at.sb("RB", [128, 512])
    RC = at.sb("RC", [128, 512]); SQB = at.sb("SQB", [128, 512])
    NRM = at.sb("NRM", [2, 512])
    QROT = at.sb("QROT", [128, 4 * 512], BF16)
    P_sb = at.sb("P_sb", [128, 2 * 640], BF16)
    PT_sb = at.sb("PT_sb", [128, 2 * 640], BF16)
    rowm = at.sb("rowm", [128, 2])
    rsum = at.sb("rsum", [128, 4])
    rinv = at.sb("rinv", [128, 2])
    DIAG = at.sb("DIAG", [128, 2 * 128], BF16)
    o_sb = at.sb("o_sb", [128, 4 * 512], BF16)
    qsc = at.sb("qsc", [128, 1]); qscs = at.sb("qscs", [128, 1])
    ksc = at.sb("ksc", [128, 1]); kscs = at.sb("kscs", [128, 1])

    def s_w(e):
        for kk in range(G):
            for j, nm in enumerate(("wq", "wqs", "wk", "wks", "wv")):
                p.inc(e.dma_start(out=WQKV[:, kk * 2560 + j * 512:kk * 2560 + (j + 1) * 512],
                                  in_=E[pre + nm][kk * 128:(kk + 1) * 128, :]), "ld", 16)
        for c4 in range(4):
            p.inc(e.dma_start(out=WO[:, c4 * 1024:(c4 + 1) * 1024],
                              in_=E[pre + "wo"][c4 * 128:(c4 + 1) * 128, :]), "ld", 16)
        p.inc(e.dma_start(out=TBC[:, :], in_=E["f_ck"][:, :]), "ld", 16)
        p.inc(e.dma_start(out=TBS[:, :], in_=E["f_sk"][:, :]), "ld", 16)
        for qt in range(4):
            p.inc(e.dma_start(out=MSK4[:, qt * 640:(qt + 1) * 640],
                              in_=E["f_mask4"][qt, :, :]), "ld", 16)
        p.inc(e.dma_start(out=M0[:, :], in_=E["f_m0"][:, :]), "ld", 16)
        p.inc(e.dma_start(out=M4[:, :], in_=E["f_m4"][:, :]), "ld", 16)
        qsa = E["f_qscale"][l]
        ksa = E["f_kscale"][l]
        for rep in range(2):
            b0 = rep * 64
            p.inc(e.dma_start(out=qsc[b0:b0 + 64, :], in_=qsa[0:64, :]), "ld", 16)
            p.inc(e.dma_start(out=ksc[b0:b0 + 64, :], in_=ksa[0:64, :]), "ld", 16)
            p.inc(e.dma_start(out=qscs[b0:b0 + 32, :], in_=qsa[32:64, :]), "ld", 16)
            p.inc(e.dma_start(out=qscs[b0 + 32:b0 + 64, :], in_=qsa[0:32, :]), "ld", 16)
            p.inc(e.dma_start(out=kscs[b0:b0 + 32, :], in_=ksa[32:64, :]), "ld", 16)
            p.inc(e.dma_start(out=kscs[b0 + 32:b0 + 64, :], in_=ksa[0:32, :]), "ld", 16)
    blk.gpsimd(s_w)
    p.barrier()

    def shifted(col0, wid, kk):
        off = 1 if kk < 4 else 0
        return XIT[:, kk * 513 + col0 + off:kk * 513 + col0 + off + wid]

    def emit_qk_rot(kind, col0_xi, tok0_tbl, nq, dst_rot, dst_col):
        jW, jWs = (0, 1) if kind == "q" else (2, 3)
        CT, ST = TBC, TBS
        scl, scls = (qsc, qscs) if kind == "q" else (ksc, kscs)

        def s_ld(e):
            p.w(e, "st")
            for kk in range(G):
                p.inc(e.dma_start(out=XIT[:, kk * 513:(kk + 1) * 513],
                                  in_=src[kk * 128:(kk + 1) * 128, col0_xi - 1:col0_xi + 512]), "ld", 16)
        blk.gpsimd(s_ld)

        for qc in range(4):
            def s_mm(e, qc=qc):
                p.w(e, "ld")
                if qc == 0:
                    p.w(e, "ex")
                    p.w(e, "ev")
                for kk in range(G):
                    ins = nc.tensor.matmul(PS0[:, 0:512],
                                           WQKV[:, kk * 2560 + jW * 512 + qc * 128:kk * 2560 + jW * 512 + (qc + 1) * 128],
                                           shifted(0, 512, kk), start=(kk == 0), stop=(kk == G - 1))
                    if kk == G - 1:
                        p.inc(ins, "mm", 1)
                for kk in range(G):
                    ins = nc.tensor.matmul(PS0[:, 512:1024],
                                           WQKV[:, kk * 2560 + jWs * 512 + qc * 128:kk * 2560 + jWs * 512 + (qc + 1) * 128],
                                           shifted(0, 512, kk), start=(kk == 0), stop=(kk == G - 1))
                    if kk == G - 1:
                        p.inc(ins, "mm", 1)
            blk.tensor(s_mm)

            def s_sq(e):
                p.w(e, "mm")
                e.square(SQB[:, :], PS0[:, 0:512])
                e.drain()
                e.sem_inc(p.sems["ex"], 1)
            blk.scalar(s_sq)
            p.cnt["ex"] += 1

            def s_ev(e, qc=qc):
                p.w(e, "ex")
                e.tensor_scalar(RA[:, :], PS0[:, 0:512], scl[:, 0:1], None, OP.mult)
                e.tensor_scalar(RB[:, :], PS0[:, 512:1024], scls[:, 0:1], None, OP.mult)
                e.drain()
                e.sem_inc(p.sems["ev"], 1)
            blk.vector(s_ev)
            p.cnt["ev"] += 1

            def s_nrm(e, qc=qc):
                p.w(e, "ev")
                p.inc(nc.tensor.matmul(PS1[0:2, 0:512], u.eblk[:, :], SQB[:, :],
                                       start=True, stop=True), "mm", 1)
            blk.tensor(s_nrm)

            def s_n2(e):
                p.w(e, "mm")
                e.tensor_copy(NRM[0:2, 0:512], PS1[0:2, 0:512])
                e.drain()
                e.sem_inc(p.sems["ev"], 1)
            blk.vector(s_n2)
            p.cnt["ev"] += 1

            def s_n3(e):
                p.w(e, "ev")
                e.sqrt(NRM[0:2, 0:512], NRM[0:2, 0:512])
                e.drain()
                e.sem_inc(p.sems["ex"], 1)
            blk.scalar(s_n3)
            p.cnt["ex"] += 1

            def s_n4(e):
                p.w(e, "ex")
                e.tensor_scalar(NRM[0:2, 0:512], NRM[0:2, 0:512], 1e-12, None, OP.max)
                e.drain()
                e.reciprocal(NRM[0:2, 0:512], NRM[0:2, 0:512])
                e.drain()
                e.sem_inc(p.sems["ev"], 1)
            blk.vector(s_n4)
            p.cnt["ev"] += 1

            def s_bc(e):
                p.w(e, "ev")
                p.inc(nc.tensor.matmul(PS1[:, 512:1024], u.eblkT[:, :], NRM[0:2, 0:512],
                                       start=True, stop=True), "mm", 1)
            blk.tensor(s_bc)

            def s_rope(e, qc=qc):
                p.w(e, "mm")
                e.tensor_mul(RC[:, :], RA[:, :], CT[:, tok0_tbl:tok0_tbl + 512])
                e.drain()
                e.tensor_mul(RA[:, :], RB[:, :], ST[:, tok0_tbl:tok0_tbl + 512])
                e.drain()
                e.tensor_add(RC[:, :], RC[:, :], RA[:, :])
                e.drain()
                e.tensor_mul(dst_rot[:, dst_col + qc * (KS if kind == "k" else 512):
                                     dst_col + qc * (KS if kind == "k" else 512) + 512],
                             RC[:, :], PS1[:, 512:1024])
                e.drain()
                e.sem_inc(p.sems["ev"], 1)
            blk.vector(s_rope)
            p.cnt["ev"] += 1

            p.barrier()

    ab_sc = Scope(nc)
    PSV = ab_sc.ps("PSV", [128, 1024])
    for kt in range(9):
        col0 = 2 + 512 * kt
        emit_qk_rot("k", col0, 512 * kt, 512, KROT, 512 * kt)
        for vc in range(4):
            def s_vmm(e, kt=kt, vc=vc):
                for kk in range(G):
                    ins = nc.tensor.matmul(PSV[:, 0:512],
                                           shifted(vc * 128, 128, kk),
                                           WQKV[:, kk * 2560 + 4 * 512:kk * 2560 + 5 * 512],
                                           start=(kk == 0), stop=(kk == G - 1))
                    if kk == G - 1:
                        p.inc(ins, "mm", 1)
            blk.tensor(s_vmm)

            def s_vev(e, kt=kt, vc=vc):
                p.w(e, "mm")
                e.tensor_copy(VT[:, (4 * kt + vc) * 512:(4 * kt + vc + 1) * 512], PSV[:, 0:512])
                e.drain()
                e.sem_inc(p.sems["ev"], 1)
            blk.vector(s_vev)
            p.cnt["ev"] += 1

            p.barrier()

    ab_sc.close()

    tapd = E.get("_taps", {})
    if l == 0 and "dbgkrot" in tapd:
        def g_dk(e):
            p.inc(e.dma_start(out=tapd["dbgkrot"][:, :], in_=KROT[:, :]), "st", 16)
            p.inc(e.dma_start(out=tapd["dbgvt"][:, :], in_=VT[:, :]), "st", 16)
        blk.gpsimd(g_dk)
        p.barrier()

    if hasattr(p, 'marks'): p.marks.append(('f%d.keys' % l, nc.n_instructions()))
    c_sc = Scope(nc)
    PSPT = c_sc.ps("PSPT", [128, 1024])
    PSD = c_sc.ps("PSD", [128, 1024])
    def s_qtbl(e):
        p.inc(e.dma_start(out=TBC[:, 0:T], in_=E["f_cq"][:, :]), "ld", 16)
        p.inc(e.dma_start(out=TBS[:, 0:T], in_=E["f_sq"][:, :]), "ld", 16)
    blk.gpsimd(s_qtbl)
    p.barrier()

    for qg in range(8):
        emit_qk_rot("q", KH + 512 * qg, 512 * qg, 512, QROT, 0)

        if l == 0 and qg == 0 and "dbgqrot" in E.get("_taps", {}):
            def g_dq(e, qg=qg):
                p.inc(e.dma_start(out=E["_taps"]["dbgqrot"][:, :], in_=QROT[:, :]), "st", 16)
            blk.gpsimd(g_dq)
            p.barrier()

        sim_mm = {}; ev_max = {}; ex_exp = {}; ev_rinv = {}
        mm_tr = {}; ex_pt = {}; mm_pv = {}; ev_oev = {}
        SIMP = [PS0, PS1]

        def stage2(it, qg=qg):
            h = it % 8
            qtl = it // 8
            qt = qg * 4 + qtl
            pg = it // 2

            def t_tr(e, it=it, h=h):
                e.wait_ge(p.sems["ev"], ev_rinv[it])
                if it - 1 in ex_pt:
                    e.wait_ge(p.sems["ex"], ex_pt[it - 1])
                for j in range(5):
                    slot = PSPT[:, j * 128:(j + 1) * 128]
                    ins = nc.tensor.matmul(slot,
                                           P_sb[:, (it % 2) * 640 + j * 128:(it % 2) * 640 + (j + 1) * 128],
                                           DIAG[:, (it % 2) * 128:(it % 2 + 1) * 128],
                                           start=True, stop=True)
                    if j == 4:
                        p.inc(ins, "mm", 1)
                        mm_tr[it] = p.cnt["mm"]
            blk.tensor(t_tr)

            def a_ev(e, it=it):
                e.wait_ge(p.sems["mm"], mm_tr[it])
                e.activation(out=PT_sb[:, (it % 2) * 640:(it % 2) * 640 + 640],
                             in_=PSPT[:, 0:640], func=AF.Copy)
                e.drain()
                e.sem_inc(p.sems["ex"], 1)
                p.cnt["ex"] += 1
                ex_pt[it] = p.cnt["ex"]
            blk.scalar(a_ev)

            def t_pv(e, it=it, h=h, qt=qt, pg=pg):
                if pg - C2D in ev_oev:
                    e.wait_ge(p.sems["ev"], ev_oev[pg - C2D])
                e.wait_ge(p.sems["ex"], ex_pt[it])
                for j in range(5):
                    pd = PSD[(h % 2) * 64:(h % 2) * 64 + 64,
                             (pg % 2) * 512:(pg % 2) * 512 + 128]
                    ins = nc.tensor.matmul(pd, VT[:, (qt + j) * 512 + h * 64:(qt + j) * 512 + (h + 1) * 64],
                                           PT_sb[:, (it % 2) * 640 + j * 128:(it % 2) * 640 + (j + 1) * 128],
                                           start=(j == 0), stop=(j == 4),
                                           tile_position=(0, (h % 2) * 64))
                    if j == 4:
                        p.inc(ins, "mm", 1)
                        mm_pv[it] = p.cnt["mm"]
            blk.tensor(t_pv)

            if h % 2 == 1:
                def v_oev(e, it=it, h=h, pg=pg, qtl=qtl):
                    e.wait_ge(p.sems["mm"], mm_pv[it])
                    e.wait_ge(p.sems["mm"], mm_pv[it - 1])
                    pd = PSD[:, (pg % 2) * 512:(pg % 2) * 512 + 128]
                    e.tensor_copy(o_sb[:, (h // 2) * 512 + qtl * 128:(h // 2) * 512 + (qtl + 1) * 128], pd)
                    e.drain()
                    e.sem_inc(p.sems["ev"], 1)
                    p.cnt["ev"] += 1
                    ev_oev[pg] = p.cnt["ev"]
                blk.vector(v_oev)

        for it in range(32):
            h = it % 8
            qtl = it // 8
            qt = qg * 4 + qtl
            if C2BAR:
                p.barrier()
            if it >= 2:
                stage2(it - 2)
            if C2BAR:
                p.barrier()

            def t_sim(e, it=it, h=h, qt=qt, qtl=qtl):
                if it - C2D in ex_exp:
                    e.wait_ge(p.sems["ex"], ex_exp[it - C2D])
                lhs = QROT[(h % 2) * 64:(h % 2) * 64 + 64,
                           (h // 2) * 512 + qtl * 128:(h // 2) * 512 + (qtl + 1) * 128]
                rhsbase = (h // 2) * KS + 128 * qt
                pb = SIMP[it % 2]
                nc.tensor.matmul(pb[:, 0:512], lhs, KROT[(h % 2) * 64:(h % 2) * 64 + 64, rhsbase:rhsbase + 512],
                                 start=True, stop=True)
                p.inc(nc.tensor.matmul(pb[:, 512:640], lhs,
                                       KROT[(h % 2) * 64:(h % 2) * 64 + 64, rhsbase + 512:rhsbase + 640],
                                       start=True, stop=True), "mm", 1)
                sim_mm[it] = p.cnt["mm"]
            blk.tensor(t_sim)

            def v_max(e, it=it, qt=qt):
                e.wait_ge(p.sems["mm"], sim_mm[it])
                pb = SIMP[it % 2]
                if qt < 4:
                    e.tensor_add(pb[:, 0:512], pb[:, 0:512], MSK4[:, (qt % 4) * 640:(qt % 4) * 640 + 512])
                    e.tensor_add(pb[:, 512:640], pb[:, 512:640], MSK4[:, (qt % 4) * 640 + 512:(qt % 4 + 1) * 640])
                else:
                    e.tensor_add(pb[:, 0:128], pb[:, 0:128], M0[:, :])
                    e.tensor_add(pb[:, 512:640], pb[:, 512:640], M4[:, :])
                e.drain()
                e.reduce_max(rowm[:, it % 2:it % 2 + 1], pb[:, 0:640], axis=AX.X)
                e.drain()
                e.tensor_scalar(rowm[:, it % 2:it % 2 + 1], rowm[:, it % 2:it % 2 + 1], -8.0, None, OP.mult)
                e.drain()
                e.sem_inc(p.sems["ev"], 1)
                p.cnt["ev"] += 1
                ev_max[it] = p.cnt["ev"]
            blk.vector(v_max)

            def a_exp(e, it=it):
                e.wait_ge(p.sems["ev"], ev_max[it])
                if it - C2D in mm_tr:
                    e.wait_ge(p.sems["mm"], mm_tr[it - C2D])
                pb = SIMP[it % 2]
                e.activation(out=P_sb[:, (it % 2) * 640:(it % 2) * 640 + 512], in_=pb[:, 0:512],
                             func=AF.Exp, bias=rowm[:, it % 2:it % 2 + 1], scale=8.0,
                             accum_out=rsum[:, (it % 2) * 2:(it % 2) * 2 + 1])
                e.activation(out=P_sb[:, (it % 2) * 640 + 512:(it % 2) * 640 + 640], in_=pb[:, 512:640],
                             func=AF.Exp, bias=rowm[:, it % 2:it % 2 + 1], scale=8.0,
                             accum_out=rsum[:, (it % 2) * 2 + 1:(it % 2) * 2 + 2])
                e.drain()
                e.sem_inc(p.sems["ex"], 1)
                p.cnt["ex"] += 1
                ex_exp[it] = p.cnt["ex"]
            blk.scalar(a_exp)

            def v_rs(e, it=it):
                e.wait_ge(p.sems["ex"], ex_exp[it])
                e.tensor_add(rinv[:, it % 2:it % 2 + 1], rsum[:, (it % 2) * 2:(it % 2) * 2 + 1],
                             rsum[:, (it % 2) * 2 + 1:(it % 2) * 2 + 2])
                e.drain()
                e.reciprocal(rinv[:, it % 2:it % 2 + 1], rinv[:, it % 2:it % 2 + 1])
                e.drain()
                e.tensor_scalar(DIAG[:, (it % 2) * 128:(it % 2 + 1) * 128], u.ident[:, :],
                                rinv[:, it % 2:it % 2 + 1], None, OP.mult)
                e.drain()
                e.sem_inc(p.sems["ev"], 1)
                p.cnt["ev"] += 1
                ev_rinv[it] = p.cnt["ev"]
            blk.vector(v_rs)

        stage2(30)
        stage2(31)

        if l == 0 and qg == 0 and "dbgo" in E.get("_taps", {}):
            p.barrier()
            def g_do(e):
                p.inc(e.dma_start(out=E["_taps"]["dbgo"][:, :], in_=o_sb[:, :]), "st", 16)
                if "dbgp" in E["_taps"]:
                    p.inc(e.dma_start(out=E["_taps"]["dbgp"][:, :], in_=P_sb[:, :]), "st", 16)
            blk.gpsimd(g_do)
            p.barrier()

        def s_ldxi(e, qg=qg):
            p.w(e, "st")
            for kk in range(G):
                p.inc(e.dma_start(out=XPR[:, kk * 512:(kk + 1) * 512],
                                  in_=src[kk * 128:(kk + 1) * 128, KH + qg * 512:KH + (qg + 1) * 512]), "ld", 16)
        blk.sync(s_ldxi)

        ev_xp = {}
        for oc in range(G):
            def s_wo(e, oc=oc):
                if oc == 0:
                    p.w(e, "ld")
                    p.w(e, "ex")
                    p.w(e, "ev")
                if oc - 2 in ev_xp:
                    e.wait_ge(p.sems["ev"], ev_xp[oc - 2])
                for c4 in range(4):
                    ins = nc.tensor.matmul(PS0[:, (oc % 2) * 512:(oc % 2) * 512 + 512],
                                           WO[:, c4 * 1024 + oc * 128:c4 * 1024 + (oc + 1) * 128],
                                           o_sb[:, c4 * 512:(c4 + 1) * 512],
                                           start=(c4 == 0), stop=(c4 == 3))
                    if c4 == 3:
                        p.inc(ins, "mm", 1)
                        ev_xp[(oc, "mm")] = p.cnt["mm"]
            blk.tensor(s_wo)

            def s_xp(e, oc=oc):
                e.wait_ge(p.sems["mm"], ev_xp[(oc, "mm")])
                e.scalar_tensor_tensor(out=XPR[:, oc * 512:(oc + 1) * 512],
                                       in0=PS0[:, (oc % 2) * 512:(oc % 2) * 512 + 512],
                                       scalar=1.0, in1=XPR[:, oc * 512:(oc + 1) * 512],
                                       op0=OP.mult, op1=OP.add)
                e.drain()
                e.sem_inc(p.sems["ev"], 1)
                p.cnt["ev"] += 1
                ev_xp[oc] = p.cnt["ev"]
            blk.vector(s_xp)

        def s_stxp(e, qg=qg):
            p.w(e, "ev")
            for kk in range(G):
                p.inc(e.dma_start(out=xpd[kk * 128:(kk + 1) * 128, 1 + qg * 512:1 + (qg + 1) * 512],
                                  in_=XPR[:, kk * 512:(kk + 1) * 512]), "st", 16)
        blk.sync(s_stxp)
        p.barrier()

        if l == 0 and qg == 0 and "dbgo2" in E.get("_taps", {}):
            def g_do2(e):
                p.inc(e.dma_start(out=E["_taps"]["dbgo2"][:, :], in_=o_sb[:, :]), "st", 16)
                p.inc(e.dma_start(out=E["_taps"]["dbgxpr"][:, 0:G * 512], in_=XPR[:, :]), "st", 16)
            blk.gpsimd(g_do2)
            p.barrier()

    c_sc.close()
    at.close()

    ex_sc = Scope(nc)
    emit_exchange(p, u, ex_sc, xpd, T - 7, 8, E["snd_s"], E["rcv_s"], xpd, 0, cols=(7, 8))
    ex_sc.close()

    if hasattr(p, 'marks'): p.marks.append(('f%d.attn' % l, nc.n_instructions()))
    T3F = 512
    NTF = T // T3F
    ff = Scope(nc)
    W1A = ff.sb("W1A", [128, G * HIDP], BF16)
    W1G = ff.sb("W1G", [128, G * HIDP], BF16)
    W2 = ff.sb("W2", [128, NHC * 1024], BF16)
    XPT = ff.sb("XPT", [128, G * (T3F + 1)])
    SQF = ff.sb("SQF", [128, 2 * T3F])
    H3 = ff.sb("H3", [128, G * T3F], BF16)
    HH = ff.sb("HH", [128, NHC * T3F], BF16)
    fr1 = ff.sb("fr1", [1, T3F]); fr2 = ff.sb("fr2", [1, T3F])
    lgrow = ff.sb("lgrow", [1, D]); lbrow = ff.sb("lbrow", [1, D])
    b1a_c = ff.sb("b1a_c", [128, NHC]); b1g_c = ff.sb("b1g_c", [128, NHC])
    b2_c = ff.sb("b2_c", [128, G])
    PS2 = ff.ps("PS2ff", [128, 1024])
    PS3 = ff.ps("PS3ff", [128, 1024])

    def s_wf(e):
        for kk in range(G):
            p.inc(e.dma_start(out=W1A[:, kk * HIDP:(kk + 1) * HIDP],
                              in_=E[pre + "w1a"][kk * 128:(kk + 1) * 128, :]), "ld", 16)
            p.inc(e.dma_start(out=W1G[:, kk * HIDP:(kk + 1) * HIDP],
                              in_=E[pre + "w1g"][kk * 128:(kk + 1) * 128, :]), "ld", 16)
        for hc in range(NHC):
            p.inc(e.dma_start(out=W2[:, hc * 1024:(hc + 1) * 1024],
                              in_=E[pre + "w2"][hc * 128:(hc + 1) * 128, :]), "ld", 16)
            p.inc(e.dma_start(out=b1a_c[:, hc:hc + 1], in_=E[pre + "b1a"][hc * 128:(hc + 1) * 128, :]), "ld", 16)
            p.inc(e.dma_start(out=b1g_c[:, hc:hc + 1], in_=E[pre + "b1g"][hc * 128:(hc + 1) * 128, :]), "ld", 16)
        for g in range(G):
            p.inc(e.dma_start(out=b2_c[:, g:g + 1], in_=E[pre + "b2"][g * 128:(g + 1) * 128, :]), "ld", 16)
        p.inc(e.dma_start(out=lgrow[:, :], in_=E[pre + "lg"][:, :]), "ld", 16)
        p.inc(e.dma_start(out=lbrow[:, :], in_=E[pre + "lb"][:, :]), "ld", 16)
    blk.gpsimd(s_wf)
    p.barrier()

    for tf in range(NTF):
        c0 = tf * T3F

        def fsh(kk, wid=T3F):
            off = 1 if kk < 4 else 0
            return XPT[:, kk * (T3F + 1) + off:kk * (T3F + 1) + off + wid]

        def s_lx(e, c0=c0):
            p.w(e, "st")
            for kk in range(G):
                p.inc(e.dma_start(out=XPT[:, kk * (T3F + 1):(kk + 1) * (T3F + 1)],
                                  in_=xpd[kk * 128:(kk + 1) * 128, c0:c0 + T3F + 1]), "ld", 16)
        blk.sync(s_lx)

        for kk in range(G):
            def s_sqf(e, kk=kk):
                if kk == 0:
                    p.w(e, "ld")
                else:
                    p.w(e, "mm")
                e.square(SQF[:, 0:T3F], fsh(kk))
                e.drain()
                e.sem_inc(p.sems["ex"], 1)
            blk.scalar(s_sqf)
            p.cnt["ex"] += 1

            def s_mmsq(e, kk=kk):
                p.w(e, "ex")
                ins = nc.tensor.matmul(PS0[0:1, 0:T3F], u.ones_col[:, :], SQF[:, 0:T3F],
                                       start=(kk == 0), stop=(kk == G - 1))
                if kk == G - 1:
                    p.inc(ins, "mm", 1)
                else:
                    p.inc(ins, "mm", 1)
            blk.tensor(s_mmsq)

        def s_mmsum(e):
            for kk in range(G):
                ins = nc.tensor.matmul(PS1[0:1, 0:T3F], u.ones_col[:, :], fsh(kk),
                                       start=(kk == 0), stop=(kk == G - 1))
                if kk == G - 1:
                    p.inc(ins, "mm", 1)
        blk.tensor(s_mmsum)

        def s_st2(e):
            p.w(e, "mm")
            e.tensor_scalar(fr2[0:1, :], PS1[0:1, 0:T3F], 1.0 / 1024.0, None, OP.mult)
            e.drain()
            e.tensor_mul(fr1[0:1, :], fr2[0:1, :], fr2[0:1, :])
            e.drain()
            e.scalar_tensor_tensor(out=fr1[0:1, :], in0=PS0[0:1, 0:T3F], scalar=1.0 / 1024.0,
                                   in1=fr1[0:1, :], op0=OP.mult, op1=OP.subtract)
            e.drain()
            e.tensor_scalar(fr1[0:1, :], fr1[0:1, :], 1e-5, None, OP.add)
            e.drain()
            e.reciprocal(fr1[0:1, :], fr1[0:1, :])
            e.drain()
            e.sem_inc(p.sems["ev"], 1)
        blk.vector(s_st2)
        p.cnt["ev"] += 1

        def s_sq3(e):
            p.w(e, "ev")
            e.sqrt(fr1[0:1, :], fr1[0:1, :])
            e.drain()
            e.sem_inc(p.sems["ex"], 1)
        blk.scalar(s_sq3)
        p.cnt["ex"] += 1

        def s_negm(e):
            p.w(e, "ex")
            e.scalar_tensor_tensor(out=fr2[0:1, :], in0=fr2[0:1, :], scalar=-1.0,
                                   in1=fr1[0:1, :], op0=OP.mult, op1=OP.mult)
            e.drain()
            e.sem_inc(p.sems["ev"], 1)
        blk.vector(s_negm)
        p.cnt["ev"] += 1

        fst = {}
        for kk in range(G):
            def s_r1(e, kk=kk):
                if kk == 0:
                    p.w(e, "ev")
                else:
                    e.wait_ge(p.sems["ev"], fst[("h3", kk - 1)])
                p.inc(nc.tensor.matmul(PS2[:, 0:T3F], lgrow[0:1, kk * 128:(kk + 1) * 128], fr1[0:1, :],
                                       start=True, stop=True), "mm", 1)
                nc.tensor.matmul(PS3[:, 0:T3F], lbrow[0:1, kk * 128:(kk + 1) * 128], u.onesN[0:1, 0:T3F],
                                 start=True, stop=False)
                p.inc(nc.tensor.matmul(PS3[:, 0:T3F], lgrow[0:1, kk * 128:(kk + 1) * 128], fr2[0:1, :],
                                       start=False, stop=True), "mm", 1)
                fst[("r1", kk)] = p.cnt["mm"]
            blk.tensor(s_r1)

            def s_h3(e, kk=kk):
                e.wait_ge(p.sems["mm"], fst[("r1", kk)])
                e.scalar_tensor_tensor(out=SQF[:, 0:T3F], in0=fsh(kk), scalar=1.0,
                                       in1=PS2[:, 0:T3F], op0=OP.mult, op1=OP.mult)
                e.drain()
                e.tensor_add(H3[:, kk * T3F:(kk + 1) * T3F], SQF[:, 0:T3F], PS3[:, 0:T3F])
                e.drain()
                e.sem_inc(p.sems["ev"], 1)
                p.cnt["ev"] += 1
                fst[("h3", kk)] = p.cnt["ev"]
            blk.vector(s_h3)

        for hc in range(NHC):
            def s_w1(e, hc=hc):
                if hc == 0:
                    p.w(e, "ev")
                if ("hh", hc - 2) in fst:
                    e.wait_ge(p.sems["ev"], fst[("hh", hc - 2)])
                    e.wait_ge(p.sems["ex"], fst[("gl", hc - 2)])
                for kk in range(G):
                    ins = nc.tensor.matmul(PS0[:, (hc % 2) * 512:(hc % 2) * 512 + T3F],
                                           W1A[:, kk * HIDP + hc * 128:kk * HIDP + (hc + 1) * 128],
                                           H3[:, kk * T3F:(kk + 1) * T3F],
                                           start=(kk == 0), stop=(kk == G - 1))
                for kk in range(G):
                    ins = nc.tensor.matmul(PS1[:, (hc % 2) * 512:(hc % 2) * 512 + T3F],
                                           W1G[:, kk * HIDP + hc * 128:kk * HIDP + (hc + 1) * 128],
                                           H3[:, kk * T3F:(kk + 1) * T3F],
                                           start=(kk == 0), stop=(kk == G - 1))
                    if kk == G - 1:
                        p.inc(ins, "mm", 1)
                        fst[("w1", hc)] = p.cnt["mm"]
            blk.tensor(s_w1)

            def s_gelu(e, hc=hc):
                e.wait_ge(p.sems["mm"], fst[("w1", hc)])
                if ("hh", hc - 2) in fst:
                    e.wait_ge(p.sems["ev"], fst[("hh", hc - 2)])
                e.activation(out=SQF[:, (hc % 2) * T3F:(hc % 2) * T3F + T3F],
                             in_=PS1[:, (hc % 2) * 512:(hc % 2) * 512 + T3F],
                             func=AF.Gelu, bias=b1g_c[:, hc:hc + 1], scale=1.0)
                e.drain()
                e.sem_inc(p.sems["ex"], 1)
                p.cnt["ex"] += 1
                fst[("gl", hc)] = p.cnt["ex"]
            blk.scalar(s_gelu)

            def s_hh(e, hc=hc):
                e.wait_ge(p.sems["ex"], fst[("gl", hc)])
                e.scalar_tensor_tensor(out=HH[:, hc * T3F:(hc + 1) * T3F],
                                       in0=PS0[:, (hc % 2) * 512:(hc % 2) * 512 + T3F],
                                       scalar=b1a_c[:, hc:hc + 1], in1=SQF[:, (hc % 2) * T3F:(hc % 2) * T3F + T3F],
                                       op0=OP.add, op1=OP.mult)
                e.drain()
                e.sem_inc(p.sems["ev"], 1)
                p.cnt["ev"] += 1
                fst[("hh", hc)] = p.cnt["ev"]
            blk.vector(s_hh)

        for oc in range(G):
            def s_w2(e, oc=oc):
                if oc == 0:
                    p.w(e, "ev")
                if ("o", oc - 2) in fst:
                    e.wait_ge(p.sems["ev"], fst[("o", oc - 2)])
                for hc in range(NHC):
                    ins = nc.tensor.matmul(PS2[:, (oc % 2) * 512:(oc % 2) * 512 + T3F],
                                           W2[:, hc * 1024 + oc * 128:hc * 1024 + (oc + 1) * 128],
                                           HH[:, hc * T3F:(hc + 1) * T3F],
                                           start=(hc == 0), stop=(hc == NHC - 1))
                    if hc == NHC - 1:
                        p.inc(ins, "mm", 1)
                        fst[("w2", oc)] = p.cnt["mm"]
            blk.tensor(s_w2)

            def s_out(e, oc=oc):
                e.wait_ge(p.sems["mm"], fst[("w2", oc)])
                e.scalar_tensor_tensor(out=XPT[:, oc * (T3F + 1) + 1:oc * (T3F + 1) + 1 + T3F],
                                       in0=PS2[:, (oc % 2) * 512:(oc % 2) * 512 + T3F],
                                       scalar=b2_c[:, oc:oc + 1],
                                       in1=XPT[:, oc * (T3F + 1) + 1:oc * (T3F + 1) + 1 + T3F],
                                       op0=OP.add, op1=OP.add)
                e.drain()
                e.sem_inc(p.sems["ev"], 1)
                p.cnt["ev"] += 1
                fst[("o", oc)] = p.cnt["ev"]
            blk.vector(s_out)

        def s_sto(e, c0=c0):
            p.w(e, "ev")
            for kk in range(G):
                p.inc(e.dma_start(out=dst[kk * 128:(kk + 1) * 128, dst_col0 + c0:dst_col0 + c0 + T3F],
                                  in_=XPT[:, kk * (T3F + 1) + 1:kk * (T3F + 1) + 1 + T3F]), "st", 16)
        blk.sync(s_sto)
        p.barrier()

    ff.close()
    if hasattr(p, 'marks'): p.marks.append(('f%d.ffn' % l, nc.n_instructions()))


def build_l0f(p, E, u):
    nc, blk = p.nc, p.blk

    sc = Scope(nc)
    emit_exchange(p, u, sc, E["x1_d"], T - KH, KH, E["snd_h"], E["rcv_h"], E["hx1_d"], 0)
    sc.close()

    lnsc = Scope(nc)
    LA = lnsc.sb("LA", [128, G * 512])
    LB = lnsc.sb("LB", [128, G * 512])
    r1 = lnsc.sb("r1", [1, 512]); r2 = lnsc.sb("r2", [1, 512])
    PSa = lnsc.ps("PSaln", [128, 1024]); PSb = lnsc.ps("PSbln", [128, 1024])
    emit_ln_job(p, u, LA, LB, r1, r2, E["hx1_d"], 0, E["xi0_d"], 0, 512, PSa, PSb)
    emit_ln_job(p, u, LA, LB, r1, r2, E["hx1_d"], 512, E["xi0_d"], 512, 2, PSa, PSb)
    for i in range(8):
        emit_ln_job(p, u, LA, LB, r1, r2, E["x1_d"], i * 512, E["xi0_d"], KH + i * 512, 512, PSa, PSb)
    lnsc.close()

    build_l0f_layer(p, E, u, 0, E["xi0_d"], E["xp0_d"], E["xi1_d"], KH)
    sc = Scope(nc)
    emit_exchange(p, u, sc, E["xi1_d"], KH + T - KH, KH, E["snd_h"], E["rcv_h"], E["xi1_d"], 0)
    sc.close()
    build_l0f_layer(p, E, u, 1, E["xi1_d"], E["xp1_d"], E["xi2_d"], 0)

    sc = Scope(nc)
    XA = sc.sb("XA", [128, G * 512])
    XB = sc.sb("XB", [128, G * 512])
    for i in range(8):
        def s_l(e, i=i):
            p.w(e, "st")
            for g in range(G):
                p.inc(e.dma_start(out=XA[:, g * 512:(g + 1) * 512],
                                  in_=E["x1_d"][g * 128:(g + 1) * 128, i * 512:(i + 1) * 512]), "ld", 16)
                p.inc(e.dma_start(out=XB[:, g * 512:(g + 1) * 512],
                                  in_=E["xi2_d"][g * 128:(g + 1) * 128, i * 512:(i + 1) * 512]), "ld", 16)
        blk.sync(s_l)

        def s_a(e):
            p.w(e, "ld")
            e.tensor_add(XA[:, :], XA[:, :], XB[:, :])
            e.drain()
            e.sem_inc(p.sems["ev"], 1)
        blk.vector(s_a)
        p.cnt["ev"] += 1

        def s_s(e, i=i):
            p.w(e, "ev")
            for g in range(G):
                p.inc(e.dma_start(out=E["x2_d"][g * 128:(g + 1) * 128, i * 512:(i + 1) * 512],
                                  in_=XA[:, g * 512:(g + 1) * 512]), "st", 16)
        blk.sync(s_s)
        p.barrier()
    sc.close()


def emit_ln_sb(p, u, XT, W, OUT, grow, brow, fr1, fr2, SQ1, PSa, PSb, eps=1e-5):
    nc, blk = p.nc, p.blk
    for kk in range(G):
        def s_sq(e, kk=kk):
            if kk > 0:
                p.w(e, "mm")
            else:
                p.w(e, "ld")
            e.square(SQ1[:, 0:W], XT[:, kk * W:(kk + 1) * W])
            e.drain()
            e.sem_inc(p.sems["ex"], 1)
        blk.scalar(s_sq)
        p.cnt["ex"] += 1

        def s_mm(e, kk=kk):
            p.w(e, "ex")
            p.inc(nc.tensor.matmul(u.PS0[0:1, 0:W], u.ones_col[:, :], SQ1[:, 0:W],
                                   start=(kk == 0), stop=(kk == G - 1)), "mm", 1)
        blk.tensor(s_mm)

    def s_mmsum(e):
        for kk in range(G):
            ins = nc.tensor.matmul(u.PS1[0:1, 0:W], u.ones_col[:, :], XT[:, kk * W:(kk + 1) * W],
                                   start=(kk == 0), stop=(kk == G - 1))
            if kk == G - 1:
                p.inc(ins, "mm", 1)
    blk.tensor(s_mmsum)

    def s_st(e):
        p.w(e, "mm")
        e.tensor_scalar(fr2[0:1, 0:W], u.PS1[0:1, 0:W], 1.0 / 1024.0, None, OP.mult)
        e.drain()
        e.tensor_mul(fr1[0:1, 0:W], fr2[0:1, 0:W], fr2[0:1, 0:W])
        e.drain()
        e.scalar_tensor_tensor(out=fr1[0:1, 0:W], in0=u.PS0[0:1, 0:W], scalar=1.0 / 1024.0,
                               in1=fr1[0:1, 0:W], op0=OP.mult, op1=OP.subtract)
        e.drain()
        e.tensor_scalar(fr1[0:1, 0:W], fr1[0:1, 0:W], eps, None, OP.add)
        e.drain()
        e.reciprocal(fr1[0:1, 0:W], fr1[0:1, 0:W])
        e.drain()
        e.sem_inc(p.sems["ev"], 1)
    blk.vector(s_st)
    p.cnt["ev"] += 1

    def s_sqr(e):
        p.w(e, "ev")
        e.sqrt(fr1[0:1, 0:W], fr1[0:1, 0:W])
        e.drain()
        e.sem_inc(p.sems["ex"], 1)
    blk.scalar(s_sqr)
    p.cnt["ex"] += 1

    def s_nm(e):
        p.w(e, "ex")
        e.scalar_tensor_tensor(out=fr2[0:1, 0:W], in0=fr2[0:1, 0:W], scalar=-1.0,
                               in1=fr1[0:1, 0:W], op0=OP.mult, op1=OP.mult)
        e.drain()
        e.sem_inc(p.sems["ev"], 1)
    blk.vector(s_nm)
    p.cnt["ev"] += 1

    lnst = {}
    for kk in range(G):
        def s_r1(e, kk=kk):
            if kk == 0:
                p.w(e, "ev")
            else:
                e.wait_ge(p.sems["ev"], lnst[kk - 1])
            if grow is not None:
                p.inc(nc.tensor.matmul(PSa[:, 0:W], grow[0:1, kk * 128:(kk + 1) * 128], fr1[0:1, 0:W],
                                       start=True, stop=True), "mm", 1)
                nc.tensor.matmul(PSb[:, 0:W], brow[0:1, kk * 128:(kk + 1) * 128], u.onesN[0:1, 0:W],
                                 start=True, stop=False)
                p.inc(nc.tensor.matmul(PSb[:, 0:W], grow[0:1, kk * 128:(kk + 1) * 128], fr2[0:1, 0:W],
                                       start=False, stop=True), "mm", 1)
            else:
                p.inc(nc.tensor.matmul(PSa[:, 0:W], u.ones_row[:, :], fr1[0:1, 0:W],
                                       start=True, stop=True), "mm", 1)
                p.inc(nc.tensor.matmul(PSb[:, 0:W], u.ones_row[:, :], fr2[0:1, 0:W],
                                       start=True, stop=True), "mm", 1)
        blk.tensor(s_r1)

        def s_ap(e, kk=kk):
            p.w(e, "mm")
            e.scalar_tensor_tensor(out=SQ1[:, 0:W], in0=XT[:, kk * W:(kk + 1) * W], scalar=1.0,
                                   in1=PSa[:, 0:W], op0=OP.mult, op1=OP.mult)
            e.drain()
            e.tensor_add(OUT[:, kk * W:(kk + 1) * W], SQ1[:, 0:W], PSb[:, 0:W])
            e.drain()
            e.sem_inc(p.sems["ev"], 1)
            p.cnt["ev"] += 1
            lnst[kk] = p.cnt["ev"]
        blk.vector(s_ap)


def emit_trig(p, u, PIN, pofs, POUT, cofs, sofs, W, TMP1, TMP2):
    nc, blk = p.nc, p.blk
    TWO_PI = float(2.0 * math.pi)
    MAGIC = 12582912.0

    def s_t(e):
        p.w(e, "mm")
        p.w(e, "ex")
        e.tensor_scalar(TMP1[:, 0:W], PIN, 0.125, None, OP.mult)
        e.drain()
        e.tensor_scalar(TMP2[:, 0:W], TMP1[:, 0:W], 1.0 / TWO_PI, MAGIC, OP.mult, OP.add)
        e.drain()
        e.tensor_scalar(TMP2[:, 0:W], TMP2[:, 0:W], MAGIC, None, OP.subtract)
        e.drain()
        e.scalar_tensor_tensor(out=TMP2[:, 0:W], in0=TMP2[:, 0:W], scalar=-TWO_PI,
                               in1=TMP1[:, 0:W], op0=OP.mult, op1=OP.add)
        e.drain()
        e.sem_inc(p.sems["ev"], 1)
    blk.vector(s_t)
    p.cnt["ev"] += 1

    def s_sin(e):
        p.w(e, "ev")
        e.activation(out=POUT[:, sofs:sofs + W], in_=TMP2[:, 0:W], func=AF.Sin)
        e.drain()
        e.sem_inc(p.sems["ex"], 1)
    blk.scalar(s_sin)
    p.cnt["ex"] += 1

    def s_t2(e):
        p.w(e, "ex")
        e.tensor_scalar(TMP2[:, 0:W], TMP1[:, 0:W], math.pi / 2.0, None, OP.add)
        e.drain()
        e.tensor_scalar(TMP1[:, 0:W], TMP2[:, 0:W], 1.0 / TWO_PI, MAGIC, OP.mult, OP.add)
        e.drain()
        e.tensor_scalar(TMP1[:, 0:W], TMP1[:, 0:W], MAGIC, None, OP.subtract)
        e.drain()
        e.scalar_tensor_tensor(out=TMP1[:, 0:W], in0=TMP1[:, 0:W], scalar=-TWO_PI,
                               in1=TMP2[:, 0:W], op0=OP.mult, op1=OP.add)
        e.drain()
        e.sem_inc(p.sems["ev"], 1)
    blk.vector(s_t2)
    p.cnt["ev"] += 1

    def s_cos(e):
        p.w(e, "ev")
        e.activation(out=POUT[:, cofs:cofs + W], in_=TMP1[:, 0:W], func=AF.Sin)
        e.drain()
        e.sem_inc(p.sems["ex"], 1)
    blk.scalar(s_cos)
    p.cnt["ex"] += 1


def build_l1(p, E, u):
    nc, blk = p.nc, p.blk
    PS0, PS1 = u.PS0, u.PS1

    pk = Scope(nc)
    WK = pk.sb("WK", [128, G * 1024], BF16)
    WV = pk.sb("WV", [128, G * 1024], BF16)
    PROJ = pk.sb("PROJ", [128, 128], BF16)
    XT = pk.sb("XT", [128, G * 512])
    XLN = pk.sb("XLN", [128, G * 512], BF16)
    KF = pk.sb("KF", [128, G * 512], BF16)
    VTK = pk.sb("VTK", [128, 4 * 1024], BF16)
    KP = pk.sb("KP", [128, 2 * 256], BF16)
    TMP1 = pk.sb("TMP1", [128, 256]); TMP2 = pk.sb("TMP2", [128, 256])
    SQ1 = pk.sb("SQ1", [128, 512])
    fr1 = pk.sb("fr1", [1, 512]); fr2 = pk.sb("fr2", [1, 512])
    g1row = pk.sb("g1row", [1, D]); b1row = pk.sb("b1row", [1, D])
    bkc = pk.sb("bkc", [128, G])
    BVP = pk.sb("BVP", [128, 1024])
    bvrow = pk.sb("bvrow", [1, D])
    KVS = pk.sb("KVS", [128, 16 * 128])
    KVACC = pk.sb("KVACC", [128, 16 * 128])
    PSkv0 = pk.ps("PSkv0", [128, 1024])
    PSkv1 = pk.ps("PSkv1", [128, 1024])
    PSa = u.PS0
    PSb = u.PS1

    def s_wk(e):
        for kk in range(G):
            p.inc(e.dma_start(out=WK[:, kk * 1024:(kk + 1) * 1024],
                              in_=E["a_wk"][kk * 128:(kk + 1) * 128, :]), "ld", 16)
            p.inc(e.dma_start(out=WV[:, kk * 1024:(kk + 1) * 1024],
                              in_=E["a_wv"][kk * 128:(kk + 1) * 128, :]), "ld", 16)
            p.inc(e.dma_start(out=bkc[:, kk:kk + 1], in_=E["a_bk"][kk * 128:(kk + 1) * 128, :]), "ld", 16)
        p.inc(e.dma_start(out=PROJ[0:64, :], in_=E["a_proj"][:, :]), "ld", 16)
        p.inc(e.dma_start(out=PROJ[64:128, :], in_=E["a_proj"][:, :]), "ld", 16)
    blk.gpsimd(s_wk)

    def s_row(e):
        p.inc(e.dma_start(out=g1row[:, :], in_=E["a_ln1_g"][:, :]), "ld", 16)
        p.inc(e.dma_start(out=b1row[:, :], in_=E["a_ln1_b"][:, :]), "ld", 16)
        p.inc(e.dma_start(out=bvrow[:, :], in_=E["a_bv"][:, :]), "ld", 16)
    blk.sync(s_row)
    p.barrier()

    def s_bvp(e):
        p.inc(nc.tensor.matmul(PSa[:, 0:512], u.ones_row[:, :], bvrow[0:1, 0:512],
                               start=True, stop=True), "mm", 1)
        p.inc(nc.tensor.matmul(PSa[:, 512:1024], u.ones_row[:, :], bvrow[0:1, 512:1024],
                               start=True, stop=True), "mm", 1)
    blk.tensor(s_bvp)

    def s_bvp2(e):
        p.w(e, "mm")
        e.tensor_copy(BVP[:, :], PSa[:, 0:1024])
        e.drain()
        e.memset(KVACC[:, :], 0.0)
        e.drain()
        e.sem_inc(p.sems["ev"], 1)
    blk.vector(s_bvp2)
    p.cnt["ev"] += 1
    p.barrier()

    for tt in range(NT):
        c0 = tt * TT

        def s_lx(e, c0=c0):
            for kk in range(G):
                p.inc(e.dma_start(out=XT[:, kk * 512:(kk + 1) * 512],
                                  in_=E["x2_d"][kk * 128:(kk + 1) * 128, c0:c0 + 512]), "ld", 16)
        blk.sync(s_lx)

        def s_wait(e):
            p.w(e, "ld")
            e.engine_nop()
            e.drain()
            e.sem_inc(p.sems["ev"], 1)
        blk.vector(s_wait)
        p.cnt["ev"] += 1
        p.barrier()

        emit_ln_sb(p, u, XT, 512, XLN, g1row, b1row, fr1, fr2, SQ1, u.PS0[:, 512:1024], u.PS1[:, 512:1024])

        def s_stln(e, c0=c0):
            p.w(e, "ev")
            for kk in range(G):
                p.inc(e.dma_start(out=E["x2ln_d"][kk * 128:(kk + 1) * 128, c0:c0 + 512],
                                  in_=XLN[:, kk * 512:(kk + 1) * 512]), "st", 16)
        blk.gpsimd(s_stln)

        for oc in range(G):
            def s_kmm(e, oc=oc):
                if oc == 0:
                    p.w(e, "ev")
                for kk in range(G):
                    ins = nc.tensor.matmul(PSa[:, (oc % 2) * 512:(oc % 2) * 512 + 512],
                                           WK[:, kk * 1024 + oc * 128:kk * 1024 + (oc + 1) * 128],
                                           XLN[:, kk * 512:(kk + 1) * 512],
                                           start=(kk == 0), stop=(kk == G - 1))
                    if kk == G - 1:
                        p.inc(ins, "mm", 1)
            blk.tensor(s_kmm)

            def s_kev(e, oc=oc):
                p.w(e, "mm")
                e.tensor_scalar(KF[:, oc * 512:(oc + 1) * 512], PSa[:, (oc % 2) * 512:(oc % 2) * 512 + 512],
                                bkc[:, oc:oc + 1], None, OP.add)
                e.drain()
                e.sem_inc(p.sems["ev"], 1)
            blk.vector(s_kev)
            p.cnt["ev"] += 1
            p.barrier()

        for vc in range(4):
            def s_vmm(e, vc=vc):
                for kk in range(G):
                    ins = nc.tensor.matmul(PSb[:, 0:512],
                                           XLN[:, kk * 512 + vc * 128:kk * 512 + (vc + 1) * 128],
                                           WV[:, kk * 1024:kk * 1024 + 512],
                                           start=(kk == 0), stop=(kk == G - 1))
                    ins = nc.tensor.matmul(PSb[:, 512:1024],
                                           XLN[:, kk * 512 + vc * 128:kk * 512 + (vc + 1) * 128],
                                           WV[:, kk * 1024 + 512:(kk + 1) * 1024],
                                           start=(kk == 0), stop=(kk == G - 1))
                    if kk == G - 1:
                        p.inc(ins, "mm", 1)
            blk.tensor(s_vmm)

            def s_vev(e, vc=vc):
                p.w(e, "mm")
                e.tensor_add(VTK[:, vc * 1024:(vc + 1) * 1024], PSb[:, 0:1024], BVP[:, :])
                e.drain()
                e.sem_inc(p.sems["ev"], 1)
            blk.vector(s_vev)
            p.cnt["ev"] += 1
            p.barrier()

        for sub in range(4):
            for hp in range(8):
                def s_pmm(e, sub=sub, hp=hp):
                    for hh in range(2):
                        b0 = hh * 64
                        ins = nc.tensor.matmul(PSb[:, hh * 512:hh * 512 + 128],
                                               KF[b0:b0 + 64, hp * 512 + sub * 128:hp * 512 + (sub + 1) * 128],
                                               PROJ[b0:b0 + 64, :],
                                               start=True, stop=True)
                        p.inc(ins, "mm", 1)
                blk.tensor(s_pmm)

                for hh in range(2):
                    emit_trig(p, u, PSb[:, hh * 512:hh * 512 + 128], 0,
                              KP, hh * 256, hh * 256 + 128, 128, TMP1, TMP2)

                def s_kvmm(e, sub=sub, hp=hp):
                    p.w(e, "ex")
                    for hh in range(2):
                        h = hp * 2 + hh
                        pskv = PSkv0 if h < 8 else PSkv1
                        hb = h % 8
                        for mc in range(2):
                            ins = nc.tensor.matmul(
                                pskv[:, hb * 128 + mc * 64:hb * 128 + (mc + 1) * 64],
                                KP[:, hh * 256 + mc * 128:hh * 256 + (mc + 1) * 128],
                                VTK[:, sub * 1024 + h * 64:sub * 1024 + (h + 1) * 64],
                                start=True, stop=True)
                            if hp == 7 and hh == 1 and mc == 1:
                                p.inc(ins, "mm", 1)
                blk.tensor(s_kvmm)
                p.barrier()

            def s_kvacc(e, sub=sub):
                e.tensor_add(KVACC[:, 0:1024], KVACC[:, 0:1024], PSkv0[:, 0:1024])
                e.drain()
                e.tensor_add(KVACC[:, 1024:2048], KVACC[:, 1024:2048], PSkv1[:, 0:1024])
                e.drain()
                e.sem_inc(p.sems["ev"], 1)
            blk.vector(s_kvacc)
            p.cnt["ev"] += 1
            p.barrier()

    def s_kvc(e):
        p.inc(e.dma_start(out=E["kv_snd"][:, :], in_=KVACC[:, :]), "st", 16)
        p.w(e, "st")
        p.inc(e.collective_compute(
            "AllReduce", OP.add,
            replica_groups=[[0, 1], [2, 3], [4, 5], [6, 7]],
            ins=[E["kv_snd"].opt()], outs=[E["kv_rcv"].opt()],
        ), "cc", 1)
        p.w(e, "cc")
        p.inc(e.dma_start(out=KVS[:, :], in_=E["kv_rcv"][:, :]), "ld", 16)
    blk.gpsimd(s_kvc)
    p.barrier()

    def s_kvb(e):
        p.w(e, "ld")
        e.tensor_copy(XLN[:, 0:2048], KVS[:, :])
        e.drain()
        e.sem_inc(p.sems["ev"], 1)
    blk.vector(s_kvb)
    p.cnt["ev"] += 1

    def s_kvs(e):
        p.w(e, "ev")
        p.inc(e.dma_start(out=E["kvb_d"][:, :], in_=XLN[:, 0:2048]), "st", 16)
    blk.gpsimd(s_kvs)
    p.barrier()
    pk.close()

    if hasattr(p, 'marks'): p.marks.append(('l1.K', nc.n_instructions()))
    qa = Scope(nc)
    WQ = qa.sb("WQ", [128, G * 1024], BF16)
    WO1 = qa.sb("WO1", [128, G * 1024], BF16)
    PROJ2 = qa.sb("PROJ2", [128, 128], BF16)
    KVB = qa.sb("KVB", [128, 2048], BF16)
    XLN2 = qa.sb("XLN2", [128, G * 512], BF16)
    QF = qa.sb("QF", [128, G * 512], BF16)
    QP = qa.sb("QP", [128, 2 * 512], BF16)
    ATT = qa.sb("ATT", [128, G * 512], BF16)
    XR = qa.sb("XR", [128, G * 512])
    TMP1q = qa.sb("TMP1q", [128, 512]); TMP2q = qa.sb("TMP2q", [128, 512])
    bqc = qa.sb("bqc", [128, G]); boc = qa.sb("boc", [128, G])
    PSq = qa.ps("PSq", [128, 1024])
    PSat = qa.ps("PSat", [128, 1024])

    def s_wq(e):
        for kk in range(G):
            p.inc(e.dma_start(out=WQ[:, kk * 1024:(kk + 1) * 1024],
                              in_=E["a_wq"][kk * 128:(kk + 1) * 128, :]), "ld", 16)
            p.inc(e.dma_start(out=WO1[:, kk * 1024:(kk + 1) * 1024],
                              in_=E["a_wo"][kk * 128:(kk + 1) * 128, :]), "ld", 16)
            p.inc(e.dma_start(out=bqc[:, kk:kk + 1], in_=E["a_bq"][kk * 128:(kk + 1) * 128, :]), "ld", 16)
            p.inc(e.dma_start(out=boc[:, kk:kk + 1], in_=E["a_bo"][kk * 128:(kk + 1) * 128, :]), "ld", 16)
        p.inc(e.dma_start(out=PROJ2[0:64, :], in_=E["a_proj"][:, :]), "ld", 16)
        p.inc(e.dma_start(out=PROJ2[64:128, :], in_=E["a_proj"][:, :]), "ld", 16)
        p.inc(e.dma_start(out=KVB[:, :], in_=E["kvb_d"][:, :]), "ld", 16)
    blk.gpsimd(s_wq)
    p.barrier()

    for tt in range(NT):
        c0 = tt * TT

        def s_lx(e, c0=c0):
            p.w(e, "st")
            for kk in range(G):
                p.inc(e.dma_start(out=XLN2[:, kk * 512:(kk + 1) * 512],
                                  in_=E["x2ln_d"][kk * 128:(kk + 1) * 128, c0:c0 + 512]), "ld", 16)
                p.inc(e.dma_start(out=XR[:, kk * 512:(kk + 1) * 512],
                                  in_=E["x2_d"][kk * 128:(kk + 1) * 128, c0:c0 + 512]), "ld", 16)
        blk.sync(s_lx)

        qast = {}
        for oc in range(G):
            def s_qmm(e, oc=oc):
                if oc == 0:
                    p.w(e, "ld")
                    p.w(e, "ev")
                if ("qe", oc - 2) in qast:
                    e.wait_ge(p.sems["ev"], qast[("qe", oc - 2)])
                for kk in range(G):
                    ins = nc.tensor.matmul(PSq[:, (oc % 2) * 512:(oc % 2) * 512 + 512],
                                           WQ[:, kk * 1024 + oc * 128:kk * 1024 + (oc + 1) * 128],
                                           XLN2[:, kk * 512:(kk + 1) * 512],
                                           start=(kk == 0), stop=(kk == G - 1))
                    if kk == G - 1:
                        p.inc(ins, "mm", 1)
                        qast[("qm", oc)] = p.cnt["mm"]
            blk.tensor(s_qmm)

            def s_qev(e, oc=oc):
                e.wait_ge(p.sems["mm"], qast[("qm", oc)])
                e.tensor_scalar(QF[:, oc * 512:(oc + 1) * 512], PSq[:, (oc % 2) * 512:(oc % 2) * 512 + 512],
                                bqc[:, oc:oc + 1], None, OP.add)
                e.drain()
                e.sem_inc(p.sems["ev"], 1)
                p.cnt["ev"] += 1
                qast[("qe", oc)] = p.cnt["ev"]
            blk.vector(s_qev)

        for h in range(16):
            def s_pf(e, h=h):
                if h == 0:
                    p.w(e, "ev")
                if ("tg", h - 1) in qast:
                    e.wait_ge(p.sems["ev"], qast[("tg", h - 1)])
                if ("ae", h - 4) in qast:
                    e.wait_ge(p.sems["ev"], qast[("ae", h - 4)])
                b0 = (h % 2) * 64
                p.inc(nc.tensor.matmul(PSq[:, 0:512],
                                       PROJ2[b0:b0 + 64, :],
                                       QF[b0:b0 + 64, (h // 2) * 512:(h // 2 + 1) * 512],
                                       start=True, stop=True), "mm", 1)
            blk.tensor(s_pf)

            emit_trig(p, u, PSq[:, 0:512], 0, QP, 0, 512, 512, TMP1q, TMP2q)
            qast[("tg", h)] = p.cnt["ev"]

            def s_att(e, h=h):
                p.w(e, "ex")
                for mc in range(2):
                    ins = nc.tensor.matmul(PSat[((h % 2)) * 64:((h % 2)) * 64 + 64,
                                                ((h // 2) % 2) * 512:((h // 2) % 2) * 512 + 512],
                                           KVB[:, h * 128 + mc * 64:h * 128 + (mc + 1) * 64],
                                           QP[:, mc * 512:(mc + 1) * 512],
                                           start=(mc == 0), stop=(mc == 1),
                                           tile_position=(0, (h % 2) * 64))
                    if mc == 1:
                        p.inc(ins, "mm", 1)
                        qast[("at", h)] = p.cnt["mm"]
            blk.tensor(s_att)

            if h % 4 == 3:
                def s_atev(e, h=h):
                    e.wait_ge(p.sems["mm"], qast[("at", h)])
                    pair0 = h // 2 - 1
                    pair1 = h // 2
                    e.tensor_scalar(ATT[:, pair0 * 512:pair0 * 512 + 512],
                                    PSat[:, 0:512], 1.0 / 128.0, None, OP.mult)
                    e.drain()
                    e.tensor_scalar(ATT[:, pair1 * 512:pair1 * 512 + 512],
                                    PSat[:, 512:1024], 1.0 / 128.0, None, OP.mult)
                    e.drain()
                    e.sem_inc(p.sems["ev"], 1)
                    p.cnt["ev"] += 1
                    qast[("ae", h)] = p.cnt["ev"]
                blk.vector(s_atev)

        for oc in range(G):
            def s_womm(e, oc=oc):
                if oc == 0:
                    p.w(e, "ev")
                if ("re", oc - 2) in qast:
                    e.wait_ge(p.sems["ev"], qast[("re", oc - 2)])
                for kk in range(G):
                    ins = nc.tensor.matmul(PSq[:, (oc % 2) * 512:(oc % 2) * 512 + 512],
                                           WO1[:, kk * 1024 + oc * 128:kk * 1024 + (oc + 1) * 128],
                                           ATT[:, kk * 512:(kk + 1) * 512],
                                           start=(kk == 0), stop=(kk == G - 1))
                    if kk == G - 1:
                        p.inc(ins, "mm", 1)
                        qast[("wm", oc)] = p.cnt["mm"]
            blk.tensor(s_womm)

            def s_res(e, oc=oc):
                e.wait_ge(p.sems["mm"], qast[("wm", oc)])
                e.scalar_tensor_tensor(out=XR[:, oc * 512:(oc + 1) * 512],
                                       in0=PSq[:, (oc % 2) * 512:(oc % 2) * 512 + 512],
                                       scalar=boc[:, oc:oc + 1], in1=XR[:, oc * 512:(oc + 1) * 512],
                                       op0=OP.add, op1=OP.add)
                e.drain()
                e.sem_inc(p.sems["ev"], 1)
                p.cnt["ev"] += 1
                qast[("re", oc)] = p.cnt["ev"]
            blk.vector(s_res)

        def s_sx(e, c0=c0):
            p.w(e, "ev")
            for kk in range(G):
                p.inc(e.dma_start(out=E["xq_d"][kk * 128:(kk + 1) * 128, c0:c0 + 512],
                                  in_=XR[:, kk * 512:(kk + 1) * 512]), "st", 16)
        blk.sync(s_sx)
        p.barrier()

        if tt == 0 and "dbgatt" in E.get("_taps", {}):
            def g_da(e):
                p.inc(e.dma_start(out=E["_taps"]["dbgatt"][:, :], in_=ATT[:, :]), "st", 16)
                p.inc(e.dma_start(out=E["_taps"]["dbgqf"][:, :], in_=QF[:, :]), "st", 16)
                p.inc(e.dma_start(out=E["_taps"]["dbgqp"][:, :], in_=QP[:, :]), "st", 16)
            blk.gpsimd(g_da)
            p.barrier()
    qa.close()

    if hasattr(p, 'marks'): p.marks.append(('l1.QA', nc.n_instructions()))
    qf = Scope(nc)
    W1L = qf.sb("W1L", [128, G * 4096], BF16)
    W2L = qf.sb("W2L", [128, 32 * 1024], BF16)
    XT2 = qf.sb("XT2", [128, G * 512])
    XB2 = qf.sb("XB2", [128, G * 512], BF16)
    HH2 = qf.sb("HH2", [128, 32 * 512], BF16)
    SQ2 = qf.sb("SQ2", [128, 512])
    fr1b = qf.sb("fr1b", [1, 512]); fr2b = qf.sb("fr2b", [1, 512])
    g2row = qf.sb("g2row", [1, D]); b2row = qf.sb("b2row", [1, D])
    b1c = qf.sb("b1c", [128, 32]); b2c2 = qf.sb("b2c2", [128, G])
    PSc = qf.ps("PSc", [128, 1024])
    PSd = qf.ps("PSd", [128, 1024])

    def s_wf(e):
        for kk in range(G):
            p.inc(e.dma_start(out=W1L[:, kk * 4096:(kk + 1) * 4096],
                              in_=E["a_w1"][kk * 128:(kk + 1) * 128, :]), "ld", 16)
        for hc in range(32):
            p.inc(e.dma_start(out=W2L[:, hc * 1024:(hc + 1) * 1024],
                              in_=E["a_w2"][hc * 128:(hc + 1) * 128, :]), "ld", 16)
            p.inc(e.dma_start(out=b1c[:, hc:hc + 1], in_=E["a_b1"][hc * 128:(hc + 1) * 128, :]), "ld", 16)
        for kk in range(G):
            p.inc(e.dma_start(out=b2c2[:, kk:kk + 1], in_=E["a_b2"][kk * 128:(kk + 1) * 128, :]), "ld", 16)
    blk.gpsimd(s_wf)

    def s_row2(e):
        p.inc(e.dma_start(out=g2row[:, :], in_=E["a_ln2_g"][:, :]), "ld", 16)
        p.inc(e.dma_start(out=b2row[:, :], in_=E["a_ln2_b"][:, :]), "ld", 16)
    blk.sync(s_row2)
    p.barrier()

    for tt in range(NT):
        c0 = tt * TT

        def s_lx(e, c0=c0):
            p.w(e, "st")
            for kk in range(G):
                p.inc(e.dma_start(out=XT2[:, kk * 512:(kk + 1) * 512],
                                  in_=E["xq_d"][kk * 128:(kk + 1) * 128, c0:c0 + 512]), "ld", 16)
        blk.sync(s_lx)

        def s_wt(e):
            p.w(e, "ld")
            e.engine_nop()
            e.drain()
            e.sem_inc(p.sems["ev"], 1)
        blk.vector(s_wt)
        p.cnt["ev"] += 1
        p.barrier()

        emit_ln_sb(p, u, XT2, 512, XB2, g2row, b2row, fr1b, fr2b, SQ2, u.PS0[:, 512:1024], u.PS1[:, 512:1024])

        qfst = {}
        for hc in range(32):
            def s_w1(e, hc=hc):
                if hc == 0:
                    p.w(e, "ev")
                if ("rl", hc - 2) in qfst:
                    e.wait_ge(p.sems["ex"], qfst[("rl", hc - 2)])
                for kk in range(G):
                    ins = nc.tensor.matmul(PSc[:, (hc % 2) * 512:(hc % 2) * 512 + 512],
                                           W1L[:, kk * 4096 + hc * 128:kk * 4096 + (hc + 1) * 128],
                                           XB2[:, kk * 512:(kk + 1) * 512],
                                           start=(kk == 0), stop=(kk == G - 1))
                    if kk == G - 1:
                        p.inc(ins, "mm", 1)
                        qfst[("w1", hc)] = p.cnt["mm"]
            blk.tensor(s_w1)

            def s_rl(e, hc=hc):
                e.wait_ge(p.sems["mm"], qfst[("w1", hc)])
                e.activation(out=HH2[:, hc * 512:(hc + 1) * 512],
                             in_=PSc[:, (hc % 2) * 512:(hc % 2) * 512 + 512],
                             func=AF.Relu, bias=b1c[:, hc:hc + 1], scale=1.0)
                e.drain()
                e.sem_inc(p.sems["ex"], 1)
                p.cnt["ex"] += 1
                qfst[("rl", hc)] = p.cnt["ex"]
            blk.scalar(s_rl)

        for oc in range(G):
            def s_w2(e, oc=oc):
                if oc == 0:
                    p.w(e, "ex")
                if ("o", oc - 2) in qfst:
                    e.wait_ge(p.sems["ev"], qfst[("o", oc - 2)])
                for hc in range(32):
                    ins = nc.tensor.matmul(PSd[:, (oc % 2) * 512:(oc % 2) * 512 + 512],
                                           W2L[:, hc * 1024 + oc * 128:hc * 1024 + (oc + 1) * 128],
                                           HH2[:, hc * 512:(hc + 1) * 512],
                                           start=(hc == 0), stop=(hc == 31))
                    if hc == 31:
                        p.inc(ins, "mm", 1)
                        qfst[("w2", oc)] = p.cnt["mm"]
            blk.tensor(s_w2)

            def s_o(e, oc=oc):
                e.wait_ge(p.sems["mm"], qfst[("w2", oc)])
                e.scalar_tensor_tensor(out=XT2[:, oc * 512:(oc + 1) * 512],
                                       in0=PSd[:, (oc % 2) * 512:(oc % 2) * 512 + 512],
                                       scalar=b2c2[:, oc:oc + 1], in1=XT2[:, oc * 512:(oc + 1) * 512],
                                       op0=OP.add, op1=OP.add)
                e.drain()
                e.sem_inc(p.sems["ev"], 1)
                p.cnt["ev"] += 1
                qfst[("o", oc)] = p.cnt["ev"]
            blk.vector(s_o)

        def s_sx(e, c0=c0):
            p.w(e, "ev")
            for kk in range(G):
                p.inc(e.dma_start(out=E["x3_d"][kk * 128:(kk + 1) * 128, c0:c0 + 512],
                                  in_=XT2[:, kk * 512:(kk + 1) * 512]), "st", 16)
        blk.sync(s_sx)
        p.barrier()
    qf.close()


def build_l1f(p, E, u, out_fm):
    nc, blk = p.nc, p.blk
    zf = Scope(nc)
    WS = zf.sb("WS", [128, G * 1024])
    SGN = zf.sb("SGN", [128, G * 1024], BF16)
    XT3 = zf.sb("XT3", [128, G * 512])
    XB3 = zf.sb("XB3", [128, G * 512], BF16)
    SQ3 = zf.sb("SQ3", [128, 512])
    fr1c = zf.sb("fr1c", [1, 512]); fr2c = zf.sb("fr2c", [1, 512])
    PSe = zf.ps("PSe", [128, 1024])
    PSf = zf.ps("PSf", [128, 1024])

    def s_w(e):
        for kk in range(G):
            p.inc(e.dma_start(out=WS[:, kk * 1024:(kk + 1) * 1024],
                              in_=E["z_wT"][kk * 128:(kk + 1) * 128, :]), "ld", 16)
    blk.sync(s_w)

    def s_sgn(e):
        p.w(e, "ld")
        e.sign(SGN[:, :], WS[:, :])
        e.drain()
        e.sem_inc(p.sems["ex"], 1)
    blk.scalar(s_sgn)
    p.cnt["ex"] += 1
    p.barrier()

    for tt in range(NT):
        c0 = tt * TT

        def s_lx(e, c0=c0):
            p.w(e, "st")
            for kk in range(G):
                p.inc(e.dma_start(out=XT3[:, kk * 512:(kk + 1) * 512],
                                  in_=E["x3_d"][kk * 128:(kk + 1) * 128, c0:c0 + 512]), "ld", 16)
        blk.sync(s_lx)

        def s_wt(e):
            p.w(e, "ld")
            e.engine_nop()
            e.drain()
            e.sem_inc(p.sems["ev"], 1)
        blk.vector(s_wt)
        p.cnt["ev"] += 1
        p.barrier()

        emit_ln_sb(p, u, XT3, 512, XB3, None, None, fr1c, fr2c, SQ3, u.PS0[:, 512:1024], u.PS1[:, 512:1024])

        for oc in range(G):
            def s_mm(e, oc=oc):
                if oc == 0:
                    p.w(e, "ev")
                for kk in range(G):
                    ins = nc.tensor.matmul(PSe[:, (oc % 2) * 512:(oc % 2) * 512 + 512],
                                           SGN[:, kk * 1024 + oc * 128:kk * 1024 + (oc + 1) * 128],
                                           XB3[:, kk * 512:(kk + 1) * 512],
                                           start=(kk == 0), stop=(kk == G - 1))
                    if kk == G - 1:
                        p.inc(ins, "mm", 1)
            blk.tensor(s_mm)

            def s_o(e, oc=oc):
                p.w(e, "mm")
                e.scalar_tensor_tensor(out=XT3[:, oc * 512:(oc + 1) * 512],
                                       in0=PSe[:, (oc % 2) * 512:(oc % 2) * 512 + 512],
                                       scalar=1.0, in1=XT3[:, oc * 512:(oc + 1) * 512],
                                       op0=OP.mult, op1=OP.add)
                e.drain()
                e.sem_inc(p.sems["ev"], 1)
            blk.vector(s_o)
            p.cnt["ev"] += 1
            p.barrier()

        def s_sx(e, c0=c0):
            for kk in range(G):
                p.inc(e.dma_start(out=out_fm[kk * 128:(kk + 1) * 128, c0:c0 + 512],
                                  in_=XT3[:, kk * 512:(kk + 1) * 512]), "st", 16)
        blk.sync(s_sx)
        p.barrier()
    zf.close()


def build_program(taps=()):
    p = Prog()
    nc = p.nc

    ein = lambda nm, shp, dt=F32: nc.dram_tensor(nm, shp, dt, kind="ExternalInput").ap()
    dscr = lambda nm, shp, dt=F32: nc.dram_tensor(nm, shp, dt).ap()

    E = {}
    E["xin"] = ein("xin", [D, XH + T])
    for nm in ("w_dt", "w_v", "w_vs", "w_gate", "hm_dense", "outproj"):
        E[nm] = ein(nm, [D, D])
    E["convw"] = ein("convw", [D, 4])
    for nm in ("convb", "nrmsc", "dtbias"):
        E[nm] = ein(nm, [D, 1])
    E["ropecos"] = ein("ropecos", [128, T])
    E["ropesin"] = ein("ropesin", [128, T])
    for nm in ("flg_take", "flg_send", "flg_negbig"):
        E[nm] = ein(nm, [128, 1])
    E["c_eblk"] = ein("c_eblk", [128, 2])
    E["c_eblkT"] = ein("c_eblkT", [2, 128])

    E["g_d"] = dscr("g_d", [D, T])
    E["vin_d"] = dscr("vin_d", [D, T])
    E["x1_d"] = dscr("x1_d", [D, T])
    E["xln_d"] = dscr("xln_d", [D, T])
    E["snd_d"] = dscr("snd_d", [128, 24])
    E["rcv_d"] = dscr("rcv_d", [128, 24])

    for l in range(2):
        pre = "f%d_" % l
        for nm, shp in (("wq", [D, 512]), ("wqs", [D, 512]), ("wk", [D, 512]),
                        ("wks", [D, 512]), ("wv", [D, 512]), ("wo", [512, D]),
                        ("w1a", [D, HIDP]), ("w1g", [D, HIDP]), ("w2", [HIDP, D]),
                        ("b1a", [HIDP, 1]), ("b1g", [HIDP, 1]), ("b2", [D, 1]),
                        ("lg", [1, D]), ("lb", [1, D])):
            E[pre + nm] = ein(pre + nm, shp)
    E["f_qscale"] = ein("f_qscale", [2, 64, 1])
    E["f_kscale"] = ein("f_kscale", [2, 64, 1])
    E["f_cq"] = ein("f_cq", [128, T], BF16)
    E["f_sq"] = ein("f_sq", [128, T], BF16)
    E["f_ck"] = ein("f_ck", [128, KS], BF16)
    E["f_sk"] = ein("f_sk", [128, KS], BF16)
    E["f_mask4"] = ein("f_mask4", [4, 128, 640])
    E["f_m0"] = ein("f_m0", [128, 128])
    E["f_m4"] = ein("f_m4", [128, 128])

    E["hx1_d"] = dscr("hx1_d", [D, KH])
    E["xi0_d"] = dscr("xi0_d", [D, 2 + KS])
    E["xi1_d"] = dscr("xi1_d", [D, 2 + KS])
    E["xi2_d"] = dscr("xi2_d", [D, T])
    E["xp0_d"] = dscr("xp0_d", [D, 1 + T])
    E["xp1_d"] = dscr("xp1_d", [D, 1 + T])
    E["x2_d"] = dscr("x2_d", [D, T])
    E["snd_h"] = dscr("snd_h", [D, KH])
    E["rcv_h"] = dscr("rcv_h", [D, KH])
    E["snd_s"] = dscr("snd_s", [D, 8])
    E["rcv_s"] = dscr("rcv_s", [D, 8])

    for nm, shp in (("a_wq", [D, D]), ("a_wk", [D, D]), ("a_wv", [D, D]), ("a_wo", [D, D]),
                    ("a_bq", [D, 1]), ("a_bk", [D, 1]), ("a_bo", [D, 1]), ("a_b2", [D, 1]),
                    ("a_bv", [1, D]), ("a_b1", [4096, 1]),
                    ("a_w1", [D, 4096]), ("a_w2", [4096, D]),
                    ("a_ln1_g", [1, D]), ("a_ln1_b", [1, D]), ("a_ln2_g", [1, D]), ("a_ln2_b", [1, D]),
                    ("a_proj", [64, 128]), ("z_wT", [D, D])):
        E[nm] = ein(nm, shp)

    E["x2ln_d"] = dscr("x2ln_d", [D, T], BF16)
    E["kv_snd"] = dscr("kv_snd", [128, 2048])
    E["kv_rcv"] = dscr("kv_rcv", [128, 2048])
    E["kvb_d"] = dscr("kvb_d", [128, 2048], BF16)
    E["xq_d"] = dscr("xq_d", [D, T])
    E["x3_d"] = dscr("x3_d", [D, T])

    out_fm = nc.dram_tensor("out_fm", [D, T], F32, kind="ExternalOutput").ap()
    tap_aps = {}
    _tshapes = {"dbgkrot": [128, 4 * KS], "dbgvt": [128, NKC * 512],
                "dbgqrot": [128, 4 * 512], "dbgo": [128, 4 * 512], "dbgp": [128, 2 * 640],
                "dbgatt": [128, G * 512], "dbgqf": [128, G * 512], "dbgqp": [128, 2 * 512],
                "dbgo2": [128, 4 * 512], "dbgxpr": [128, G * 512]}
    for tname in taps:
        shp = list(E[tname].tensor.shape) if tname in E else _tshapes.get(tname, [D, T])
        tap_aps[tname] = nc.dram_tensor("tap_" + tname, shp, F32, kind="ExternalOutput").ap()

    with nc.Block() as blk:
        p.blk = blk
        for s in ("ld", "mm", "ev", "ex", "st", "bar", "cc", "gp", "tp"):
            p.sem(s)

        E["_taps"] = tap_aps
        u = build_utils(p, E)
        p.marks = [("start", nc.n_instructions())]
        build_l0a(p, E, u)
        p.marks.append(("l0a", nc.n_instructions()))
        build_l0f(p, E, u)
        p.marks.append(("l0f", nc.n_instructions()))
        build_l1(p, E, u)
        p.marks.append(("l1a", nc.n_instructions()))
        build_l1f(p, E, u, out_fm)
        p.marks.append(("l1f", nc.n_instructions()))

        def s_out(e):
            for tname, ap in tap_aps.items():
                if tname in E:
                    p.inc(e.dma_start(out=ap[:, :], in_=E[tname][:, :]), "st", 16)
            p.w(e, "st")
        blk.gpsimd(s_out)

    return p, tap_aps


def kernel(**inputs):
    in_maps = prep_inputs(inputs)
    p, _ = build_program()
    res = run_bass_kernel_spmd(p.nc, in_maps, core_ids=list(range(8)))
    out = np.zeros((B, N, D), np.float32)
    for c in range(8):
        b, hf = c // 2, c % 2
        out[b, hf * T:(hf + 1) * T, :] = res.results[c]["out_fm"].T
    return out


# version 05 (final)
# speedup vs baseline: 1.1421x; 1.0054x over previous
import math
import numpy as np
import concourse.bass as bass
import concourse.mybir as mybir
from concourse.bass_utils import run_bass_kernel_spmd

F32 = mybir.dt.float32
BF16 = mybir.dt.bfloat16
AF = mybir.ActivationFunctionType
OP = mybir.AluOpType
AX = mybir.AxisListType

B, N, D = 4, 8192, 1024
T = 4096
XH = 4
CH = 256
NCH = T // CH
SEG = 1024
NSEG = T // SEG
TT = 512
NT = T // TT
G = 8
NEGBIG = -3.0e38
MASKV = -1.0e30
C2_SERIAL = False
C2D = 2
C2BAR = False

KH = 514
KS = 4608
NKC = KS // 128
HIDP = 2816
NHC = HIDP // 128


def _fm(a):
    return np.ascontiguousarray(np.asarray(a, dtype=np.float32).T)


def _xpos_tables(t0):
    import ml_dtypes
    half_idx = np.arange(0, 64, 2, dtype=np.float64)
    inv = 1.0 / 10000.0 ** (half_idx / 64.0)
    base = (half_idx + 0.4 * 64.0) / (1.4 * 64.0)

    def mk(tpos, kind):
        W = len(tpos)
        ct = np.zeros((128, W), np.float32)
        st = np.zeros((128, W), np.float32)
        power = (tpos - 4096.0) / 256.0
        for c in range(64):
            jj = c % 32
            ang = tpos * inv[jj]
            sc = base[jj] ** power
            s = sc if kind == "q" else 1.0 / sc
            cv = (np.cos(ang) * s).astype(np.float32)
            sv = (np.sin(ang) * s).astype(np.float32)
            sgn = -1.0 if c < 32 else 1.0
            for rep in range(2):
                ct[rep * 64 + c] = cv
                st[rep * 64 + c] = sgn * sv
        return ct.astype(ml_dtypes.bfloat16), st.astype(ml_dtypes.bfloat16)

    tq = np.arange(t0, t0 + T, dtype=np.float64)
    tk = np.arange(t0 - 512, t0 + T, dtype=np.float64)
    cq, sq = mk(tq, "q")
    ck, sk = mk(tk, "k")
    return cq, sq, ck, sk


def _mask4(hf):
    m = np.zeros((4, 128, 640), np.float32)
    for qt in range(4):
        for j in range(5):
            mchunk = qt + j - 4
            tkk = mchunk * 128 + np.arange(128)[None, :]
            tqq = qt * 128 + np.arange(128)[:, None]
            ok = (tkk <= tqq) & (tkk >= tqq - 512)
            if hf == 0:
                ok = ok & (tkk >= 0)
            m[qt, :, j * 128:(j + 1) * 128] = np.where(ok, 0.0, MASKV)
    return m


def _rope_tables(t0):
    tpos = np.arange(t0, t0 + T, dtype=np.float64)
    inv = 1.0 / 10000.0 ** (np.arange(0, 64, 2, dtype=np.float64) / 64.0)
    cosr = np.zeros((128, T), np.float32)
    sinr = np.zeros((128, T), np.float32)
    for c in range(64):
        jj = c % 32
        ang = tpos * inv[jj]
        cv = np.cos(ang).astype(np.float32)
        sv = np.sin(ang).astype(np.float32)
        sgn = -1.0 if c < 32 else 1.0
        for rep in range(2):
            cosr[rep * 64 + c] = cv
            sinr[rep * 64 + c] = sgn * sv
    return cosr, sinr


def prep_inputs(inputs):
    x = np.asarray(inputs["L0_attn_x"], dtype=np.float32)
    ipw = np.asarray(inputs["L0_attn_in_proj_w"], dtype=np.float32)
    w_dt = np.ascontiguousarray(ipw[:, 0:D])
    w_v = np.ascontiguousarray(ipw[:, D:2 * D])
    perm = np.array([(f // 64) * 64 + ((f % 64) + 32) % 64 for f in range(D)])
    w_vs = np.ascontiguousarray(w_v[:, perm])
    w_gate = np.ascontiguousarray(ipw[:, 2 * D:3 * D])

    hm = np.asarray(inputs["L0_attn_head_mix"], dtype=np.float32)
    HM = np.zeros((D, D), np.float32)
    HM4 = HM.reshape(16, 64, 16, 64)
    for c in range(64):
        HM4[:, c, :, c] = hm

    common = {
        "w_dt": w_dt, "w_v": w_v, "w_vs": w_vs, "w_gate": w_gate,
        "hm_dense": HM,
        "outproj": np.ascontiguousarray(np.asarray(inputs["L0_attn_out_proj_w"], dtype=np.float32)),
        "convw": np.ascontiguousarray(np.asarray(inputs["L0_attn_conv_w"], dtype=np.float32)[:, 0, :]),
        "convb": np.asarray(inputs["L0_attn_conv_b"], dtype=np.float32).reshape(D, 1),
        "nrmsc": np.asarray(inputs["L0_attn_norm_scale"], dtype=np.float32).reshape(D, 1),
        "dtbias": np.asarray(inputs["L0_attn_dt_bias"], dtype=np.float32).reshape(D, 1),
    }

    wqkv = np.asarray(inputs["L0_ffn_wqkv"], dtype=np.float32)
    permI = np.array([(f // 64) * 64 + ((f % 64) + 32) % 64 for f in range(512)])
    for l in range(2):
        common[f"f{l}_wq"] = np.ascontiguousarray(wqkv[l][:, 0:512])
        common[f"f{l}_wqs"] = np.ascontiguousarray(wqkv[l][:, 0:512][:, permI])
        common[f"f{l}_wk"] = np.ascontiguousarray(wqkv[l][:, 512:1024])
        common[f"f{l}_wks"] = np.ascontiguousarray(wqkv[l][:, 512:1024][:, permI])
        common[f"f{l}_wv"] = np.ascontiguousarray(wqkv[l][:, 1024:1536])
        common[f"f{l}_wo"] = np.ascontiguousarray(np.asarray(inputs["L0_ffn_wo"], dtype=np.float32)[l])
        w1 = np.asarray(inputs["L0_ffn_w1"], dtype=np.float32)[l]
        w1a = np.zeros((D, HIDP), np.float32); w1a[:, :2730] = w1[:, :2730]
        w1g = np.zeros((D, HIDP), np.float32); w1g[:, :2730] = w1[:, 2730:]
        common[f"f{l}_w1a"] = w1a
        common[f"f{l}_w1g"] = w1g
        b1 = np.asarray(inputs["L0_ffn_b1"], dtype=np.float32)[l]
        b1a = np.zeros((HIDP, 1), np.float32); b1a[:2730, 0] = b1[:2730]
        b1g = np.zeros((HIDP, 1), np.float32); b1g[:2730, 0] = b1[2730:]
        common[f"f{l}_b1a"] = b1a
        common[f"f{l}_b1g"] = b1g
        w2 = np.asarray(inputs["L0_ffn_w2"], dtype=np.float32)[l]
        w2p = np.zeros((HIDP, D), np.float32); w2p[:2730] = w2
        common[f"f{l}_w2"] = w2p
        common[f"f{l}_b2"] = np.asarray(inputs["L0_ffn_b2"], dtype=np.float32)[l].reshape(D, 1)
        common[f"f{l}_lg"] = np.asarray(inputs["L0_ffn_ln_g"], dtype=np.float32)[l].reshape(1, D)
        common[f"f{l}_lb"] = np.asarray(inputs["L0_ffn_ln_b"], dtype=np.float32)[l].reshape(1, D)
    common["f_qscale"] = np.ascontiguousarray(np.asarray(inputs["L0_ffn_q_scale"], dtype=np.float32).reshape(2, 64, 1))
    common["f_kscale"] = np.ascontiguousarray(np.asarray(inputs["L0_ffn_k_scale"], dtype=np.float32).reshape(2, 64, 1))

    pp = np.arange(128)[:, None]
    cc_ = np.arange(128)[None, :]
    common["f_m0"] = np.where(cc_ >= pp, 0.0, MASKV).astype(np.float32)
    common["f_m4"] = np.where(cc_ <= pp, 0.0, MASKV).astype(np.float32)

    for nm in ("wq", "wk", "wv", "wo"):
        common["a_" + nm] = np.ascontiguousarray(np.asarray(inputs["L1_attn_" + nm], dtype=np.float32))
    for nm in ("bq", "bk", "bo", "b2"):
        common["a_" + nm] = np.asarray(inputs["L1_attn_" + nm], dtype=np.float32).reshape(D, 1)
    common["a_bv"] = np.asarray(inputs["L1_attn_bv"], dtype=np.float32).reshape(1, D)
    common["a_b1"] = np.asarray(inputs["L1_attn_b1"], dtype=np.float32).reshape(4096, 1)
    common["a_w1"] = np.ascontiguousarray(np.asarray(inputs["L1_attn_w1"], dtype=np.float32))
    common["a_w2"] = np.ascontiguousarray(np.asarray(inputs["L1_attn_w2"], dtype=np.float32))
    for nm in ("ln1_g", "ln1_b", "ln2_g", "ln2_b"):
        common["a_" + nm] = np.asarray(inputs["L1_attn_" + nm], dtype=np.float32).reshape(1, D)
    common["a_proj"] = np.ascontiguousarray(np.asarray(inputs["L1_attn_proj_mat"], dtype=np.float32))
    common["z_wT"] = np.ascontiguousarray(np.asarray(inputs["L1_ffn_weight"], dtype=np.float32).T)

    in_maps = []
    for c in range(8):
        b, hf = c // 2, c % 2
        t0 = hf * T
        shard = _fm(x[b, t0:t0 + T, :])
        halo = _fm(x[b, t0 - XH:t0, :]) if hf else np.zeros((D, XH), np.float32)
        cosr, sinr = _rope_tables(t0)
        m = dict(common)
        m["xin"] = np.ascontiguousarray(np.concatenate([halo, shard], axis=1))
        m["ropecos"] = cosr
        m["ropesin"] = sinr
        eb = np.zeros((128, 2), np.float32); eb[0:64, 0] = 1.0; eb[64:128, 1] = 1.0
        ebT = np.zeros((2, 128), np.float32); ebT[0, 0:64] = 1.0; ebT[1, 64:128] = 1.0
        m["c_eblk"] = eb; m["c_eblkT"] = ebT
        m["flg_take"] = np.full((128, 1), float(hf), np.float32)
        m["flg_send"] = np.full((128, 1), float(1 - hf), np.float32)
        m["flg_negbig"] = np.full((128, 1), float((1 - hf) * NEGBIG), np.float32)
        cq, sq, ck, sk = _xpos_tables(t0)
        m["f_cq"] = cq; m["f_sq"] = sq; m["f_ck"] = ck; m["f_sk"] = sk
        m["f_mask4"] = _mask4(hf)
        in_maps.append(m)
    return in_maps


class Scope:
    _n = [0]

    def __init__(self, nc):
        self.nc = nc
        self.guards = []

    def _u(self, nm):
        Scope._n[0] += 1
        return "%s_u%d" % (nm, Scope._n[0])

    def sb(self, nm, shp, dt=F32):
        g = self.nc.sbuf_tensor(self._u(nm), shp, dt)
        t = g.__enter__()
        self.guards.append(g)
        return t

    def ps(self, nm, shp, dt=F32):
        g = self.nc.psum_tensor(self._u(nm), shp, dt)
        t = g.__enter__()
        self.guards.append(g)
        return t

    def close(self):
        for g in reversed(self.guards):
            g.__exit__(None, None, None)
        self.guards = []


class Prog:
    def __init__(self):
        self.nc = bass.Bass("TRN2", target_bir_lowering=False, debug=False)
        self.sems = {}
        self.cnt = {}
        self.blk = None
        self._bar = 0

    def sem(self, name):
        h = self.nc.semaphore(name).__enter__()
        self.sems[name] = h
        self.cnt[name] = 0

    def inc(self, instr, name, k=1):
        instr.then_inc(self.sems[name], k)
        self.cnt[name] += k

    def w(self, eng, name, val=None):
        eng.wait_ge(self.sems[name], self.cnt[name] if val is None else val)

    def barrier(self):
        self._bar += 1
        tgt = 5 * self._bar
        bs = self.sems["bar"]

        def mk(has_drain):
            def f(e):
                if has_drain:
                    e.drain()
                e.sem_inc(bs, 1)
                e.wait_ge(bs, tgt)
            return f
        self.blk.sync(mk(False))
        self.blk.tensor(mk(True))
        self.blk.vector(mk(True))
        self.blk.scalar(mk(True))
        self.blk.gpsimd(mk(True))
        self.cnt["bar"] = tgt


class U:
    pass


def build_utils(p, E):
    nc, blk = p.nc, p.blk
    u = U()
    sc = Scope(nc)
    u.scope = sc
    u.ones_col = sc.sb("ones_col", [128, 1])
    u.ones_row = sc.sb("ones_row", [1, 128])
    u.onesN = sc.sb("onesN", [1, 512])
    u.neg20w = sc.sb("neg20w", [128, CH])
    u.ftk = sc.sb("ftk", [128, 1])
    u.fsd = sc.sb("fsd", [128, 1])
    u.fnb = sc.sb("fnb", [128, 1])
    u.ident = sc.sb("ident", [128, 128], BF16)
    u.eblk = sc.sb("eblk", [128, 2])
    u.eblkT = sc.sb("eblkT", [2, 128])
    u.iota_f = sc.sb("iota_f", [128, 128])
    u.iota_p = sc.sb("iota_p", [128, 1])
    u.PS0 = sc.ps("PS0", [128, 1024])
    u.PS1 = sc.ps("PS1", [128, 1024])

    def sec_sync(e):
        p.inc(e.dma_start(out=u.ftk[:, :], in_=E["flg_take"][:, :]), "ld", 16)
        p.inc(e.dma_start(out=u.fsd[:, :], in_=E["flg_send"][:, :]), "ld", 16)
        p.inc(e.dma_start(out=u.fnb[:, :], in_=E["flg_negbig"][:, :]), "ld", 16)
        p.inc(e.dma_start(out=u.eblk[:, :], in_=E["c_eblk"][:, :]), "ld", 16)
        p.inc(e.dma_start(out=u.eblkT[:, :], in_=E["c_eblkT"][:, :]), "ld", 16)
    blk.sync(sec_sync)

    def sec_gp(e):
        e.iota(u.iota_f[:, :], pattern=[[1, 128]], base=0, channel_multiplier=0,
               allow_small_or_imprecise_dtypes=True)
        e.iota(u.iota_p[:, :], pattern=[[0, 1]], base=0, channel_multiplier=1,
               allow_small_or_imprecise_dtypes=True)
        e.drain()
        e.sem_inc(p.sems["gp"], 1)
    blk.gpsimd(sec_gp)
    p.cnt["gp"] += 1

    def sec_v(e):
        e.memset(u.ones_col[:, :], 1.0)
        e.memset(u.ones_row[:, :], 1.0)
        e.memset(u.onesN[:, :], 1.0)
        e.memset(u.neg20w[:, :], -20.0)
        p.w(e, "gp")
        e.drain()
        e.tensor_scalar(u.ident[:, :], u.iota_f[:, :], u.iota_p[:, 0:1], 1.0, OP.is_equal, OP.mult)
        e.drain()
    blk.vector(sec_v)

    p.barrier()
    return u


def build_l0a(p, E, u):
    nc, blk = p.nc, p.blk
    xin = E["xin"]
    ones_col, ones_row = u.ones_col, u.ones_row
    neg20w, ftk, fsd, fnb = u.neg20w, u.ftk, u.fsd, u.fnb
    PS0, PS1 = u.PS0, u.PS1

    per = Scope(nc)
    xc_sb = per.sb("xc_sb", [128, G * T], BF16)
    fin_sb = per.sb("fin_sb", [128, G * T], BF16)
    convb_sb = per.sb("convb_sb", [128, G])
    nrmsc_sb = per.sb("nrmsc_sb", [128, G])
    dtb_sb = per.sb("dtb_sb", [128, G])
    convw_sb = per.sb("convw_sb", [128, 4 * G])

    def sec_sync(e):
        for g in range(G):
            p.inc(e.dma_start(out=convb_sb[:, g:g + 1], in_=E["convb"][g * 128:(g + 1) * 128, :]), "ld", 16)
            p.inc(e.dma_start(out=nrmsc_sb[:, g:g + 1], in_=E["nrmsc"][g * 128:(g + 1) * 128, :]), "ld", 16)
            p.inc(e.dma_start(out=dtb_sb[:, g:g + 1], in_=E["dtbias"][g * 128:(g + 1) * 128, :]), "ld", 16)
            p.inc(e.dma_start(out=convw_sb[:, 4 * g:4 * g + 4], in_=E["convw"][g * 128:(g + 1) * 128, :]), "ld", 16)
    blk.sync(sec_sync)

    p.barrier()

    Wd = TT + XH
    b1 = Scope(nc)
    BA = b1.sb("BA", [128, G * Wd])
    BB = b1.sb("BB", [128, G * Wd])
    BC = b1.sb("BC", [128, G * Wd])
    rstat = b1.sb("rstat", [1, Wd])
    rstat2 = b1.sb("rstat2", [1, Wd])
    PS2 = b1.ps("PS2b1", [128, 1024])
    PS3 = b1.ps("PS3b1", [128, 1024])

    def bsl(buf, g, wid=Wd):
        return buf[:, g * Wd:g * Wd + wid]

    for tt in range(NT):
        c0 = tt * TT

        def s_load(e, c0=c0):
            for g in range(G):
                p.inc(e.dma_start(out=bsl(BA, g), in_=xin[g * 128:(g + 1) * 128, c0:c0 + Wd]), "ld", 16)
        blk.sync(s_load)

        def s_sq(e):
            p.w(e, "ld")
            e.square(BB[:, :], BA[:, :])
            e.drain()
            e.sem_inc(p.sems["ex"], 1)
        blk.scalar(s_sq)
        p.cnt["ex"] += 1

        def s_mm1(e):
            p.w(e, "ex")
            for (a, b2) in ((0, 512), (512, Wd)):
                for g in range(G):
                    ins = nc.tensor.matmul(PS0[0:1, a:b2], ones_col[:, :], BB[:, g * Wd + a:g * Wd + b2],
                                           start=(g == 0), stop=(g == G - 1))
                    if g == G - 1:
                        p.inc(ins, "mm", 1)
            for (a, b2) in ((0, 512), (512, Wd)):
                for g in range(G):
                    ins = nc.tensor.matmul(PS1[0:1, a:b2], ones_col[:, :], BA[:, g * Wd + a:g * Wd + b2],
                                           start=(g == 0), stop=(g == G - 1))
                    if g == G - 1:
                        p.inc(ins, "mm", 1)
        blk.tensor(s_mm1)

        def s_stat(e):
            p.w(e, "mm")
            e.tensor_scalar(rstat2[0:1, 0:Wd], PS1[0:1, 0:Wd], 1.0 / 1024.0, None, OP.mult)
            e.drain()
            e.tensor_mul(rstat[0:1, 0:Wd], rstat2[0:1, 0:Wd], rstat2[0:1, 0:Wd])
            e.drain()
            e.scalar_tensor_tensor(out=rstat[0:1, 0:Wd], in0=PS0[0:1, 0:Wd], scalar=1.0 / 1024.0,
                                   in1=rstat[0:1, 0:Wd], op0=OP.mult, op1=OP.subtract)
            e.drain()
            e.tensor_scalar(rstat[0:1, 0:Wd], rstat[0:1, 0:Wd], 1e-5, None, OP.add)
            e.drain()
            e.reciprocal(rstat[0:1, 0:Wd], rstat[0:1, 0:Wd])
            e.drain()
            e.sem_inc(p.sems["ev"], 1)
        blk.vector(s_stat)
        p.cnt["ev"] += 1

        def s_sqrt(e):
            p.w(e, "ev")
            e.sqrt(rstat[0:1, 0:Wd], rstat[0:1, 0:Wd])
            e.drain()
            e.sem_inc(p.sems["ex"], 1)
        blk.scalar(s_sqrt)
        p.cnt["ex"] += 1

        def s_bneg(e):
            p.w(e, "ex")
            e.scalar_tensor_tensor(out=rstat2[0:1, 0:Wd], in0=rstat2[0:1, 0:Wd], scalar=-1.0,
                                   in1=rstat[0:1, 0:Wd], op0=OP.mult, op1=OP.mult)
            e.drain()
            e.sem_inc(p.sems["ev"], 1)
        blk.vector(s_bneg)
        p.cnt["ev"] += 1

        def s_mm2(e):
            p.w(e, "ev")
            for (a, b2) in ((0, 512), (512, Wd)):
                p.inc(nc.tensor.matmul(PS2[:, a:b2], ones_row[:, :], rstat[0:1, a:b2],
                                       start=True, stop=True), "mm", 1)
                p.inc(nc.tensor.matmul(PS3[:, a:b2], ones_row[:, :], rstat2[0:1, a:b2],
                                       start=True, stop=True), "mm", 1)
        blk.tensor(s_mm2)

        def s_xln(e, c0=c0):
            p.w(e, "mm")
            for g in range(G):
                e.scalar_tensor_tensor(out=bsl(BB, g), in0=bsl(BA, g), scalar=1.0,
                                       in1=PS2[:, 0:Wd], op0=OP.mult, op1=OP.mult)
            e.drain()
            for g in range(G):
                e.tensor_add(bsl(BB, g), bsl(BB, g), PS3[:, 0:Wd])
            e.drain()
            e.sem_inc(p.sems["ev"], 1)
        blk.vector(s_xln)
        p.cnt["ev"] += 1

        def s_stln(e, c0=c0):
            p.w(e, "ev")
            for g in range(G):
                p.inc(e.dma_start(out=E["xln_d"][g * 128:(g + 1) * 128, c0:c0 + TT],
                                  in_=BB[:, g * Wd + XH:g * Wd + XH + TT]), "st", 16)
        blk.sync(s_stln)

        def s_sq2(e):
            p.w(e, "ev")
            e.square(BA[:, :], BB[:, :])
            e.drain()
            e.sem_inc(p.sems["ex"], 1)
        blk.scalar(s_sq2)
        p.cnt["ex"] += 1

        def s_mm3(e):
            p.w(e, "ex")
            for (a, b2) in ((0, 512), (512, Wd)):
                for g in range(G):
                    ins = nc.tensor.matmul(PS0[0:1, a:b2], ones_col[:, :], BA[:, g * Wd + a:g * Wd + b2],
                                           start=(g == 0), stop=(g == G - 1))
                    if g == G - 1:
                        p.inc(ins, "mm", 1)
        blk.tensor(s_mm3)

        def s_stat2(e):
            p.w(e, "mm")
            e.tensor_scalar(rstat[0:1, 0:Wd], PS0[0:1, 0:Wd], 1.0 / 1024.0, 1e-6, OP.mult, OP.add)
            e.drain()
            e.reciprocal(rstat[0:1, 0:Wd], rstat[0:1, 0:Wd])
            e.drain()
            e.sem_inc(p.sems["ev"], 1)
        blk.vector(s_stat2)
        p.cnt["ev"] += 1

        def s_sqrt2(e):
            p.w(e, "ev")
            e.sqrt(rstat[0:1, 0:Wd], rstat[0:1, 0:Wd])
            e.drain()
            e.sem_inc(p.sems["ex"], 1)
        blk.scalar(s_sqrt2)
        p.cnt["ex"] += 1

        def s_mm4(e):
            p.w(e, "ex")
            for (a, b2) in ((0, 512), (512, Wd)):
                p.inc(nc.tensor.matmul(PS2[:, a:b2], ones_row[:, :], rstat[0:1, a:b2],
                                       start=True, stop=True), "mm", 1)
        blk.tensor(s_mm4)

        def s_xn(e):
            p.w(e, "mm")
            p.w(e, "st")
            for g in range(G):
                e.scalar_tensor_tensor(out=bsl(BC, g), in0=bsl(BB, g),
                                       scalar=nrmsc_sb[:, g:g + 1], in1=PS2[:, 0:Wd],
                                       op0=OP.mult, op1=OP.mult)
            e.drain()
            for g in range(G):
                e.tensor_scalar(bsl(BC, g), bsl(BC, g), 60000.0, -60000.0, OP.min, OP.max)
            e.drain()
            for g in range(G):
                e.tensor_scalar(bsl(BA, g, TT), BC[:, g * Wd + XH:g * Wd + XH + TT],
                                convw_sb[:, 4 * g + 3:4 * g + 4], None, OP.mult)
            e.drain()
            cur = BA
            for k in (2, 1, 0):
                nxt = BB if cur is BA else BA
                off = XH - 3 + k
                for g in range(G):
                    e.scalar_tensor_tensor(out=bsl(nxt, g, TT),
                                           in0=BC[:, g * Wd + off:g * Wd + off + TT],
                                           scalar=convw_sb[:, 4 * g + k:4 * g + k + 1],
                                           in1=bsl(cur, g, TT), op0=OP.mult, op1=OP.add)
                e.drain()
                cur = nxt
            e.sem_inc(p.sems["ev"], 1)
        blk.vector(s_xn)
        p.cnt["ev"] += 1

        def s_silu(e, tt=tt):
            p.w(e, "ev")
            for g in range(G):
                e.activation(out=xc_sb[:, g * T + tt * TT:g * T + (tt + 1) * TT],
                             in_=bsl(BB, g, TT), func=AF.Silu,
                             bias=convb_sb[:, g:g + 1], scale=1.0)
            e.drain()
            e.sem_inc(p.sems["ex"], 1)
        blk.scalar(s_silu)
        p.cnt["ex"] += 1

        p.barrier()

    b1.close()

    b2 = Scope(nc)
    cos_sb = b2.sb("cos_sb", [128, T])
    sin_sb = b2.sb("sin_sb", [128, T])
    W3 = b2.sb("W3", [128, G * 384], BF16)
    SA = b2.sb("SA", [128, SEG]); SB = b2.sb("SB", [128, SEG]); SC = b2.sb("SC", [128, SEG])
    SD = b2.sb("SD", [128, SEG]); SE = b2.sb("SE", [128, SEG])
    bdec = b2.sb("bdec", [128, G * NCH]); bout = b2.sb("bout", [128, G * NCH])
    st_a = b2.sb("st_a", [128, NCH]); st_b = b2.sb("st_b", [128, NCH])
    st_c = b2.sb("st_c", [128, NCH]); st_d = b2.sb("st_d", [128, NCH])
    st_e = b2.sb("st_e", [128, NCH])
    sv1 = b2.sb("sv1", [128, 1]); sv2 = b2.sb("sv2", [128, 1]); sv3 = b2.sb("sv3", [128, 1])
    snd_sb = b2.sb("snd_sb", [128, 24]); rcv_sb = b2.sb("rcv_sb", [128, 24])
    cc_sb = b2.sb("cc_sb", [128, G * NCH])
    PS2 = b2.ps("PS2b2", [128, 1024])

    def s_tbl(e):
        p.inc(e.dma_start(out=cos_sb[:, :], in_=E["ropecos"][:, :]), "ld", 16)
        p.inc(e.dma_start(out=sin_sb[:, :], in_=E["ropesin"][:, :]), "ld", 16)
    blk.sync(s_tbl)

    for g in range(G):
        def s_w3(e, g=g):
            for kk in range(G):
                for j, w in enumerate((E["w_dt"], E["w_v"], E["w_vs"])):
                    p.inc(e.dma_start(out=W3[:, kk * 384 + j * 128:kk * 384 + (j + 1) * 128],
                                      in_=w[kk * 128:(kk + 1) * 128, g * 128:(g + 1) * 128]), "ld", 16)
        blk.gpsimd(s_w3)

        for sg in range(NSEG):
            seg0 = sg * SEG

            def s_mm(e, g=g, seg0=seg0):
                p.w(e, "ld")
                PSL = [(PS0, 0), (PS0, 512), (PS1, 0), (PS1, 512), (PS2, 0), (PS2, 512)]
                for j in range(3):
                    for t2 in range(2):
                        pb, po = PSL[j * 2 + t2]
                        for kk in range(G):
                            ins = nc.tensor.matmul(
                                pb[:, po:po + 512],
                                W3[:, kk * 384 + j * 128:kk * 384 + (j + 1) * 128],
                                xc_sb[:, kk * T + seg0 + t2 * 512:kk * T + seg0 + (t2 + 1) * 512],
                                start=(kk == 0), stop=(kk == G - 1))
                            if kk == G - 1:
                                p.inc(ins, "mm", 1)
            blk.tensor(s_mm)
            mmn = p.cnt["mm"]

            def s_ev1(e, g=g, mmn=mmn):
                e.wait_ge(p.sems["mm"], mmn - 4)
                e.tensor_scalar(SA[:, 0:512], PS0[:, 0:512], dtb_sb[:, g:g + 1], 30.0, OP.add, OP.min)
                e.tensor_scalar(SA[:, 512:1024], PS0[:, 512:1024], dtb_sb[:, g:g + 1], 30.0, OP.add, OP.min)
                e.drain()
                e.sem_inc(p.sems["ev"], 1)
                e.wait_ge(p.sems["mm"], mmn - 2)
                e.tensor_copy(SB[:, 0:512], PS1[:, 0:512])
                e.tensor_copy(SB[:, 512:1024], PS1[:, 512:1024])
                e.drain()
                e.wait_ge(p.sems["mm"], mmn)
                e.tensor_copy(SC[:, 0:512], PS2[:, 0:512])
                e.tensor_copy(SC[:, 512:1024], PS2[:, 512:1024])
                e.drain()
                e.sem_inc(p.sems["ev"], 1)
            blk.vector(s_ev1)
            p.cnt["ev"] += 2
            evb = p.cnt["ev"]

            def s_sp(e, evb=evb):
                e.wait_ge(p.sems["ev"], evb - 1)
                e.activation(out=SD[:, :], in_=SA[:, :], func=AF.Exp)
                e.drain()
                e.activation(out=SA[:, :], in_=SD[:, :], func=AF.Ln, bias=1.0)
                e.drain()
                e.sem_inc(p.sems["ex"], 1)
            blk.scalar(s_sp)
            p.cnt["ex"] += 1

            def s_scan(e, g=g, sg=sg, seg0=seg0):
                p.w(e, "ex")
                e.tensor_scalar(SA[:, :], SA[:, :], 2.0, 0.001, OP.min, OP.max)
                e.drain()
                e.tensor_mul(SD[:, :], SB[:, :], cos_sb[:, seg0:seg0 + SEG])
                e.drain()
                e.tensor_mul(SE[:, :], SC[:, :], sin_sb[:, seg0:seg0 + SEG])
                e.drain()
                e.tensor_add(SC[:, :], SD[:, :], SE[:, :])
                e.drain()
                e.tensor_mul(SE[:, :], SC[:, :], SA[:, :])
                e.drain()
                e.tensor_scalar(SD[:, :], SA[:, :], -1.0, None, OP.mult)
                e.drain()
                for cc in range(SEG // CH):
                    e.tensor_tensor_scan(out=SC[:, cc * CH:(cc + 1) * CH],
                                         data0=SD[:, cc * CH:(cc + 1) * CH],
                                         data1=neg20w[:, :], initial=0.0,
                                         op0=OP.add, op1=OP.max)
                e.drain()
                e.tensor_copy(bdec[:, g * NCH + sg * 4:g * NCH + sg * 4 + 4], SC[:, CH - 1::CH])
                for cc in range(SEG // CH):
                    a = cc * CH
                    e.tensor_copy(SD[:, a:a + 1], SC[:, a:a + 1])
                    e.tensor_sub(SD[:, a + 1:a + CH], SC[:, a + 1:a + CH], SC[:, a:a + CH - 1])
                e.drain()
                e.sem_inc(p.sems["ev"], 1)
            blk.vector(s_scan)
            p.cnt["ev"] += 1

            def s_expg(e):
                p.w(e, "ev")
                e.activation(out=SA[:, :], in_=SD[:, :], func=AF.Exp)
                e.drain()
                e.sem_inc(p.sems["ex"], 1)
            blk.scalar(s_expg)
            p.cnt["ex"] += 1

            def s_h0(e, g=g, sg=sg):
                p.w(e, "ex")
                for cc in range(SEG // CH):
                    a = cc * CH
                    e.tensor_tensor_scan(out=SB[:, a:a + CH], data0=SA[:, a:a + CH],
                                         data1=SE[:, a:a + CH], initial=0.0,
                                         op0=OP.mult, op1=OP.add)
                e.drain()
                e.tensor_copy(bout[:, g * NCH + sg * 4:g * NCH + sg * 4 + 4], SB[:, CH - 1::CH])
                e.drain()
                e.sem_inc(p.sems["ev"], 1)
            blk.vector(s_h0)
            p.cnt["ev"] += 1

            def s_store(e, g=g, seg0=seg0):
                p.w(e, "ev")
                p.inc(e.dma_start(out=E["g_d"][g * 128:(g + 1) * 128, seg0:seg0 + SEG], in_=SA[:, :]), "st", 16)
                p.inc(e.dma_start(out=E["vin_d"][g * 128:(g + 1) * 128, seg0:seg0 + SEG], in_=SE[:, :]), "st", 16)
            blk.sync(s_store)

            p.barrier()

    if hasattr(p, 'marks'): p.marks.append(('l0a.b2a', nc.n_instructions()))
    for g in range(G):
        def v_stats(e, g=g):
            sl = slice(g * NCH, (g + 1) * NCH)
            e.tensor_tensor_scan(out=st_a[:, :], data0=bdec[:, sl], data1=neg20w[:, 0:NCH],
                                 initial=0.0, op0=OP.add, op1=OP.bypass)
            e.drain()
            e.tensor_scalar(st_b[:, :], st_a[:, :], -80.0, 0.0, OP.max, OP.min)
            e.drain()
            e.reduce_max(sv1[:, :], st_b[:, :], axis=AX.X)
            e.drain()
            e.tensor_scalar(st_c[:, :], st_b[:, :], sv1[:, 0:1], -20.0, OP.subtract, OP.max)
            e.drain()
            e.tensor_scalar(st_c[:, :], st_c[:, :], 0.0, None, OP.min)
            e.drain()
            e.sem_inc(p.sems["ev"], 1)
        blk.vector(v_stats)
        p.cnt["ev"] += 1

        def s_eseed(e):
            p.w(e, "ev")
            e.activation(out=st_d[:, :], in_=st_c[:, :], func=AF.Exp, scale=-1.0)
            e.drain()
            e.sem_inc(p.sems["ex"], 1)
        blk.scalar(s_eseed)
        p.cnt["ex"] += 1

        def v_send(e, g=g):
            p.w(e, "ex")
            sl = slice(g * NCH, (g + 1) * NCH)
            e.tensor_mul(st_d[:, :], bout[:, sl], st_d[:, :])
            e.drain()
            e.reduce_sum(sv2[:, :], st_d[:, :], axis=AX.X)
            e.drain()
            e.tensor_scalar(snd_sb[:, 3 * g:3 * g + 1], st_a[:, NCH - 1:NCH], fsd[:, 0:1], None, OP.mult)
            e.tensor_scalar(snd_sb[:, 3 * g + 1:3 * g + 2], sv1[:, 0:1], fsd[:, 0:1], None, OP.mult)
            e.tensor_scalar(snd_sb[:, 3 * g + 2:3 * g + 3], sv2[:, 0:1], fsd[:, 0:1], None, OP.mult)
            e.drain()
            e.sem_inc(p.sems["ev"], 1)
        blk.vector(v_send)
        p.cnt["ev"] += 1

    def g_coll(e):
        p.w(e, "ev")
        p.inc(e.dma_start(out=E["snd_d"][:, :], in_=snd_sb[:, :]), "st", 16)
        p.w(e, "st")
        p.inc(e.collective_compute(
            "AllReduce", OP.add,
            replica_groups=[[0, 1], [2, 3], [4, 5], [6, 7]],
            ins=[E["snd_d"].opt()],
            outs=[E["rcv_d"].opt()],
        ), "cc", 1)
        p.w(e, "cc")
        p.inc(e.dma_start(out=rcv_sb[:, :], in_=E["rcv_d"][:, :]), "ld", 16)
    blk.gpsimd(g_coll)

    p.barrier()

    for g in range(G):
        def v_cc1(e, g=g):
            sl = slice(g * NCH, (g + 1) * NCH)
            p.w(e, "ld")
            e.tensor_scalar(sv1[:, :], rcv_sb[:, 3 * g:3 * g + 1], ftk[:, 0:1], None, OP.mult)
            e.tensor_scalar(sv2[:, :], rcv_sb[:, 3 * g + 1:3 * g + 2], ftk[:, 0:1], fnb[:, 0:1], OP.mult, OP.add)
            e.drain()
            e.tensor_tensor_scan(out=st_a[:, :], data0=bdec[:, sl], data1=neg20w[:, 0:NCH],
                                 initial=0.0, op0=OP.add, op1=OP.bypass)
            e.drain()
            e.tensor_scalar(st_b[:, :], st_a[:, :], sv1[:, 0:1], -80.0, OP.add, OP.max)
            e.drain()
            e.tensor_scalar(st_b[:, :], st_b[:, :], 0.0, None, OP.min)
            e.drain()
            e.reduce_max(sv3[:, :], st_b[:, :], axis=AX.X)
            e.drain()
            e.tensor_max(sv3[:, :], sv3[:, :], sv2[:, :])
            e.drain()
            e.tensor_scalar(st_c[:, :], st_b[:, :], sv3[:, 0:1], -20.0, OP.subtract, OP.max)
            e.drain()
            e.tensor_scalar(st_c[:, :], st_c[:, :], 0.0, None, OP.min)
            e.drain()
            e.sem_inc(p.sems["ev"], 1)
        blk.vector(v_cc1)
        p.cnt["ev"] += 1

        def s_cc1(e):
            p.w(e, "ev")
            e.activation(out=st_d[:, :], in_=st_c[:, :], func=AF.Exp, scale=-1.0)
            e.drain()
            e.activation(out=st_e[:, :], in_=st_c[:, :], func=AF.Exp)
            e.drain()
            e.sem_inc(p.sems["ex"], 1)
        blk.scalar(s_cc1)
        p.cnt["ex"] += 1

        def v_cc2(e, g=g):
            p.w(e, "ex")
            sl = slice(g * NCH, (g + 1) * NCH)
            e.tensor_mul(st_d[:, :], bout[:, sl], st_d[:, :])
            e.drain()
            e.tensor_scalar(st_a[:, 0:1], rcv_sb[:, 3 * g + 2:3 * g + 3], ftk[:, 0:1], None, OP.mult)
            e.tensor_copy(st_a[:, 1:NCH], st_d[:, 0:NCH - 1])
            e.drain()
            e.tensor_tensor_scan(out=st_b[:, :], data0=st_a[:, :], data1=neg20w[:, 0:NCH],
                                 initial=0.0, op0=OP.add, op1=OP.bypass)
            e.drain()
            e.tensor_mul(cc_sb[:, sl], st_b[:, :], st_e[:, :])
            e.drain()
            e.sem_inc(p.sems["ev"], 1)
        blk.vector(v_cc2)
        p.cnt["ev"] += 1

    p.barrier()

    for g in range(G):
        for sg in range(NSEG):
            seg0 = sg * SEG

            def s_ld2(e, g=g, seg0=seg0):
                p.inc(e.dma_start(out=SA[:, :], in_=E["g_d"][g * 128:(g + 1) * 128, seg0:seg0 + SEG]), "ld", 16)
                p.inc(e.dma_start(out=SE[:, :], in_=E["vin_d"][g * 128:(g + 1) * 128, seg0:seg0 + SEG]), "ld", 16)
            blk.sync(s_ld2)

            def v_hs(e, g=g, sg=sg, seg0=seg0):
                p.w(e, "ld")
                for cc in range(SEG // CH):
                    a = cc * CH
                    ci = g * NCH + sg * 4 + cc
                    e.tensor_tensor_scan(out=SB[:, a:a + CH], data0=SA[:, a:a + CH],
                                         data1=SE[:, a:a + CH], initial=cc_sb[:, ci:ci + 1],
                                         op0=OP.mult, op1=OP.add)
                e.drain()
                e.tensor_copy(fin_sb[:, g * T + seg0:g * T + seg0 + SEG], SB[:, :])
                e.drain()
                e.sem_inc(p.sems["ev"], 1)
            blk.vector(v_hs)
            p.cnt["ev"] += 1

            p.barrier()

    tap_aps = E.get("_taps", {})
    if "cc" in tap_aps:
        def g_tapcc(e):
            p.inc(e.dma_start(out=tap_aps["cc"][0:128, 0:G * NCH], in_=cc_sb[:, :]), "st", 16)
            p.inc(e.dma_start(out=tap_aps["cc"][128:256, 0:G * NCH], in_=bdec[:, :]), "st", 16)
            p.inc(e.dma_start(out=tap_aps["cc"][256:384, 0:G * NCH], in_=bout[:, :]), "st", 16)
            p.inc(e.dma_start(out=tap_aps["cc"][384:512, 0:24], in_=rcv_sb[:, :]), "st", 16)
            p.w(e, "st")
        blk.gpsimd(g_tapcc)
        p.barrier()

    b2.close()

    if hasattr(p, 'marks'): p.marks.append(('l0a.b2b', nc.n_instructions()))
    T3 = 256
    NT3 = T // T3
    b3 = Scope(nc)
    HMW = b3.sb("HMW", [128, G * D], BF16)
    WGT = b3.sb("WGT", [128, G * D], BF16)
    WOP = b3.sb("WOP", [128, G * D], BF16)
    gt_sb = b3.sb("gt_sb", [128, T3], BF16)
    mg_sb = b3.sb("mg_sb", [128, G * T3], BF16)
    xr_sb = b3.sb("xr_sb", [128, G * T3])
    xl_sb = b3.sb("xl_sb", [128, G * T3])
    PS2 = b3.ps("PS2b3", [128, 1024])

    def g_wb3(e):
        for g in range(G):
            p.inc(e.dma_start(out=HMW[:, g * D:(g + 1) * D], in_=E["hm_dense"][g * 128:(g + 1) * 128, :]), "ld", 16)
            p.inc(e.dma_start(out=WGT[:, g * D:(g + 1) * D], in_=E["w_gate"][g * 128:(g + 1) * 128, :]), "ld", 16)
            p.inc(e.dma_start(out=WOP[:, g * D:(g + 1) * D], in_=E["outproj"][g * 128:(g + 1) * 128, :]), "ld", 16)
    blk.gpsimd(g_wb3)
    p.barrier()

    for tt in range(NT3):
        c0 = tt * T3

        def s_ldx(e, c0=c0):
            p.w(e, "st")
            for g in range(G):
                p.inc(e.dma_start(out=xr_sb[:, g * T3:(g + 1) * T3],
                                  in_=xin[g * 128:(g + 1) * 128, XH + c0:XH + c0 + T3]), "ld", 16)
                p.inc(e.dma_start(out=xl_sb[:, g * T3:(g + 1) * T3],
                                  in_=E["xln_d"][g * 128:(g + 1) * 128, c0:c0 + T3]), "ld", 16)
        blk.sync(s_ldx)

        def v_pre(e):
            p.w(e, "ld")
            for g in range(G):
                e.tensor_add(xr_sb[:, g * T3:(g + 1) * T3], xr_sb[:, g * T3:(g + 1) * T3],
                             xl_sb[:, g * T3:(g + 1) * T3])
            e.drain()
            e.sem_inc(p.sems["ev"], 1)
        blk.vector(v_pre)
        p.cnt["ev"] += 1

        for oc in range(G):
            def s_mm3(e, oc=oc, c0=c0):
                p.w(e, "ld")
                for kk in range(G):
                    ins = nc.tensor.matmul(PS0[:, 0:T3], HMW[:, kk * D + oc * 128:kk * D + (oc + 1) * 128],
                                           fin_sb[:, kk * T + c0:kk * T + c0 + T3],
                                           start=(kk == 0), stop=(kk == G - 1))
                    if kk == G - 1:
                        p.inc(ins, "mm", 1)
                for kk in range(G):
                    ins = nc.tensor.matmul(PS1[:, 0:T3], WGT[:, kk * D + oc * 128:kk * D + (oc + 1) * 128],
                                           xc_sb[:, kk * T + c0:kk * T + c0 + T3],
                                           start=(kk == 0), stop=(kk == G - 1))
                    if kk == G - 1:
                        p.inc(ins, "mm", 1)
            blk.tensor(s_mm3)

            def s_sig(e):
                p.w(e, "mm")
                e.activation(out=gt_sb[:, :], in_=PS1[:, 0:T3], func=AF.Sigmoid)
                e.drain()
                e.sem_inc(p.sems["ex"], 1)
            blk.scalar(s_sig)
            p.cnt["ex"] += 1

            def s_mg(e, oc=oc):
                p.w(e, "ex")
                e.scalar_tensor_tensor(out=mg_sb[:, oc * T3:(oc + 1) * T3], in0=PS0[:, 0:T3],
                                       scalar=1.0, in1=gt_sb[:, :], op0=OP.mult, op1=OP.mult)
                e.drain()
                e.sem_inc(p.sems["ev"], 1)
            blk.vector(s_mg)
            p.cnt["ev"] += 1

            p.barrier()

        for oc in range(G):
            def s_mm4(e, oc=oc):
                for kk in range(G):
                    ins = nc.tensor.matmul(PS2[:, 0:T3], WOP[:, kk * D + oc * 128:kk * D + (oc + 1) * 128],
                                           mg_sb[:, kk * T3:(kk + 1) * T3],
                                           start=(kk == 0), stop=(kk == G - 1))
                    if kk == G - 1:
                        p.inc(ins, "mm", 1)
            blk.tensor(s_mm4)

            def s_res(e, oc=oc):
                p.w(e, "mm")
                e.scalar_tensor_tensor(out=xr_sb[:, oc * T3:(oc + 1) * T3], in0=PS2[:, 0:T3],
                                       scalar=1.0, in1=xr_sb[:, oc * T3:(oc + 1) * T3],
                                       op0=OP.mult, op1=OP.add)
                e.drain()
                e.sem_inc(p.sems["ev"], 1)
            blk.vector(s_res)
            p.cnt["ev"] += 1

            p.barrier()

        def s_stx(e, c0=c0):
            for g in range(G):
                p.inc(e.dma_start(out=E["x1_d"][g * 128:(g + 1) * 128, c0:c0 + T3],
                                  in_=xr_sb[:, g * T3:(g + 1) * T3]), "st", 16)
        blk.sync(s_stx)

        p.barrier()

    b3.close()

    tap_aps = E.get("_taps", {})
    if tap_aps:
        def g_taps(e):
            for nm, src in (("xc", xc_sb), ("fin", fin_sb)):
                if nm in tap_aps:
                    for g in range(G):
                        p.inc(e.dma_start(out=tap_aps[nm][g * 128:(g + 1) * 128, :],
                                          in_=src[:, g * T:(g + 1) * T]), "st", 16)
        blk.gpsimd(g_taps)
        p.barrier()
    per.close()


def emit_ln_job(p, u, LA, LB, r1, r2, src, scol, dst, dcol, w, PSa=None, PSb=None):
    nc, blk = p.nc, p.blk

    def s_load(e):
        for g in range(G):
            p.inc(e.dma_start(out=LA[:, g * 512:g * 512 + w], in_=src[g * 128:(g + 1) * 128, scol:scol + w]), "ld", 16)
    blk.sync(s_load)

    def s_sq(e):
        p.w(e, "ld")
        for g in range(G):
            e.square(LB[:, g * 512:g * 512 + w], LA[:, g * 512:g * 512 + w])
        e.drain()
        e.sem_inc(p.sems["ex"], 1)
    blk.scalar(s_sq)
    p.cnt["ex"] += 1

    def s_mm(e):
        p.w(e, "ex")
        for g in range(G):
            ins = nc.tensor.matmul(u.PS0[0:1, 0:w], u.ones_col[:, :], LB[:, g * 512:g * 512 + w],
                                   start=(g == 0), stop=(g == G - 1))
            if g == G - 1:
                p.inc(ins, "mm", 1)
        for g in range(G):
            ins = nc.tensor.matmul(u.PS1[0:1, 0:w], u.ones_col[:, :], LA[:, g * 512:g * 512 + w],
                                   start=(g == 0), stop=(g == G - 1))
            if g == G - 1:
                p.inc(ins, "mm", 1)
    blk.tensor(s_mm)

    def s_stat(e):
        p.w(e, "mm")
        e.tensor_scalar(r2[0:1, 0:w], u.PS1[0:1, 0:w], 1.0 / 1024.0, None, OP.mult)
        e.drain()
        e.tensor_mul(r1[0:1, 0:w], r2[0:1, 0:w], r2[0:1, 0:w])
        e.drain()
        e.scalar_tensor_tensor(out=r1[0:1, 0:w], in0=u.PS0[0:1, 0:w], scalar=1.0 / 1024.0,
                               in1=r1[0:1, 0:w], op0=OP.mult, op1=OP.subtract)
        e.drain()
        e.tensor_scalar(r1[0:1, 0:w], r1[0:1, 0:w], 1e-5, None, OP.add)
        e.drain()
        e.reciprocal(r1[0:1, 0:w], r1[0:1, 0:w])
        e.drain()
        e.sem_inc(p.sems["ev"], 1)
    blk.vector(s_stat)
    p.cnt["ev"] += 1

    def s_sqrt(e):
        p.w(e, "ev")
        e.sqrt(r1[0:1, 0:w], r1[0:1, 0:w])
        e.drain()
        e.sem_inc(p.sems["ex"], 1)
    blk.scalar(s_sqrt)
    p.cnt["ex"] += 1

    def s_b(e):
        p.w(e, "ex")
        e.scalar_tensor_tensor(out=r2[0:1, 0:w], in0=r2[0:1, 0:w], scalar=-1.0,
                               in1=r1[0:1, 0:w], op0=OP.mult, op1=OP.mult)
        e.drain()
        e.sem_inc(p.sems["ev"], 1)
    blk.vector(s_b)
    p.cnt["ev"] += 1

    def s_bc(e):
        p.w(e, "ev")
        p.inc(nc.tensor.matmul(PSa[:, 0:w], u.ones_row[:, :], r1[0:1, 0:w], start=True, stop=True), "mm", 1)
        p.inc(nc.tensor.matmul(PSb[:, 0:w], u.ones_row[:, :], r2[0:1, 0:w], start=True, stop=True), "mm", 1)
    blk.tensor(s_bc)

    def s_app(e):
        p.w(e, "mm")
        for g in range(G):
            e.scalar_tensor_tensor(out=LB[:, g * 512:g * 512 + w], in0=LA[:, g * 512:g * 512 + w],
                                   scalar=1.0, in1=PSa[:, 0:w], op0=OP.mult, op1=OP.mult)
        e.drain()
        for g in range(G):
            e.tensor_add(LB[:, g * 512:g * 512 + w], LB[:, g * 512:g * 512 + w], PSb[:, 0:w])
        e.drain()
        e.sem_inc(p.sems["ev"], 1)
    blk.vector(s_app)
    p.cnt["ev"] += 1

    def s_store(e):
        p.w(e, "ev")
        for g in range(G):
            p.inc(e.dma_start(out=dst[g * 128:(g + 1) * 128, dcol:dcol + w], in_=LB[:, g * 512:g * 512 + w]), "st", 16)
    blk.sync(s_store)
    p.barrier()


def emit_exchange(p, u, sc, src, scol, w, snd_d, rcv_d, dst, dcol, cols=None):
    nc, blk = p.nc, p.blk
    HB = sc.sb("HBx%d" % p.cnt["cc"], [128, G * w])

    def s_l(e):
        p.w(e, "st")
        for g in range(G):
            p.inc(e.dma_start(out=HB[:, g * w:(g + 1) * w], in_=src[g * 128:(g + 1) * 128, scol:scol + w]), "ld", 16)
    blk.sync(s_l)

    def s_m(e):
        p.w(e, "ld")
        e.tensor_scalar(HB[:, :], HB[:, :], u.fsd[:, 0:1], None, OP.mult)
        e.drain()
        e.sem_inc(p.sems["ev"], 1)
    blk.vector(s_m)
    p.cnt["ev"] += 1

    def s_c(e):
        p.w(e, "ev")
        for g in range(G):
            p.inc(e.dma_start(out=snd_d[g * 128:(g + 1) * 128, 0:w], in_=HB[:, g * w:(g + 1) * w]), "st", 16)
        p.w(e, "st")
        p.inc(e.collective_compute(
            "AllReduce", OP.add,
            replica_groups=[[0, 1], [2, 3], [4, 5], [6, 7]],
            ins=[snd_d.opt()], outs=[rcv_d.opt()],
        ), "cc", 1)
        p.w(e, "cc")
        for g in range(G):
            p.inc(e.dma_start(out=HB[:, g * w:(g + 1) * w], in_=rcv_d[g * 128:(g + 1) * 128, 0:w]), "ld", 16)
    blk.gpsimd(s_c)

    a, b2 = (0, w) if cols is None else cols

    def s_m2(e):
        p.w(e, "ld")
        e.tensor_scalar(HB[:, :], HB[:, :], u.ftk[:, 0:1], None, OP.mult)
        e.drain()
        e.sem_inc(p.sems["ev"], 1)
    blk.vector(s_m2)
    p.cnt["ev"] += 1

    def s_s(e):
        p.w(e, "ev")
        for g in range(G):
            p.inc(e.dma_start(out=dst[g * 128:(g + 1) * 128, dcol:dcol + (b2 - a)],
                              in_=HB[:, g * w + a:g * w + b2],
                              allow_slow_non_contiguous=True), "st", 16)
    blk.sync(s_s)
    p.barrier()


def build_l0f_layer(p, E, u, l, src, xpd, dst, dst_col0):
    nc, blk = p.nc, p.blk
    PS0, PS1 = u.PS0, u.PS1
    pre = "f%d_" % l

    at = Scope(nc)
    WQKV = at.sb("WQKV", [128, G * 2560], BF16)
    WO = at.sb("WO", [128, 4 * 1024], BF16)
    KROT = at.sb("KROT", [128, 4 * KS], BF16)
    VT = at.sb("VT", [128, NKC * 512], BF16)
    TBC = at.sb("TBC", [128, KS], BF16); TBS = at.sb("TBS", [128, KS], BF16)
    MSK4 = at.sb("MSK4", [128, 4 * 640])
    M0 = at.sb("M0", [128, 128]); M4 = at.sb("M4", [128, 128])
    XIT = at.sb("XIT", [128, G * 513], BF16)
    XPR = at.sb("XPR", [128, G * 512])
    RA = at.sb("RA", [128, 512]); RB = at.sb("RB", [128, 512])
    RC = at.sb("RC", [128, 512]); SQB = at.sb("SQB", [128, 512])
    NRM = at.sb("NRM", [2, 512])
    QROT = at.sb("QROT", [128, 4 * 512], BF16)
    P_sb = at.sb("P_sb", [128, 2 * 640], BF16)
    PT_sb = at.sb("PT_sb", [128, 2 * 640], BF16)
    rowm = at.sb("rowm", [128, 2])
    rsum = at.sb("rsum", [128, 4])
    rinv = at.sb("rinv", [128, 2])
    DIAG = at.sb("DIAG", [128, 2 * 128], BF16)
    o_sb = at.sb("o_sb", [128, 4 * 512], BF16)
    qsc = at.sb("qsc", [128, 1]); qscs = at.sb("qscs", [128, 1])
    ksc = at.sb("ksc", [128, 1]); kscs = at.sb("kscs", [128, 1])

    def s_w(e):
        for kk in range(G):
            for j, nm in enumerate(("wq", "wqs", "wk", "wks", "wv")):
                p.inc(e.dma_start(out=WQKV[:, kk * 2560 + j * 512:kk * 2560 + (j + 1) * 512],
                                  in_=E[pre + nm][kk * 128:(kk + 1) * 128, :]), "ld", 16)
        for c4 in range(4):
            p.inc(e.dma_start(out=WO[:, c4 * 1024:(c4 + 1) * 1024],
                              in_=E[pre + "wo"][c4 * 128:(c4 + 1) * 128, :]), "ld", 16)
        p.inc(e.dma_start(out=TBC[:, :], in_=E["f_ck"][:, :]), "ld", 16)
        p.inc(e.dma_start(out=TBS[:, :], in_=E["f_sk"][:, :]), "ld", 16)
        for qt in range(4):
            p.inc(e.dma_start(out=MSK4[:, qt * 640:(qt + 1) * 640],
                              in_=E["f_mask4"][qt, :, :]), "ld", 16)
        p.inc(e.dma_start(out=M0[:, :], in_=E["f_m0"][:, :]), "ld", 16)
        p.inc(e.dma_start(out=M4[:, :], in_=E["f_m4"][:, :]), "ld", 16)
        qsa = E["f_qscale"][l]
        ksa = E["f_kscale"][l]
        for rep in range(2):
            b0 = rep * 64
            p.inc(e.dma_start(out=qsc[b0:b0 + 64, :], in_=qsa[0:64, :]), "ld", 16)
            p.inc(e.dma_start(out=ksc[b0:b0 + 64, :], in_=ksa[0:64, :]), "ld", 16)
            p.inc(e.dma_start(out=qscs[b0:b0 + 32, :], in_=qsa[32:64, :]), "ld", 16)
            p.inc(e.dma_start(out=qscs[b0 + 32:b0 + 64, :], in_=qsa[0:32, :]), "ld", 16)
            p.inc(e.dma_start(out=kscs[b0:b0 + 32, :], in_=ksa[32:64, :]), "ld", 16)
            p.inc(e.dma_start(out=kscs[b0 + 32:b0 + 64, :], in_=ksa[0:32, :]), "ld", 16)
    blk.gpsimd(s_w)
    p.barrier()

    def shifted(col0, wid, kk):
        off = 1 if kk < 4 else 0
        return XIT[:, kk * 513 + col0 + off:kk * 513 + col0 + off + wid]

    def emit_qk_rot(kind, col0_xi, tok0_tbl, nq, dst_rot, dst_col):
        jW, jWs = (0, 1) if kind == "q" else (2, 3)
        CT, ST = TBC, TBS
        scl, scls = (qsc, qscs) if kind == "q" else (ksc, kscs)

        def s_ld(e):
            p.w(e, "st")
            for kk in range(G):
                p.inc(e.dma_start(out=XIT[:, kk * 513:(kk + 1) * 513],
                                  in_=src[kk * 128:(kk + 1) * 128, col0_xi - 1:col0_xi + 512]), "ld", 16)
        blk.gpsimd(s_ld)

        for qc in range(4):
            def s_mm(e, qc=qc):
                p.w(e, "ld")
                if qc == 0:
                    p.w(e, "ex")
                    p.w(e, "ev")
                for kk in range(G):
                    ins = nc.tensor.matmul(PS0[:, 0:512],
                                           WQKV[:, kk * 2560 + jW * 512 + qc * 128:kk * 2560 + jW * 512 + (qc + 1) * 128],
                                           shifted(0, 512, kk), start=(kk == 0), stop=(kk == G - 1))
                    if kk == G - 1:
                        p.inc(ins, "mm", 1)
                for kk in range(G):
                    ins = nc.tensor.matmul(PS0[:, 512:1024],
                                           WQKV[:, kk * 2560 + jWs * 512 + qc * 128:kk * 2560 + jWs * 512 + (qc + 1) * 128],
                                           shifted(0, 512, kk), start=(kk == 0), stop=(kk == G - 1))
                    if kk == G - 1:
                        p.inc(ins, "mm", 1)
            blk.tensor(s_mm)

            def s_sq(e):
                p.w(e, "mm")
                e.square(SQB[:, :], PS0[:, 0:512])
                e.drain()
                e.sem_inc(p.sems["ex"], 1)
            blk.scalar(s_sq)
            p.cnt["ex"] += 1

            def s_ev(e, qc=qc):
                p.w(e, "ex")
                e.tensor_scalar(RA[:, :], PS0[:, 0:512], scl[:, 0:1], None, OP.mult)
                e.tensor_scalar(RB[:, :], PS0[:, 512:1024], scls[:, 0:1], None, OP.mult)
                e.drain()
                e.sem_inc(p.sems["ev"], 1)
            blk.vector(s_ev)
            p.cnt["ev"] += 1

            def s_nrm(e, qc=qc):
                p.w(e, "ev")
                p.inc(nc.tensor.matmul(PS1[0:2, 0:512], u.eblk[:, :], SQB[:, :],
                                       start=True, stop=True), "mm", 1)
            blk.tensor(s_nrm)

            def s_n2(e):
                p.w(e, "mm")
                e.tensor_copy(NRM[0:2, 0:512], PS1[0:2, 0:512])
                e.drain()
                e.sem_inc(p.sems["ev"], 1)
            blk.vector(s_n2)
            p.cnt["ev"] += 1

            def s_n3(e):
                p.w(e, "ev")
                e.sqrt(NRM[0:2, 0:512], NRM[0:2, 0:512])
                e.drain()
                e.sem_inc(p.sems["ex"], 1)
            blk.scalar(s_n3)
            p.cnt["ex"] += 1

            def s_n4(e):
                p.w(e, "ex")
                e.tensor_scalar(NRM[0:2, 0:512], NRM[0:2, 0:512], 1e-12, None, OP.max)
                e.drain()
                e.reciprocal(NRM[0:2, 0:512], NRM[0:2, 0:512])
                e.drain()
                e.sem_inc(p.sems["ev"], 1)
            blk.vector(s_n4)
            p.cnt["ev"] += 1

            def s_bc(e):
                p.w(e, "ev")
                p.inc(nc.tensor.matmul(PS1[:, 512:1024], u.eblkT[:, :], NRM[0:2, 0:512],
                                       start=True, stop=True), "mm", 1)
            blk.tensor(s_bc)

            def s_rope(e, qc=qc):
                p.w(e, "mm")
                e.tensor_mul(RC[:, :], RA[:, :], CT[:, tok0_tbl:tok0_tbl + 512])
                e.drain()
                e.tensor_mul(RA[:, :], RB[:, :], ST[:, tok0_tbl:tok0_tbl + 512])
                e.drain()
                e.tensor_add(RC[:, :], RC[:, :], RA[:, :])
                e.drain()
                e.tensor_mul(dst_rot[:, dst_col + qc * (KS if kind == "k" else 512):
                                     dst_col + qc * (KS if kind == "k" else 512) + 512],
                             RC[:, :], PS1[:, 512:1024])
                e.drain()
                e.sem_inc(p.sems["ev"], 1)
            blk.vector(s_rope)
            p.cnt["ev"] += 1

            p.barrier()

    ab_sc = Scope(nc)
    PSV = ab_sc.ps("PSV", [128, 1024])
    for kt in range(9):
        col0 = 2 + 512 * kt
        emit_qk_rot("k", col0, 512 * kt, 512, KROT, 512 * kt)
        for vc in range(4):
            def s_vmm(e, kt=kt, vc=vc):
                for kk in range(G):
                    ins = nc.tensor.matmul(PSV[:, 0:512],
                                           shifted(vc * 128, 128, kk),
                                           WQKV[:, kk * 2560 + 4 * 512:kk * 2560 + 5 * 512],
                                           start=(kk == 0), stop=(kk == G - 1))
                    if kk == G - 1:
                        p.inc(ins, "mm", 1)
            blk.tensor(s_vmm)

            def s_vev(e, kt=kt, vc=vc):
                p.w(e, "mm")
                e.tensor_copy(VT[:, (4 * kt + vc) * 512:(4 * kt + vc + 1) * 512], PSV[:, 0:512])
                e.drain()
                e.sem_inc(p.sems["ev"], 1)
            blk.vector(s_vev)
            p.cnt["ev"] += 1

            p.barrier()

    ab_sc.close()

    tapd = E.get("_taps", {})
    if l == 0 and "dbgkrot" in tapd:
        def g_dk(e):
            p.inc(e.dma_start(out=tapd["dbgkrot"][:, :], in_=KROT[:, :]), "st", 16)
            p.inc(e.dma_start(out=tapd["dbgvt"][:, :], in_=VT[:, :]), "st", 16)
        blk.gpsimd(g_dk)
        p.barrier()

    if hasattr(p, 'marks'): p.marks.append(('f%d.keys' % l, nc.n_instructions()))
    c_sc = Scope(nc)
    PSPT = c_sc.ps("PSPT", [128, 1024])
    PSD = c_sc.ps("PSD", [128, 1024])
    def s_qtbl(e):
        p.inc(e.dma_start(out=TBC[:, 0:T], in_=E["f_cq"][:, :]), "ld", 16)
        p.inc(e.dma_start(out=TBS[:, 0:T], in_=E["f_sq"][:, :]), "ld", 16)
    blk.gpsimd(s_qtbl)
    p.barrier()

    for qg in range(8):
        emit_qk_rot("q", KH + 512 * qg, 512 * qg, 512, QROT, 0)

        if l == 0 and qg == 0 and "dbgqrot" in E.get("_taps", {}):
            def g_dq(e, qg=qg):
                p.inc(e.dma_start(out=E["_taps"]["dbgqrot"][:, :], in_=QROT[:, :]), "st", 16)
            blk.gpsimd(g_dq)
            p.barrier()

        sim_mm = {}; ev_max = {}; ex_exp = {}; ev_rinv = {}
        mm_tr = {}; ex_pt = {}; mm_pv = {}; ev_oev = {}
        SIMP = [PS0, PS1]

        def stage2(it, qg=qg):
            h = it % 8
            qtl = it // 8
            qt = qg * 4 + qtl
            pg = it // 2

            def t_tr(e, it=it, h=h):
                e.wait_ge(p.sems["ev"], ev_rinv[it])
                if it - 1 in ex_pt:
                    e.wait_ge(p.sems["ex"], ex_pt[it - 1])
                for j in range(5):
                    slot = PSPT[:, j * 128:(j + 1) * 128]
                    ins = nc.tensor.matmul(slot,
                                           P_sb[:, (it % 2) * 640 + j * 128:(it % 2) * 640 + (j + 1) * 128],
                                           DIAG[:, (it % 2) * 128:(it % 2 + 1) * 128],
                                           start=True, stop=True)
                    if j == 4:
                        p.inc(ins, "mm", 1)
                        mm_tr[it] = p.cnt["mm"]
            blk.tensor(t_tr)

            def a_ev(e, it=it):
                e.wait_ge(p.sems["mm"], mm_tr[it])
                e.activation(out=PT_sb[:, (it % 2) * 640:(it % 2) * 640 + 640],
                             in_=PSPT[:, 0:640], func=AF.Copy)
                e.drain()
                e.sem_inc(p.sems["ex"], 1)
                p.cnt["ex"] += 1
                ex_pt[it] = p.cnt["ex"]
            blk.scalar(a_ev)

            def t_pv(e, it=it, h=h, qt=qt, pg=pg):
                if pg - C2D in ev_oev:
                    e.wait_ge(p.sems["ev"], ev_oev[pg - C2D])
                e.wait_ge(p.sems["ex"], ex_pt[it])
                for j in range(5):
                    pd = PSD[(h % 2) * 64:(h % 2) * 64 + 64,
                             (pg % 2) * 512:(pg % 2) * 512 + 128]
                    ins = nc.tensor.matmul(pd, VT[:, (qt + j) * 512 + h * 64:(qt + j) * 512 + (h + 1) * 64],
                                           PT_sb[:, (it % 2) * 640 + j * 128:(it % 2) * 640 + (j + 1) * 128],
                                           start=(j == 0), stop=(j == 4),
                                           tile_position=(0, (h % 2) * 64))
                    if j == 4:
                        p.inc(ins, "mm", 1)
                        mm_pv[it] = p.cnt["mm"]
            blk.tensor(t_pv)

            if h % 2 == 1:
                def v_oev(e, it=it, h=h, pg=pg, qtl=qtl):
                    e.wait_ge(p.sems["mm"], mm_pv[it])
                    e.wait_ge(p.sems["mm"], mm_pv[it - 1])
                    pd = PSD[:, (pg % 2) * 512:(pg % 2) * 512 + 128]
                    e.tensor_copy(o_sb[:, (h // 2) * 512 + qtl * 128:(h // 2) * 512 + (qtl + 1) * 128], pd)
                    e.drain()
                    e.sem_inc(p.sems["ev"], 1)
                    p.cnt["ev"] += 1
                    ev_oev[pg] = p.cnt["ev"]
                blk.vector(v_oev)

        for it in range(32):
            h = it % 8
            qtl = it // 8
            qt = qg * 4 + qtl
            if C2BAR:
                p.barrier()
            if it >= 2:
                stage2(it - 2)
            if C2BAR:
                p.barrier()

            def t_sim(e, it=it, h=h, qt=qt, qtl=qtl):
                if it - C2D in ex_exp:
                    e.wait_ge(p.sems["ex"], ex_exp[it - C2D])
                lhs = QROT[(h % 2) * 64:(h % 2) * 64 + 64,
                           (h // 2) * 512 + qtl * 128:(h // 2) * 512 + (qtl + 1) * 128]
                rhsbase = (h // 2) * KS + 128 * qt
                pb = SIMP[it % 2]
                nc.tensor.matmul(pb[:, 0:512], lhs, KROT[(h % 2) * 64:(h % 2) * 64 + 64, rhsbase:rhsbase + 512],
                                 start=True, stop=True)
                p.inc(nc.tensor.matmul(pb[:, 512:640], lhs,
                                       KROT[(h % 2) * 64:(h % 2) * 64 + 64, rhsbase + 512:rhsbase + 640],
                                       start=True, stop=True), "mm", 1)
                sim_mm[it] = p.cnt["mm"]
            blk.tensor(t_sim)

            def v_max(e, it=it, qt=qt):
                e.wait_ge(p.sems["mm"], sim_mm[it])
                pb = SIMP[it % 2]
                if qt < 4:
                    e.tensor_add(pb[:, 0:512], pb[:, 0:512], MSK4[:, (qt % 4) * 640:(qt % 4) * 640 + 512])
                    e.tensor_add(pb[:, 512:640], pb[:, 512:640], MSK4[:, (qt % 4) * 640 + 512:(qt % 4 + 1) * 640])
                else:
                    e.tensor_add(pb[:, 0:128], pb[:, 0:128], M0[:, :])
                    e.tensor_add(pb[:, 512:640], pb[:, 512:640], M4[:, :])
                e.drain()
                e.reduce_max(rowm[:, it % 2:it % 2 + 1], pb[:, 0:640], axis=AX.X)
                e.drain()
                e.tensor_scalar(rowm[:, it % 2:it % 2 + 1], rowm[:, it % 2:it % 2 + 1], -8.0, None, OP.mult)
                e.drain()
                e.sem_inc(p.sems["ev"], 1)
                p.cnt["ev"] += 1
                ev_max[it] = p.cnt["ev"]
            blk.vector(v_max)

            def a_exp(e, it=it):
                e.wait_ge(p.sems["ev"], ev_max[it])
                if it - C2D in mm_tr:
                    e.wait_ge(p.sems["mm"], mm_tr[it - C2D])
                pb = SIMP[it % 2]
                e.activation(out=P_sb[:, (it % 2) * 640:(it % 2) * 640 + 640], in_=pb[:, 0:640],
                             func=AF.Exp, bias=rowm[:, it % 2:it % 2 + 1], scale=8.0,
                             accum_out=rsum[:, (it % 2) * 2:(it % 2) * 2 + 1])
                e.drain()
                e.sem_inc(p.sems["ex"], 1)
                p.cnt["ex"] += 1
                ex_exp[it] = p.cnt["ex"]
            blk.scalar(a_exp)

            def v_rs(e, it=it):
                e.wait_ge(p.sems["ex"], ex_exp[it])
                e.reciprocal(rinv[:, it % 2:it % 2 + 1], rsum[:, (it % 2) * 2:(it % 2) * 2 + 1])
                e.drain()
                e.tensor_scalar(DIAG[:, (it % 2) * 128:(it % 2 + 1) * 128], u.ident[:, :],
                                rinv[:, it % 2:it % 2 + 1], None, OP.mult)
                e.drain()
                e.sem_inc(p.sems["ev"], 1)
                p.cnt["ev"] += 1
                ev_rinv[it] = p.cnt["ev"]
            blk.vector(v_rs)

        stage2(30)
        stage2(31)

        if l == 0 and qg == 0 and "dbgo" in E.get("_taps", {}):
            p.barrier()
            def g_do(e):
                p.inc(e.dma_start(out=E["_taps"]["dbgo"][:, :], in_=o_sb[:, :]), "st", 16)
                if "dbgp" in E["_taps"]:
                    p.inc(e.dma_start(out=E["_taps"]["dbgp"][:, :], in_=P_sb[:, :]), "st", 16)
            blk.gpsimd(g_do)
            p.barrier()

        def s_ldxi(e, qg=qg):
            p.w(e, "st")
            for kk in range(G):
                p.inc(e.dma_start(out=XPR[:, kk * 512:(kk + 1) * 512],
                                  in_=src[kk * 128:(kk + 1) * 128, KH + qg * 512:KH + (qg + 1) * 512]), "ld", 16)
        blk.sync(s_ldxi)

        ev_xp = {}
        for oc in range(G):
            def s_wo(e, oc=oc):
                if oc == 0:
                    p.w(e, "ld")
                    p.w(e, "ex")
                    p.w(e, "ev")
                if oc - 2 in ev_xp:
                    e.wait_ge(p.sems["ev"], ev_xp[oc - 2])
                for c4 in range(4):
                    ins = nc.tensor.matmul(PS0[:, (oc % 2) * 512:(oc % 2) * 512 + 512],
                                           WO[:, c4 * 1024 + oc * 128:c4 * 1024 + (oc + 1) * 128],
                                           o_sb[:, c4 * 512:(c4 + 1) * 512],
                                           start=(c4 == 0), stop=(c4 == 3))
                    if c4 == 3:
                        p.inc(ins, "mm", 1)
                        ev_xp[(oc, "mm")] = p.cnt["mm"]
            blk.tensor(s_wo)

            def s_xp(e, oc=oc):
                e.wait_ge(p.sems["mm"], ev_xp[(oc, "mm")])
                e.scalar_tensor_tensor(out=XPR[:, oc * 512:(oc + 1) * 512],
                                       in0=PS0[:, (oc % 2) * 512:(oc % 2) * 512 + 512],
                                       scalar=1.0, in1=XPR[:, oc * 512:(oc + 1) * 512],
                                       op0=OP.mult, op1=OP.add)
                e.drain()
                e.sem_inc(p.sems["ev"], 1)
                p.cnt["ev"] += 1
                ev_xp[oc] = p.cnt["ev"]
            blk.vector(s_xp)

        def s_stxp(e, qg=qg):
            p.w(e, "ev")
            for kk in range(G):
                p.inc(e.dma_start(out=xpd[kk * 128:(kk + 1) * 128, 1 + qg * 512:1 + (qg + 1) * 512],
                                  in_=XPR[:, kk * 512:(kk + 1) * 512]), "st", 16)
        blk.sync(s_stxp)
        p.barrier()

        if l == 0 and qg == 0 and "dbgo2" in E.get("_taps", {}):
            def g_do2(e):
                p.inc(e.dma_start(out=E["_taps"]["dbgo2"][:, :], in_=o_sb[:, :]), "st", 16)
                p.inc(e.dma_start(out=E["_taps"]["dbgxpr"][:, 0:G * 512], in_=XPR[:, :]), "st", 16)
            blk.gpsimd(g_do2)
            p.barrier()

    c_sc.close()
    at.close()

    ex_sc = Scope(nc)
    emit_exchange(p, u, ex_sc, xpd, T - 7, 8, E["snd_s"], E["rcv_s"], xpd, 0, cols=(7, 8))
    ex_sc.close()

    if hasattr(p, 'marks'): p.marks.append(('f%d.attn' % l, nc.n_instructions()))
    T3F = 512
    NTF = T // T3F
    ff = Scope(nc)
    W1A = ff.sb("W1A", [128, G * HIDP], BF16)
    W1G = ff.sb("W1G", [128, G * HIDP], BF16)
    W2 = ff.sb("W2", [128, NHC * 1024], BF16)
    XPT = ff.sb("XPT", [128, G * (T3F + 1)])
    SQF = ff.sb("SQF", [128, 2 * T3F])
    H3 = ff.sb("H3", [128, G * T3F], BF16)
    HH = ff.sb("HH", [128, NHC * T3F], BF16)
    fr1 = ff.sb("fr1", [1, T3F]); fr2 = ff.sb("fr2", [1, T3F])
    lgrow = ff.sb("lgrow", [1, D]); lbrow = ff.sb("lbrow", [1, D])
    b1a_c = ff.sb("b1a_c", [128, NHC]); b1g_c = ff.sb("b1g_c", [128, NHC])
    b2_c = ff.sb("b2_c", [128, G])
    PS2 = ff.ps("PS2ff", [128, 1024])
    PS3 = ff.ps("PS3ff", [128, 1024])

    def s_wf(e):
        for kk in range(G):
            p.inc(e.dma_start(out=W1A[:, kk * HIDP:(kk + 1) * HIDP],
                              in_=E[pre + "w1a"][kk * 128:(kk + 1) * 128, :]), "ld", 16)
            p.inc(e.dma_start(out=W1G[:, kk * HIDP:(kk + 1) * HIDP],
                              in_=E[pre + "w1g"][kk * 128:(kk + 1) * 128, :]), "ld", 16)
        for hc in range(NHC):
            p.inc(e.dma_start(out=W2[:, hc * 1024:(hc + 1) * 1024],
                              in_=E[pre + "w2"][hc * 128:(hc + 1) * 128, :]), "ld", 16)
            p.inc(e.dma_start(out=b1a_c[:, hc:hc + 1], in_=E[pre + "b1a"][hc * 128:(hc + 1) * 128, :]), "ld", 16)
            p.inc(e.dma_start(out=b1g_c[:, hc:hc + 1], in_=E[pre + "b1g"][hc * 128:(hc + 1) * 128, :]), "ld", 16)
        for g in range(G):
            p.inc(e.dma_start(out=b2_c[:, g:g + 1], in_=E[pre + "b2"][g * 128:(g + 1) * 128, :]), "ld", 16)
        p.inc(e.dma_start(out=lgrow[:, :], in_=E[pre + "lg"][:, :]), "ld", 16)
        p.inc(e.dma_start(out=lbrow[:, :], in_=E[pre + "lb"][:, :]), "ld", 16)
    blk.gpsimd(s_wf)
    p.barrier()

    for tf in range(NTF):
        c0 = tf * T3F

        def fsh(kk, wid=T3F):
            off = 1 if kk < 4 else 0
            return XPT[:, kk * (T3F + 1) + off:kk * (T3F + 1) + off + wid]

        def s_lx(e, c0=c0):
            p.w(e, "st")
            for kk in range(G):
                p.inc(e.dma_start(out=XPT[:, kk * (T3F + 1):(kk + 1) * (T3F + 1)],
                                  in_=xpd[kk * 128:(kk + 1) * 128, c0:c0 + T3F + 1]), "ld", 16)
        blk.sync(s_lx)

        for kk in range(G):
            def s_sqf(e, kk=kk):
                if kk == 0:
                    p.w(e, "ld")
                else:
                    p.w(e, "mm")
                e.square(SQF[:, 0:T3F], fsh(kk))
                e.drain()
                e.sem_inc(p.sems["ex"], 1)
            blk.scalar(s_sqf)
            p.cnt["ex"] += 1

            def s_mmsq(e, kk=kk):
                p.w(e, "ex")
                ins = nc.tensor.matmul(PS0[0:1, 0:T3F], u.ones_col[:, :], SQF[:, 0:T3F],
                                       start=(kk == 0), stop=(kk == G - 1))
                if kk == G - 1:
                    p.inc(ins, "mm", 1)
                else:
                    p.inc(ins, "mm", 1)
            blk.tensor(s_mmsq)

        def s_mmsum(e):
            for kk in range(G):
                ins = nc.tensor.matmul(PS1[0:1, 0:T3F], u.ones_col[:, :], fsh(kk),
                                       start=(kk == 0), stop=(kk == G - 1))
                if kk == G - 1:
                    p.inc(ins, "mm", 1)
        blk.tensor(s_mmsum)

        def s_st2(e):
            p.w(e, "mm")
            e.tensor_scalar(fr2[0:1, :], PS1[0:1, 0:T3F], 1.0 / 1024.0, None, OP.mult)
            e.drain()
            e.tensor_mul(fr1[0:1, :], fr2[0:1, :], fr2[0:1, :])
            e.drain()
            e.scalar_tensor_tensor(out=fr1[0:1, :], in0=PS0[0:1, 0:T3F], scalar=1.0 / 1024.0,
                                   in1=fr1[0:1, :], op0=OP.mult, op1=OP.subtract)
            e.drain()
            e.tensor_scalar(fr1[0:1, :], fr1[0:1, :], 1e-5, None, OP.add)
            e.drain()
            e.reciprocal(fr1[0:1, :], fr1[0:1, :])
            e.drain()
            e.sem_inc(p.sems["ev"], 1)
        blk.vector(s_st2)
        p.cnt["ev"] += 1

        def s_sq3(e):
            p.w(e, "ev")
            e.sqrt(fr1[0:1, :], fr1[0:1, :])
            e.drain()
            e.sem_inc(p.sems["ex"], 1)
        blk.scalar(s_sq3)
        p.cnt["ex"] += 1

        def s_negm(e):
            p.w(e, "ex")
            e.scalar_tensor_tensor(out=fr2[0:1, :], in0=fr2[0:1, :], scalar=-1.0,
                                   in1=fr1[0:1, :], op0=OP.mult, op1=OP.mult)
            e.drain()
            e.sem_inc(p.sems["ev"], 1)
        blk.vector(s_negm)
        p.cnt["ev"] += 1

        fst = {}
        for kk in range(G):
            def s_r1(e, kk=kk):
                if kk == 0:
                    p.w(e, "ev")
                else:
                    e.wait_ge(p.sems["ev"], fst[("h3", kk - 1)])
                p.inc(nc.tensor.matmul(PS2[:, 0:T3F], lgrow[0:1, kk * 128:(kk + 1) * 128], fr1[0:1, :],
                                       start=True, stop=True), "mm", 1)
                nc.tensor.matmul(PS3[:, 0:T3F], lbrow[0:1, kk * 128:(kk + 1) * 128], u.onesN[0:1, 0:T3F],
                                 start=True, stop=False)
                p.inc(nc.tensor.matmul(PS3[:, 0:T3F], lgrow[0:1, kk * 128:(kk + 1) * 128], fr2[0:1, :],
                                       start=False, stop=True), "mm", 1)
                fst[("r1", kk)] = p.cnt["mm"]
            blk.tensor(s_r1)

            def s_h3(e, kk=kk):
                e.wait_ge(p.sems["mm"], fst[("r1", kk)])
                e.scalar_tensor_tensor(out=SQF[:, 0:T3F], in0=fsh(kk), scalar=1.0,
                                       in1=PS2[:, 0:T3F], op0=OP.mult, op1=OP.mult)
                e.drain()
                e.tensor_add(H3[:, kk * T3F:(kk + 1) * T3F], SQF[:, 0:T3F], PS3[:, 0:T3F])
                e.drain()
                e.sem_inc(p.sems["ev"], 1)
                p.cnt["ev"] += 1
                fst[("h3", kk)] = p.cnt["ev"]
            blk.vector(s_h3)

        for hc in range(NHC):
            def s_w1(e, hc=hc):
                if hc == 0:
                    p.w(e, "ev")
                if ("hh", hc - 2) in fst:
                    e.wait_ge(p.sems["ev"], fst[("hh", hc - 2)])
                    e.wait_ge(p.sems["ex"], fst[("gl", hc - 2)])
                for kk in range(G):
                    ins = nc.tensor.matmul(PS0[:, (hc % 2) * 512:(hc % 2) * 512 + T3F],
                                           W1A[:, kk * HIDP + hc * 128:kk * HIDP + (hc + 1) * 128],
                                           H3[:, kk * T3F:(kk + 1) * T3F],
                                           start=(kk == 0), stop=(kk == G - 1))
                for kk in range(G):
                    ins = nc.tensor.matmul(PS1[:, (hc % 2) * 512:(hc % 2) * 512 + T3F],
                                           W1G[:, kk * HIDP + hc * 128:kk * HIDP + (hc + 1) * 128],
                                           H3[:, kk * T3F:(kk + 1) * T3F],
                                           start=(kk == 0), stop=(kk == G - 1))
                    if kk == G - 1:
                        p.inc(ins, "mm", 1)
                        fst[("w1", hc)] = p.cnt["mm"]
            blk.tensor(s_w1)

            def s_gelu(e, hc=hc):
                e.wait_ge(p.sems["mm"], fst[("w1", hc)])
                if ("hh", hc - 2) in fst:
                    e.wait_ge(p.sems["ev"], fst[("hh", hc - 2)])
                e.activation(out=SQF[:, (hc % 2) * T3F:(hc % 2) * T3F + T3F],
                             in_=PS1[:, (hc % 2) * 512:(hc % 2) * 512 + T3F],
                             func=AF.Gelu, bias=b1g_c[:, hc:hc + 1], scale=1.0)
                e.drain()
                e.sem_inc(p.sems["ex"], 1)
                p.cnt["ex"] += 1
                fst[("gl", hc)] = p.cnt["ex"]
            blk.scalar(s_gelu)

            def s_hh(e, hc=hc):
                e.wait_ge(p.sems["ex"], fst[("gl", hc)])
                e.scalar_tensor_tensor(out=HH[:, hc * T3F:(hc + 1) * T3F],
                                       in0=PS0[:, (hc % 2) * 512:(hc % 2) * 512 + T3F],
                                       scalar=b1a_c[:, hc:hc + 1], in1=SQF[:, (hc % 2) * T3F:(hc % 2) * T3F + T3F],
                                       op0=OP.add, op1=OP.mult)
                e.drain()
                e.sem_inc(p.sems["ev"], 1)
                p.cnt["ev"] += 1
                fst[("hh", hc)] = p.cnt["ev"]
            blk.vector(s_hh)

        for oc in range(G):
            def s_w2(e, oc=oc):
                if oc == 0:
                    p.w(e, "ev")
                if ("o", oc - 2) in fst:
                    e.wait_ge(p.sems["ev"], fst[("o", oc - 2)])
                for hc in range(NHC):
                    ins = nc.tensor.matmul(PS2[:, (oc % 2) * 512:(oc % 2) * 512 + T3F],
                                           W2[:, hc * 1024 + oc * 128:hc * 1024 + (oc + 1) * 128],
                                           HH[:, hc * T3F:(hc + 1) * T3F],
                                           start=(hc == 0), stop=(hc == NHC - 1))
                    if hc == NHC - 1:
                        p.inc(ins, "mm", 1)
                        fst[("w2", oc)] = p.cnt["mm"]
            blk.tensor(s_w2)

            def s_out(e, oc=oc):
                e.wait_ge(p.sems["mm"], fst[("w2", oc)])
                e.scalar_tensor_tensor(out=XPT[:, oc * (T3F + 1) + 1:oc * (T3F + 1) + 1 + T3F],
                                       in0=PS2[:, (oc % 2) * 512:(oc % 2) * 512 + T3F],
                                       scalar=b2_c[:, oc:oc + 1],
                                       in1=XPT[:, oc * (T3F + 1) + 1:oc * (T3F + 1) + 1 + T3F],
                                       op0=OP.add, op1=OP.add)
                e.drain()
                e.sem_inc(p.sems["ev"], 1)
                p.cnt["ev"] += 1
                fst[("o", oc)] = p.cnt["ev"]
            blk.vector(s_out)

        def s_sto(e, c0=c0):
            p.w(e, "ev")
            for kk in range(G):
                p.inc(e.dma_start(out=dst[kk * 128:(kk + 1) * 128, dst_col0 + c0:dst_col0 + c0 + T3F],
                                  in_=XPT[:, kk * (T3F + 1) + 1:kk * (T3F + 1) + 1 + T3F]), "st", 16)
        blk.sync(s_sto)
        p.barrier()

    ff.close()
    if hasattr(p, 'marks'): p.marks.append(('f%d.ffn' % l, nc.n_instructions()))


def build_l0f(p, E, u):
    nc, blk = p.nc, p.blk

    sc = Scope(nc)
    emit_exchange(p, u, sc, E["x1_d"], T - KH, KH, E["snd_h"], E["rcv_h"], E["hx1_d"], 0)
    sc.close()

    lnsc = Scope(nc)
    LA = lnsc.sb("LA", [128, G * 512])
    LB = lnsc.sb("LB", [128, G * 512])
    r1 = lnsc.sb("r1", [1, 512]); r2 = lnsc.sb("r2", [1, 512])
    PSa = lnsc.ps("PSaln", [128, 1024]); PSb = lnsc.ps("PSbln", [128, 1024])
    emit_ln_job(p, u, LA, LB, r1, r2, E["hx1_d"], 0, E["xi0_d"], 0, 512, PSa, PSb)
    emit_ln_job(p, u, LA, LB, r1, r2, E["hx1_d"], 512, E["xi0_d"], 512, 2, PSa, PSb)
    for i in range(8):
        emit_ln_job(p, u, LA, LB, r1, r2, E["x1_d"], i * 512, E["xi0_d"], KH + i * 512, 512, PSa, PSb)
    lnsc.close()

    build_l0f_layer(p, E, u, 0, E["xi0_d"], E["xp0_d"], E["xi1_d"], KH)
    sc = Scope(nc)
    emit_exchange(p, u, sc, E["xi1_d"], KH + T - KH, KH, E["snd_h"], E["rcv_h"], E["xi1_d"], 0)
    sc.close()
    build_l0f_layer(p, E, u, 1, E["xi1_d"], E["xp1_d"], E["xi2_d"], 0)

    sc = Scope(nc)
    XA = sc.sb("XA", [128, G * 512])
    XB = sc.sb("XB", [128, G * 512])
    for i in range(8):
        def s_l(e, i=i):
            p.w(e, "st")
            for g in range(G):
                p.inc(e.dma_start(out=XA[:, g * 512:(g + 1) * 512],
                                  in_=E["x1_d"][g * 128:(g + 1) * 128, i * 512:(i + 1) * 512]), "ld", 16)
                p.inc(e.dma_start(out=XB[:, g * 512:(g + 1) * 512],
                                  in_=E["xi2_d"][g * 128:(g + 1) * 128, i * 512:(i + 1) * 512]), "ld", 16)
        blk.sync(s_l)

        def s_a(e):
            p.w(e, "ld")
            e.tensor_add(XA[:, :], XA[:, :], XB[:, :])
            e.drain()
            e.sem_inc(p.sems["ev"], 1)
        blk.vector(s_a)
        p.cnt["ev"] += 1

        def s_s(e, i=i):
            p.w(e, "ev")
            for g in range(G):
                p.inc(e.dma_start(out=E["x2_d"][g * 128:(g + 1) * 128, i * 512:(i + 1) * 512],
                                  in_=XA[:, g * 512:(g + 1) * 512]), "st", 16)
        blk.sync(s_s)
        p.barrier()
    sc.close()


def emit_ln_sb(p, u, XT, W, OUT, grow, brow, fr1, fr2, SQ1, PSa, PSb, eps=1e-5):
    nc, blk = p.nc, p.blk
    for kk in range(G):
        def s_sq(e, kk=kk):
            if kk > 0:
                p.w(e, "mm")
            else:
                p.w(e, "ld")
            e.square(SQ1[:, 0:W], XT[:, kk * W:(kk + 1) * W])
            e.drain()
            e.sem_inc(p.sems["ex"], 1)
        blk.scalar(s_sq)
        p.cnt["ex"] += 1

        def s_mm(e, kk=kk):
            p.w(e, "ex")
            p.inc(nc.tensor.matmul(u.PS0[0:1, 0:W], u.ones_col[:, :], SQ1[:, 0:W],
                                   start=(kk == 0), stop=(kk == G - 1)), "mm", 1)
        blk.tensor(s_mm)

    def s_mmsum(e):
        for kk in range(G):
            ins = nc.tensor.matmul(u.PS1[0:1, 0:W], u.ones_col[:, :], XT[:, kk * W:(kk + 1) * W],
                                   start=(kk == 0), stop=(kk == G - 1))
            if kk == G - 1:
                p.inc(ins, "mm", 1)
    blk.tensor(s_mmsum)

    def s_st(e):
        p.w(e, "mm")
        e.tensor_scalar(fr2[0:1, 0:W], u.PS1[0:1, 0:W], 1.0 / 1024.0, None, OP.mult)
        e.drain()
        e.tensor_mul(fr1[0:1, 0:W], fr2[0:1, 0:W], fr2[0:1, 0:W])
        e.drain()
        e.scalar_tensor_tensor(out=fr1[0:1, 0:W], in0=u.PS0[0:1, 0:W], scalar=1.0 / 1024.0,
                               in1=fr1[0:1, 0:W], op0=OP.mult, op1=OP.subtract)
        e.drain()
        e.tensor_scalar(fr1[0:1, 0:W], fr1[0:1, 0:W], eps, None, OP.add)
        e.drain()
        e.reciprocal(fr1[0:1, 0:W], fr1[0:1, 0:W])
        e.drain()
        e.sem_inc(p.sems["ev"], 1)
    blk.vector(s_st)
    p.cnt["ev"] += 1

    def s_sqr(e):
        p.w(e, "ev")
        e.sqrt(fr1[0:1, 0:W], fr1[0:1, 0:W])
        e.drain()
        e.sem_inc(p.sems["ex"], 1)
    blk.scalar(s_sqr)
    p.cnt["ex"] += 1

    def s_nm(e):
        p.w(e, "ex")
        e.scalar_tensor_tensor(out=fr2[0:1, 0:W], in0=fr2[0:1, 0:W], scalar=-1.0,
                               in1=fr1[0:1, 0:W], op0=OP.mult, op1=OP.mult)
        e.drain()
        e.sem_inc(p.sems["ev"], 1)
    blk.vector(s_nm)
    p.cnt["ev"] += 1

    lnst = {}
    for kk in range(G):
        def s_r1(e, kk=kk):
            if kk == 0:
                p.w(e, "ev")
            else:
                e.wait_ge(p.sems["ev"], lnst[kk - 1])
            if grow is not None:
                p.inc(nc.tensor.matmul(PSa[:, 0:W], grow[0:1, kk * 128:(kk + 1) * 128], fr1[0:1, 0:W],
                                       start=True, stop=True), "mm", 1)
                nc.tensor.matmul(PSb[:, 0:W], brow[0:1, kk * 128:(kk + 1) * 128], u.onesN[0:1, 0:W],
                                 start=True, stop=False)
                p.inc(nc.tensor.matmul(PSb[:, 0:W], grow[0:1, kk * 128:(kk + 1) * 128], fr2[0:1, 0:W],
                                       start=False, stop=True), "mm", 1)
            else:
                p.inc(nc.tensor.matmul(PSa[:, 0:W], u.ones_row[:, :], fr1[0:1, 0:W],
                                       start=True, stop=True), "mm", 1)
                p.inc(nc.tensor.matmul(PSb[:, 0:W], u.ones_row[:, :], fr2[0:1, 0:W],
                                       start=True, stop=True), "mm", 1)
        blk.tensor(s_r1)

        def s_ap(e, kk=kk):
            p.w(e, "mm")
            e.scalar_tensor_tensor(out=SQ1[:, 0:W], in0=XT[:, kk * W:(kk + 1) * W], scalar=1.0,
                                   in1=PSa[:, 0:W], op0=OP.mult, op1=OP.mult)
            e.drain()
            e.tensor_add(OUT[:, kk * W:(kk + 1) * W], SQ1[:, 0:W], PSb[:, 0:W])
            e.drain()
            e.sem_inc(p.sems["ev"], 1)
            p.cnt["ev"] += 1
            lnst[kk] = p.cnt["ev"]
        blk.vector(s_ap)


def emit_trig(p, u, PIN, pofs, POUT, cofs, sofs, W, TMP1, TMP2):
    nc, blk = p.nc, p.blk
    TWO_PI = float(2.0 * math.pi)
    MAGIC = 12582912.0

    def s_t(e):
        p.w(e, "mm")
        p.w(e, "ex")
        e.tensor_scalar(TMP1[:, 0:W], PIN, 0.125, None, OP.mult)
        e.drain()
        e.tensor_scalar(TMP2[:, 0:W], TMP1[:, 0:W], 1.0 / TWO_PI, MAGIC, OP.mult, OP.add)
        e.drain()
        e.tensor_scalar(TMP2[:, 0:W], TMP2[:, 0:W], MAGIC, None, OP.subtract)
        e.drain()
        e.scalar_tensor_tensor(out=TMP2[:, 0:W], in0=TMP2[:, 0:W], scalar=-TWO_PI,
                               in1=TMP1[:, 0:W], op0=OP.mult, op1=OP.add)
        e.drain()
        e.sem_inc(p.sems["ev"], 1)
    blk.vector(s_t)
    p.cnt["ev"] += 1

    def s_sin(e):
        p.w(e, "ev")
        e.activation(out=POUT[:, sofs:sofs + W], in_=TMP2[:, 0:W], func=AF.Sin)
        e.drain()
        e.sem_inc(p.sems["ex"], 1)
    blk.scalar(s_sin)
    p.cnt["ex"] += 1

    def s_t2(e):
        p.w(e, "ex")
        e.tensor_scalar(TMP2[:, 0:W], TMP1[:, 0:W], math.pi / 2.0, None, OP.add)
        e.drain()
        e.tensor_scalar(TMP1[:, 0:W], TMP2[:, 0:W], 1.0 / TWO_PI, MAGIC, OP.mult, OP.add)
        e.drain()
        e.tensor_scalar(TMP1[:, 0:W], TMP1[:, 0:W], MAGIC, None, OP.subtract)
        e.drain()
        e.scalar_tensor_tensor(out=TMP1[:, 0:W], in0=TMP1[:, 0:W], scalar=-TWO_PI,
                               in1=TMP2[:, 0:W], op0=OP.mult, op1=OP.add)
        e.drain()
        e.sem_inc(p.sems["ev"], 1)
    blk.vector(s_t2)
    p.cnt["ev"] += 1

    def s_cos(e):
        p.w(e, "ev")
        e.activation(out=POUT[:, cofs:cofs + W], in_=TMP1[:, 0:W], func=AF.Sin)
        e.drain()
        e.sem_inc(p.sems["ex"], 1)
    blk.scalar(s_cos)
    p.cnt["ex"] += 1


def build_l1(p, E, u):
    nc, blk = p.nc, p.blk
    PS0, PS1 = u.PS0, u.PS1

    pk = Scope(nc)
    WK = pk.sb("WK", [128, G * 1024], BF16)
    WV = pk.sb("WV", [128, G * 1024], BF16)
    PROJ = pk.sb("PROJ", [128, 128], BF16)
    XT = pk.sb("XT", [128, G * 512])
    XLN = pk.sb("XLN", [128, G * 512], BF16)
    KF = pk.sb("KF", [128, G * 512], BF16)
    VTK = pk.sb("VTK", [128, 4 * 1024], BF16)
    KP = pk.sb("KP", [128, 2 * 256], BF16)
    TMP1 = pk.sb("TMP1", [128, 256]); TMP2 = pk.sb("TMP2", [128, 256])
    SQ1 = pk.sb("SQ1", [128, 512])
    fr1 = pk.sb("fr1", [1, 512]); fr2 = pk.sb("fr2", [1, 512])
    g1row = pk.sb("g1row", [1, D]); b1row = pk.sb("b1row", [1, D])
    bkc = pk.sb("bkc", [128, G])
    BVP = pk.sb("BVP", [128, 1024])
    bvrow = pk.sb("bvrow", [1, D])
    KVS = pk.sb("KVS", [128, 16 * 128])
    KVACC = pk.sb("KVACC", [128, 16 * 128])
    PSkv0 = pk.ps("PSkv0", [128, 1024])
    PSkv1 = pk.ps("PSkv1", [128, 1024])
    PSa = u.PS0
    PSb = u.PS1

    def s_wk(e):
        for kk in range(G):
            p.inc(e.dma_start(out=WK[:, kk * 1024:(kk + 1) * 1024],
                              in_=E["a_wk"][kk * 128:(kk + 1) * 128, :]), "ld", 16)
            p.inc(e.dma_start(out=WV[:, kk * 1024:(kk + 1) * 1024],
                              in_=E["a_wv"][kk * 128:(kk + 1) * 128, :]), "ld", 16)
            p.inc(e.dma_start(out=bkc[:, kk:kk + 1], in_=E["a_bk"][kk * 128:(kk + 1) * 128, :]), "ld", 16)
        p.inc(e.dma_start(out=PROJ[0:64, :], in_=E["a_proj"][:, :]), "ld", 16)
        p.inc(e.dma_start(out=PROJ[64:128, :], in_=E["a_proj"][:, :]), "ld", 16)
    blk.gpsimd(s_wk)

    def s_row(e):
        p.inc(e.dma_start(out=g1row[:, :], in_=E["a_ln1_g"][:, :]), "ld", 16)
        p.inc(e.dma_start(out=b1row[:, :], in_=E["a_ln1_b"][:, :]), "ld", 16)
        p.inc(e.dma_start(out=bvrow[:, :], in_=E["a_bv"][:, :]), "ld", 16)
    blk.sync(s_row)
    p.barrier()

    def s_bvp(e):
        p.inc(nc.tensor.matmul(PSa[:, 0:512], u.ones_row[:, :], bvrow[0:1, 0:512],
                               start=True, stop=True), "mm", 1)
        p.inc(nc.tensor.matmul(PSa[:, 512:1024], u.ones_row[:, :], bvrow[0:1, 512:1024],
                               start=True, stop=True), "mm", 1)
    blk.tensor(s_bvp)

    def s_bvp2(e):
        p.w(e, "mm")
        e.tensor_copy(BVP[:, :], PSa[:, 0:1024])
        e.drain()
        e.memset(KVACC[:, :], 0.0)
        e.drain()
        e.sem_inc(p.sems["ev"], 1)
    blk.vector(s_bvp2)
    p.cnt["ev"] += 1
    p.barrier()

    for tt in range(NT):
        c0 = tt * TT

        def s_lx(e, c0=c0):
            for kk in range(G):
                p.inc(e.dma_start(out=XT[:, kk * 512:(kk + 1) * 512],
                                  in_=E["x2_d"][kk * 128:(kk + 1) * 128, c0:c0 + 512]), "ld", 16)
        blk.sync(s_lx)

        def s_wait(e):
            p.w(e, "ld")
            e.engine_nop()
            e.drain()
            e.sem_inc(p.sems["ev"], 1)
        blk.vector(s_wait)
        p.cnt["ev"] += 1
        p.barrier()

        emit_ln_sb(p, u, XT, 512, XLN, g1row, b1row, fr1, fr2, SQ1, u.PS0[:, 512:1024], u.PS1[:, 512:1024])

        def s_stln(e, c0=c0):
            p.w(e, "ev")
            for kk in range(G):
                p.inc(e.dma_start(out=E["x2ln_d"][kk * 128:(kk + 1) * 128, c0:c0 + 512],
                                  in_=XLN[:, kk * 512:(kk + 1) * 512]), "st", 16)
        blk.gpsimd(s_stln)

        for oc in range(G):
            def s_kmm(e, oc=oc):
                if oc == 0:
                    p.w(e, "ev")
                for kk in range(G):
                    ins = nc.tensor.matmul(PSa[:, (oc % 2) * 512:(oc % 2) * 512 + 512],
                                           WK[:, kk * 1024 + oc * 128:kk * 1024 + (oc + 1) * 128],
                                           XLN[:, kk * 512:(kk + 1) * 512],
                                           start=(kk == 0), stop=(kk == G - 1))
                    if kk == G - 1:
                        p.inc(ins, "mm", 1)
            blk.tensor(s_kmm)

            def s_kev(e, oc=oc):
                p.w(e, "mm")
                e.tensor_scalar(KF[:, oc * 512:(oc + 1) * 512], PSa[:, (oc % 2) * 512:(oc % 2) * 512 + 512],
                                bkc[:, oc:oc + 1], None, OP.add)
                e.drain()
                e.sem_inc(p.sems["ev"], 1)
            blk.vector(s_kev)
            p.cnt["ev"] += 1
            p.barrier()

        for vc in range(4):
            def s_vmm(e, vc=vc):
                for kk in range(G):
                    ins = nc.tensor.matmul(PSb[:, 0:512],
                                           XLN[:, kk * 512 + vc * 128:kk * 512 + (vc + 1) * 128],
                                           WV[:, kk * 1024:kk * 1024 + 512],
                                           start=(kk == 0), stop=(kk == G - 1))
                    ins = nc.tensor.matmul(PSb[:, 512:1024],
                                           XLN[:, kk * 512 + vc * 128:kk * 512 + (vc + 1) * 128],
                                           WV[:, kk * 1024 + 512:(kk + 1) * 1024],
                                           start=(kk == 0), stop=(kk == G - 1))
                    if kk == G - 1:
                        p.inc(ins, "mm", 1)
            blk.tensor(s_vmm)

            def s_vev(e, vc=vc):
                p.w(e, "mm")
                e.tensor_add(VTK[:, vc * 1024:(vc + 1) * 1024], PSb[:, 0:1024], BVP[:, :])
                e.drain()
                e.sem_inc(p.sems["ev"], 1)
            blk.vector(s_vev)
            p.cnt["ev"] += 1
            p.barrier()

        for sub in range(4):
            for hp in range(8):
                def s_pmm(e, sub=sub, hp=hp):
                    for hh in range(2):
                        b0 = hh * 64
                        ins = nc.tensor.matmul(PSb[:, hh * 512:hh * 512 + 128],
                                               KF[b0:b0 + 64, hp * 512 + sub * 128:hp * 512 + (sub + 1) * 128],
                                               PROJ[b0:b0 + 64, :],
                                               start=True, stop=True)
                        p.inc(ins, "mm", 1)
                blk.tensor(s_pmm)

                for hh in range(2):
                    emit_trig(p, u, PSb[:, hh * 512:hh * 512 + 128], 0,
                              KP, hh * 256, hh * 256 + 128, 128, TMP1, TMP2)

                def s_kvmm(e, sub=sub, hp=hp):
                    p.w(e, "ex")
                    for hh in range(2):
                        h = hp * 2 + hh
                        pskv = PSkv0 if h < 8 else PSkv1
                        hb = h % 8
                        for mc in range(2):
                            ins = nc.tensor.matmul(
                                pskv[:, hb * 128 + mc * 64:hb * 128 + (mc + 1) * 64],
                                KP[:, hh * 256 + mc * 128:hh * 256 + (mc + 1) * 128],
                                VTK[:, sub * 1024 + h * 64:sub * 1024 + (h + 1) * 64],
                                start=True, stop=True)
                            if hp == 7 and hh == 1 and mc == 1:
                                p.inc(ins, "mm", 1)
                blk.tensor(s_kvmm)
                p.barrier()

            def s_kvacc(e, sub=sub):
                e.tensor_add(KVACC[:, 0:1024], KVACC[:, 0:1024], PSkv0[:, 0:1024])
                e.drain()
                e.tensor_add(KVACC[:, 1024:2048], KVACC[:, 1024:2048], PSkv1[:, 0:1024])
                e.drain()
                e.sem_inc(p.sems["ev"], 1)
            blk.vector(s_kvacc)
            p.cnt["ev"] += 1
            p.barrier()

    def s_kvc(e):
        p.inc(e.dma_start(out=E["kv_snd"][:, :], in_=KVACC[:, :]), "st", 16)
        p.w(e, "st")
        p.inc(e.collective_compute(
            "AllReduce", OP.add,
            replica_groups=[[0, 1], [2, 3], [4, 5], [6, 7]],
            ins=[E["kv_snd"].opt()], outs=[E["kv_rcv"].opt()],
        ), "cc", 1)
        p.w(e, "cc")
        p.inc(e.dma_start(out=KVS[:, :], in_=E["kv_rcv"][:, :]), "ld", 16)
    blk.gpsimd(s_kvc)
    p.barrier()

    def s_kvb(e):
        p.w(e, "ld")
        e.tensor_copy(XLN[:, 0:2048], KVS[:, :])
        e.drain()
        e.sem_inc(p.sems["ev"], 1)
    blk.vector(s_kvb)
    p.cnt["ev"] += 1

    def s_kvs(e):
        p.w(e, "ev")
        p.inc(e.dma_start(out=E["kvb_d"][:, :], in_=XLN[:, 0:2048]), "st", 16)
    blk.gpsimd(s_kvs)
    p.barrier()
    pk.close()

    if hasattr(p, 'marks'): p.marks.append(('l1.K', nc.n_instructions()))
    qa = Scope(nc)
    WQ = qa.sb("WQ", [128, G * 1024], BF16)
    WO1 = qa.sb("WO1", [128, G * 1024], BF16)
    PROJ2 = qa.sb("PROJ2", [128, 128], BF16)
    KVB = qa.sb("KVB", [128, 2048], BF16)
    XLN2 = qa.sb("XLN2", [128, G * 512], BF16)
    QF = qa.sb("QF", [128, G * 512], BF16)
    QP = qa.sb("QP", [128, 2 * 512], BF16)
    ATT = qa.sb("ATT", [128, G * 512], BF16)
    XR = qa.sb("XR", [128, G * 512])
    TMP1q = qa.sb("TMP1q", [128, 512]); TMP2q = qa.sb("TMP2q", [128, 512])
    bqc = qa.sb("bqc", [128, G]); boc = qa.sb("boc", [128, G])
    PSq = qa.ps("PSq", [128, 1024])
    PSat = qa.ps("PSat", [128, 1024])

    def s_wq(e):
        for kk in range(G):
            p.inc(e.dma_start(out=WQ[:, kk * 1024:(kk + 1) * 1024],
                              in_=E["a_wq"][kk * 128:(kk + 1) * 128, :]), "ld", 16)
            p.inc(e.dma_start(out=WO1[:, kk * 1024:(kk + 1) * 1024],
                              in_=E["a_wo"][kk * 128:(kk + 1) * 128, :]), "ld", 16)
            p.inc(e.dma_start(out=bqc[:, kk:kk + 1], in_=E["a_bq"][kk * 128:(kk + 1) * 128, :]), "ld", 16)
            p.inc(e.dma_start(out=boc[:, kk:kk + 1], in_=E["a_bo"][kk * 128:(kk + 1) * 128, :]), "ld", 16)
        p.inc(e.dma_start(out=PROJ2[0:64, :], in_=E["a_proj"][:, :]), "ld", 16)
        p.inc(e.dma_start(out=PROJ2[64:128, :], in_=E["a_proj"][:, :]), "ld", 16)
        p.inc(e.dma_start(out=KVB[:, :], in_=E["kvb_d"][:, :]), "ld", 16)
    blk.gpsimd(s_wq)
    p.barrier()

    for tt in range(NT):
        c0 = tt * TT

        def s_lx(e, c0=c0):
            p.w(e, "st")
            for kk in range(G):
                p.inc(e.dma_start(out=XLN2[:, kk * 512:(kk + 1) * 512],
                                  in_=E["x2ln_d"][kk * 128:(kk + 1) * 128, c0:c0 + 512]), "ld", 16)
                p.inc(e.dma_start(out=XR[:, kk * 512:(kk + 1) * 512],
                                  in_=E["x2_d"][kk * 128:(kk + 1) * 128, c0:c0 + 512]), "ld", 16)
        blk.sync(s_lx)

        qast = {}
        for oc in range(G):
            def s_qmm(e, oc=oc):
                if oc == 0:
                    p.w(e, "ld")
                    p.w(e, "ev")
                if ("qe", oc - 2) in qast:
                    e.wait_ge(p.sems["ev"], qast[("qe", oc - 2)])
                for kk in range(G):
                    ins = nc.tensor.matmul(PSq[:, (oc % 2) * 512:(oc % 2) * 512 + 512],
                                           WQ[:, kk * 1024 + oc * 128:kk * 1024 + (oc + 1) * 128],
                                           XLN2[:, kk * 512:(kk + 1) * 512],
                                           start=(kk == 0), stop=(kk == G - 1))
                    if kk == G - 1:
                        p.inc(ins, "mm", 1)
                        qast[("qm", oc)] = p.cnt["mm"]
            blk.tensor(s_qmm)

            def s_qev(e, oc=oc):
                e.wait_ge(p.sems["mm"], qast[("qm", oc)])
                e.tensor_scalar(QF[:, oc * 512:(oc + 1) * 512], PSq[:, (oc % 2) * 512:(oc % 2) * 512 + 512],
                                bqc[:, oc:oc + 1], None, OP.add)
                e.drain()
                e.sem_inc(p.sems["ev"], 1)
                p.cnt["ev"] += 1
                qast[("qe", oc)] = p.cnt["ev"]
            blk.vector(s_qev)

        for h in range(16):
            def s_pf(e, h=h):
                if h == 0:
                    p.w(e, "ev")
                if ("tg", h - 1) in qast:
                    e.wait_ge(p.sems["ev"], qast[("tg", h - 1)])
                if ("ae", h - 4) in qast:
                    e.wait_ge(p.sems["ev"], qast[("ae", h - 4)])
                b0 = (h % 2) * 64
                p.inc(nc.tensor.matmul(PSq[:, 0:512],
                                       PROJ2[b0:b0 + 64, :],
                                       QF[b0:b0 + 64, (h // 2) * 512:(h // 2 + 1) * 512],
                                       start=True, stop=True), "mm", 1)
            blk.tensor(s_pf)

            emit_trig(p, u, PSq[:, 0:512], 0, QP, 0, 512, 512, TMP1q, TMP2q)
            qast[("tg", h)] = p.cnt["ev"]

            def s_att(e, h=h):
                p.w(e, "ex")
                for mc in range(2):
                    ins = nc.tensor.matmul(PSat[((h % 2)) * 64:((h % 2)) * 64 + 64,
                                                ((h // 2) % 2) * 512:((h // 2) % 2) * 512 + 512],
                                           KVB[:, h * 128 + mc * 64:h * 128 + (mc + 1) * 64],
                                           QP[:, mc * 512:(mc + 1) * 512],
                                           start=(mc == 0), stop=(mc == 1),
                                           tile_position=(0, (h % 2) * 64))
                    if mc == 1:
                        p.inc(ins, "mm", 1)
                        qast[("at", h)] = p.cnt["mm"]
            blk.tensor(s_att)

            if h % 4 == 3:
                def s_atev(e, h=h):
                    e.wait_ge(p.sems["mm"], qast[("at", h)])
                    pair0 = h // 2 - 1
                    pair1 = h // 2
                    e.tensor_scalar(ATT[:, pair0 * 512:pair0 * 512 + 512],
                                    PSat[:, 0:512], 1.0 / 128.0, None, OP.mult)
                    e.drain()
                    e.tensor_scalar(ATT[:, pair1 * 512:pair1 * 512 + 512],
                                    PSat[:, 512:1024], 1.0 / 128.0, None, OP.mult)
                    e.drain()
                    e.sem_inc(p.sems["ev"], 1)
                    p.cnt["ev"] += 1
                    qast[("ae", h)] = p.cnt["ev"]
                blk.vector(s_atev)

        for oc in range(G):
            def s_womm(e, oc=oc):
                if oc == 0:
                    p.w(e, "ev")
                if ("re", oc - 2) in qast:
                    e.wait_ge(p.sems["ev"], qast[("re", oc - 2)])
                for kk in range(G):
                    ins = nc.tensor.matmul(PSq[:, (oc % 2) * 512:(oc % 2) * 512 + 512],
                                           WO1[:, kk * 1024 + oc * 128:kk * 1024 + (oc + 1) * 128],
                                           ATT[:, kk * 512:(kk + 1) * 512],
                                           start=(kk == 0), stop=(kk == G - 1))
                    if kk == G - 1:
                        p.inc(ins, "mm", 1)
                        qast[("wm", oc)] = p.cnt["mm"]
            blk.tensor(s_womm)

            def s_res(e, oc=oc):
                e.wait_ge(p.sems["mm"], qast[("wm", oc)])
                e.scalar_tensor_tensor(out=XR[:, oc * 512:(oc + 1) * 512],
                                       in0=PSq[:, (oc % 2) * 512:(oc % 2) * 512 + 512],
                                       scalar=boc[:, oc:oc + 1], in1=XR[:, oc * 512:(oc + 1) * 512],
                                       op0=OP.add, op1=OP.add)
                e.drain()
                e.sem_inc(p.sems["ev"], 1)
                p.cnt["ev"] += 1
                qast[("re", oc)] = p.cnt["ev"]
            blk.vector(s_res)

        def s_sx(e, c0=c0):
            p.w(e, "ev")
            for kk in range(G):
                p.inc(e.dma_start(out=E["xq_d"][kk * 128:(kk + 1) * 128, c0:c0 + 512],
                                  in_=XR[:, kk * 512:(kk + 1) * 512]), "st", 16)
        blk.sync(s_sx)
        p.barrier()

        if tt == 0 and "dbgatt" in E.get("_taps", {}):
            def g_da(e):
                p.inc(e.dma_start(out=E["_taps"]["dbgatt"][:, :], in_=ATT[:, :]), "st", 16)
                p.inc(e.dma_start(out=E["_taps"]["dbgqf"][:, :], in_=QF[:, :]), "st", 16)
                p.inc(e.dma_start(out=E["_taps"]["dbgqp"][:, :], in_=QP[:, :]), "st", 16)
            blk.gpsimd(g_da)
            p.barrier()
    qa.close()

    if hasattr(p, 'marks'): p.marks.append(('l1.QA', nc.n_instructions()))
    qf = Scope(nc)
    W1L = qf.sb("W1L", [128, G * 4096], BF16)
    W2L = qf.sb("W2L", [128, 32 * 1024], BF16)
    XT2 = qf.sb("XT2", [128, G * 512])
    XB2 = qf.sb("XB2", [128, G * 512], BF16)
    HH2 = qf.sb("HH2", [128, 32 * 512], BF16)
    SQ2 = qf.sb("SQ2", [128, 512])
    fr1b = qf.sb("fr1b", [1, 512]); fr2b = qf.sb("fr2b", [1, 512])
    g2row = qf.sb("g2row", [1, D]); b2row = qf.sb("b2row", [1, D])
    b1c = qf.sb("b1c", [128, 32]); b2c2 = qf.sb("b2c2", [128, G])
    PSc = qf.ps("PSc", [128, 1024])
    PSd = qf.ps("PSd", [128, 1024])

    def s_wf(e):
        for kk in range(G):
            p.inc(e.dma_start(out=W1L[:, kk * 4096:(kk + 1) * 4096],
                              in_=E["a_w1"][kk * 128:(kk + 1) * 128, :]), "ld", 16)
        for hc in range(32):
            p.inc(e.dma_start(out=W2L[:, hc * 1024:(hc + 1) * 1024],
                              in_=E["a_w2"][hc * 128:(hc + 1) * 128, :]), "ld", 16)
            p.inc(e.dma_start(out=b1c[:, hc:hc + 1], in_=E["a_b1"][hc * 128:(hc + 1) * 128, :]), "ld", 16)
        for kk in range(G):
            p.inc(e.dma_start(out=b2c2[:, kk:kk + 1], in_=E["a_b2"][kk * 128:(kk + 1) * 128, :]), "ld", 16)
    blk.gpsimd(s_wf)

    def s_row2(e):
        p.inc(e.dma_start(out=g2row[:, :], in_=E["a_ln2_g"][:, :]), "ld", 16)
        p.inc(e.dma_start(out=b2row[:, :], in_=E["a_ln2_b"][:, :]), "ld", 16)
    blk.sync(s_row2)
    p.barrier()

    for tt in range(NT):
        c0 = tt * TT

        def s_lx(e, c0=c0):
            p.w(e, "st")
            for kk in range(G):
                p.inc(e.dma_start(out=XT2[:, kk * 512:(kk + 1) * 512],
                                  in_=E["xq_d"][kk * 128:(kk + 1) * 128, c0:c0 + 512]), "ld", 16)
        blk.sync(s_lx)

        def s_wt(e):
            p.w(e, "ld")
            e.engine_nop()
            e.drain()
            e.sem_inc(p.sems["ev"], 1)
        blk.vector(s_wt)
        p.cnt["ev"] += 1
        p.barrier()

        emit_ln_sb(p, u, XT2, 512, XB2, g2row, b2row, fr1b, fr2b, SQ2, u.PS0[:, 512:1024], u.PS1[:, 512:1024])

        qfst = {}
        for hc in range(32):
            def s_w1(e, hc=hc):
                if hc == 0:
                    p.w(e, "ev")
                if ("rl", hc - 2) in qfst:
                    e.wait_ge(p.sems["ex"], qfst[("rl", hc - 2)])
                for kk in range(G):
                    ins = nc.tensor.matmul(PSc[:, (hc % 2) * 512:(hc % 2) * 512 + 512],
                                           W1L[:, kk * 4096 + hc * 128:kk * 4096 + (hc + 1) * 128],
                                           XB2[:, kk * 512:(kk + 1) * 512],
                                           start=(kk == 0), stop=(kk == G - 1))
                    if kk == G - 1:
                        p.inc(ins, "mm", 1)
                        qfst[("w1", hc)] = p.cnt["mm"]
            blk.tensor(s_w1)

            def s_rl(e, hc=hc):
                e.wait_ge(p.sems["mm"], qfst[("w1", hc)])
                e.activation(out=HH2[:, hc * 512:(hc + 1) * 512],
                             in_=PSc[:, (hc % 2) * 512:(hc % 2) * 512 + 512],
                             func=AF.Relu, bias=b1c[:, hc:hc + 1], scale=1.0)
                e.drain()
                e.sem_inc(p.sems["ex"], 1)
                p.cnt["ex"] += 1
                qfst[("rl", hc)] = p.cnt["ex"]
            blk.scalar(s_rl)

        for oc in range(G):
            def s_w2(e, oc=oc):
                if oc == 0:
                    p.w(e, "ex")
                if ("o", oc - 2) in qfst:
                    e.wait_ge(p.sems["ev"], qfst[("o", oc - 2)])
                for hc in range(32):
                    ins = nc.tensor.matmul(PSd[:, (oc % 2) * 512:(oc % 2) * 512 + 512],
                                           W2L[:, hc * 1024 + oc * 128:hc * 1024 + (oc + 1) * 128],
                                           HH2[:, hc * 512:(hc + 1) * 512],
                                           start=(hc == 0), stop=(hc == 31))
                    if hc == 31:
                        p.inc(ins, "mm", 1)
                        qfst[("w2", oc)] = p.cnt["mm"]
            blk.tensor(s_w2)

            def s_o(e, oc=oc):
                e.wait_ge(p.sems["mm"], qfst[("w2", oc)])
                e.scalar_tensor_tensor(out=XT2[:, oc * 512:(oc + 1) * 512],
                                       in0=PSd[:, (oc % 2) * 512:(oc % 2) * 512 + 512],
                                       scalar=b2c2[:, oc:oc + 1], in1=XT2[:, oc * 512:(oc + 1) * 512],
                                       op0=OP.add, op1=OP.add)
                e.drain()
                e.sem_inc(p.sems["ev"], 1)
                p.cnt["ev"] += 1
                qfst[("o", oc)] = p.cnt["ev"]
            blk.vector(s_o)

        def s_sx(e, c0=c0):
            p.w(e, "ev")
            for kk in range(G):
                p.inc(e.dma_start(out=E["x3_d"][kk * 128:(kk + 1) * 128, c0:c0 + 512],
                                  in_=XT2[:, kk * 512:(kk + 1) * 512]), "st", 16)
        blk.sync(s_sx)
        p.barrier()
    qf.close()


def build_l1f(p, E, u, out_fm):
    nc, blk = p.nc, p.blk
    zf = Scope(nc)
    WS = zf.sb("WS", [128, G * 1024])
    SGN = zf.sb("SGN", [128, G * 1024], BF16)
    XT3 = zf.sb("XT3", [128, G * 512])
    XB3 = zf.sb("XB3", [128, G * 512], BF16)
    SQ3 = zf.sb("SQ3", [128, 512])
    fr1c = zf.sb("fr1c", [1, 512]); fr2c = zf.sb("fr2c", [1, 512])
    PSe = zf.ps("PSe", [128, 1024])
    PSf = zf.ps("PSf", [128, 1024])

    def s_w(e):
        for kk in range(G):
            p.inc(e.dma_start(out=WS[:, kk * 1024:(kk + 1) * 1024],
                              in_=E["z_wT"][kk * 128:(kk + 1) * 128, :]), "ld", 16)
    blk.sync(s_w)

    def s_sgn(e):
        p.w(e, "ld")
        e.sign(SGN[:, :], WS[:, :])
        e.drain()
        e.sem_inc(p.sems["ex"], 1)
    blk.scalar(s_sgn)
    p.cnt["ex"] += 1
    p.barrier()

    for tt in range(NT):
        c0 = tt * TT

        def s_lx(e, c0=c0):
            p.w(e, "st")
            for kk in range(G):
                p.inc(e.dma_start(out=XT3[:, kk * 512:(kk + 1) * 512],
                                  in_=E["x3_d"][kk * 128:(kk + 1) * 128, c0:c0 + 512]), "ld", 16)
        blk.sync(s_lx)

        def s_wt(e):
            p.w(e, "ld")
            e.engine_nop()
            e.drain()
            e.sem_inc(p.sems["ev"], 1)
        blk.vector(s_wt)
        p.cnt["ev"] += 1
        p.barrier()

        emit_ln_sb(p, u, XT3, 512, XB3, None, None, fr1c, fr2c, SQ3, u.PS0[:, 512:1024], u.PS1[:, 512:1024])

        for oc in range(G):
            def s_mm(e, oc=oc):
                if oc == 0:
                    p.w(e, "ev")
                for kk in range(G):
                    ins = nc.tensor.matmul(PSe[:, (oc % 2) * 512:(oc % 2) * 512 + 512],
                                           SGN[:, kk * 1024 + oc * 128:kk * 1024 + (oc + 1) * 128],
                                           XB3[:, kk * 512:(kk + 1) * 512],
                                           start=(kk == 0), stop=(kk == G - 1))
                    if kk == G - 1:
                        p.inc(ins, "mm", 1)
            blk.tensor(s_mm)

            def s_o(e, oc=oc):
                p.w(e, "mm")
                e.scalar_tensor_tensor(out=XT3[:, oc * 512:(oc + 1) * 512],
                                       in0=PSe[:, (oc % 2) * 512:(oc % 2) * 512 + 512],
                                       scalar=1.0, in1=XT3[:, oc * 512:(oc + 1) * 512],
                                       op0=OP.mult, op1=OP.add)
                e.drain()
                e.sem_inc(p.sems["ev"], 1)
            blk.vector(s_o)
            p.cnt["ev"] += 1
            p.barrier()

        def s_sx(e, c0=c0):
            for kk in range(G):
                p.inc(e.dma_start(out=out_fm[kk * 128:(kk + 1) * 128, c0:c0 + 512],
                                  in_=XT3[:, kk * 512:(kk + 1) * 512]), "st", 16)
        blk.sync(s_sx)
        p.barrier()
    zf.close()


def build_program(taps=()):
    p = Prog()
    nc = p.nc

    ein = lambda nm, shp, dt=F32: nc.dram_tensor(nm, shp, dt, kind="ExternalInput").ap()
    dscr = lambda nm, shp, dt=F32: nc.dram_tensor(nm, shp, dt).ap()

    E = {}
    E["xin"] = ein("xin", [D, XH + T])
    for nm in ("w_dt", "w_v", "w_vs", "w_gate", "hm_dense", "outproj"):
        E[nm] = ein(nm, [D, D])
    E["convw"] = ein("convw", [D, 4])
    for nm in ("convb", "nrmsc", "dtbias"):
        E[nm] = ein(nm, [D, 1])
    E["ropecos"] = ein("ropecos", [128, T])
    E["ropesin"] = ein("ropesin", [128, T])
    for nm in ("flg_take", "flg_send", "flg_negbig"):
        E[nm] = ein(nm, [128, 1])
    E["c_eblk"] = ein("c_eblk", [128, 2])
    E["c_eblkT"] = ein("c_eblkT", [2, 128])

    E["g_d"] = dscr("g_d", [D, T])
    E["vin_d"] = dscr("vin_d", [D, T])
    E["x1_d"] = dscr("x1_d", [D, T])
    E["xln_d"] = dscr("xln_d", [D, T])
    E["snd_d"] = dscr("snd_d", [128, 24])
    E["rcv_d"] = dscr("rcv_d", [128, 24])

    for l in range(2):
        pre = "f%d_" % l
        for nm, shp in (("wq", [D, 512]), ("wqs", [D, 512]), ("wk", [D, 512]),
                        ("wks", [D, 512]), ("wv", [D, 512]), ("wo", [512, D]),
                        ("w1a", [D, HIDP]), ("w1g", [D, HIDP]), ("w2", [HIDP, D]),
                        ("b1a", [HIDP, 1]), ("b1g", [HIDP, 1]), ("b2", [D, 1]),
                        ("lg", [1, D]), ("lb", [1, D])):
            E[pre + nm] = ein(pre + nm, shp)
    E["f_qscale"] = ein("f_qscale", [2, 64, 1])
    E["f_kscale"] = ein("f_kscale", [2, 64, 1])
    E["f_cq"] = ein("f_cq", [128, T], BF16)
    E["f_sq"] = ein("f_sq", [128, T], BF16)
    E["f_ck"] = ein("f_ck", [128, KS], BF16)
    E["f_sk"] = ein("f_sk", [128, KS], BF16)
    E["f_mask4"] = ein("f_mask4", [4, 128, 640])
    E["f_m0"] = ein("f_m0", [128, 128])
    E["f_m4"] = ein("f_m4", [128, 128])

    E["hx1_d"] = dscr("hx1_d", [D, KH])
    E["xi0_d"] = dscr("xi0_d", [D, 2 + KS])
    E["xi1_d"] = dscr("xi1_d", [D, 2 + KS])
    E["xi2_d"] = dscr("xi2_d", [D, T])
    E["xp0_d"] = dscr("xp0_d", [D, 1 + T])
    E["xp1_d"] = dscr("xp1_d", [D, 1 + T])
    E["x2_d"] = dscr("x2_d", [D, T])
    E["snd_h"] = dscr("snd_h", [D, KH])
    E["rcv_h"] = dscr("rcv_h", [D, KH])
    E["snd_s"] = dscr("snd_s", [D, 8])
    E["rcv_s"] = dscr("rcv_s", [D, 8])

    for nm, shp in (("a_wq", [D, D]), ("a_wk", [D, D]), ("a_wv", [D, D]), ("a_wo", [D, D]),
                    ("a_bq", [D, 1]), ("a_bk", [D, 1]), ("a_bo", [D, 1]), ("a_b2", [D, 1]),
                    ("a_bv", [1, D]), ("a_b1", [4096, 1]),
                    ("a_w1", [D, 4096]), ("a_w2", [4096, D]),
                    ("a_ln1_g", [1, D]), ("a_ln1_b", [1, D]), ("a_ln2_g", [1, D]), ("a_ln2_b", [1, D]),
                    ("a_proj", [64, 128]), ("z_wT", [D, D])):
        E[nm] = ein(nm, shp)

    E["x2ln_d"] = dscr("x2ln_d", [D, T], BF16)
    E["kv_snd"] = dscr("kv_snd", [128, 2048])
    E["kv_rcv"] = dscr("kv_rcv", [128, 2048])
    E["kvb_d"] = dscr("kvb_d", [128, 2048], BF16)
    E["xq_d"] = dscr("xq_d", [D, T])
    E["x3_d"] = dscr("x3_d", [D, T])

    out_fm = nc.dram_tensor("out_fm", [D, T], F32, kind="ExternalOutput").ap()
    tap_aps = {}
    _tshapes = {"dbgkrot": [128, 4 * KS], "dbgvt": [128, NKC * 512],
                "dbgqrot": [128, 4 * 512], "dbgo": [128, 4 * 512], "dbgp": [128, 2 * 640],
                "dbgatt": [128, G * 512], "dbgqf": [128, G * 512], "dbgqp": [128, 2 * 512],
                "dbgo2": [128, 4 * 512], "dbgxpr": [128, G * 512]}
    for tname in taps:
        shp = list(E[tname].tensor.shape) if tname in E else _tshapes.get(tname, [D, T])
        tap_aps[tname] = nc.dram_tensor("tap_" + tname, shp, F32, kind="ExternalOutput").ap()

    with nc.Block() as blk:
        p.blk = blk
        for s in ("ld", "mm", "ev", "ex", "st", "bar", "cc", "gp", "tp"):
            p.sem(s)

        E["_taps"] = tap_aps
        u = build_utils(p, E)
        p.marks = [("start", nc.n_instructions())]
        build_l0a(p, E, u)
        p.marks.append(("l0a", nc.n_instructions()))
        build_l0f(p, E, u)
        p.marks.append(("l0f", nc.n_instructions()))
        build_l1(p, E, u)
        p.marks.append(("l1a", nc.n_instructions()))
        build_l1f(p, E, u, out_fm)
        p.marks.append(("l1f", nc.n_instructions()))

        def s_out(e):
            for tname, ap in tap_aps.items():
                if tname in E:
                    p.inc(e.dma_start(out=ap[:, :], in_=E[tname][:, :]), "st", 16)
            p.w(e, "st")
        blk.gpsimd(s_out)

    return p, tap_aps


def kernel(**inputs):
    in_maps = prep_inputs(inputs)
    p, _ = build_program()
    res = run_bass_kernel_spmd(p.nc, in_maps, core_ids=list(range(8)))
    out = np.zeros((B, N, D), np.float32)
    for c in range(8):
        b, hf = c // 2, c % 2
        out[b, hf * T:(hf + 1) * T, :] = res.results[c]["out_fm"].T
    return out
